# version 07
# baseline (speedup 1.0000x reference)
import math
import numpy as np
from contextlib import ExitStack

import concourse.bass as bass
import concourse.mybir as mybir
from concourse.bass_utils import run_bass_kernel_spmd
import ml_dtypes

F32 = mybir.dt.float32
BF16 = mybir.dt.bfloat16
AF = mybir.ActivationFunctionType
ALU = mybir.AluOpType
bf16 = ml_dtypes.bfloat16

D = 2048
T = 8192
TQ = 2048
HD = 128
NHL = 4
CD = 128
NCH = T // CD
FF = 8192
FB = 5632
NDC = D // 128
RT = 512
NTT = T // RT

DEBUG = False
LAST_MARKS = None
N_PHASES = 10


class Sem:
    def __init__(self, nc, ctx, name):
        self.h = ctx.enter_context(nc.semaphore(name))
        self.n = 0

    def inc(self, k=1):
        self.n += k
        return self.n


class P:
    pass


def chain(eng, sem, ins):
    ins.then_inc(sem.h, 1)
    sem.inc()
    eng.wait_ge(sem.h, sem.n)


class AutoQ:

    QUEUES = ("sync", "gpsimd", "squeue")

    def __init__(self, p):
        self.p = p
        self.steps = {e: [] for e in ("sync", "gpsimd", "scalar", "vector", "tensor")}
        self.esem = {"scalar": p.s_ac, "vector": p.s_ve, "tensor": p.s_mm,
                     "gpsimd": p.s_gp, "sync": p.s_ld}
        self.writer = {}
        self.readers = {}
        self.waited = {}
        if not hasattr(p, "_qmap"):
            p._qmap = {}
            p._qcount = {"sync": 0, "gpsimd": 0, "squeue": 0}
            p._qlastbuf = {}

    def _qsem(self, q, buf):
        key = (q, buf)
        if key not in self.p._qmap:
            pool = {"sync": self.p.sqs, "gpsimd": self.p.sqg, "squeue": self.p.sql}[q]
            self.p._qmap[key] = pool[self.p._qcount[q] % len(pool)]
            self.p._qcount[q] += 1
        return self.p._qmap[key]

    def _dep_waits(self, engine, reads, writes):
        waits = []

        def add(dep):
            if dep is None:
                return
            sem, tgt, oeng = dep
            k = (engine, id(sem))
            if self.waited.get(k, -1) >= tgt:
                return
            self.waited[k] = tgt
            waits.append((sem, tgt))

        for r in reads:
            add(self.writer.get(r))
        for w in writes:
            add(self.writer.get(w))
            for dep in self.readers.get(w, ()):
                add(dep)
        return waits

    def do(self, engine, emit, reads=(), writes=(), n_inc=1):
        sem = self.esem[engine]
        waits = self._dep_waits(engine, reads, writes)
        tgt = sem.inc(n_inc)
        dep = (sem, tgt, engine)
        for w in writes:
            self.writer[w] = dep
            self.readers[w] = []
        for r in reads:
            self.readers.setdefault(r, []).append(dep)

        def run(eng, waits=waits, emit=emit, sem=sem, n_inc=n_inc):
            for ws, wt in waits:
                eng.wait_ge(ws.h, wt)
            ins = emit(eng)
            ins.then_inc(sem.h, n_inc)

        self.steps[engine].append(run)

    def dma(self, q, out_ap, in_ap, out_buf, in_buf, **kw):
        qeng = {"sync": "sync", "gpsimd": "gpsimd", "squeue": "scalar"}[q]
        sem = self._qsem(q, out_buf)
        in_bufs = ([] if in_buf is None else
                   (list(in_buf) if isinstance(in_buf, list) else [in_buf]))
        waits = self._dep_waits(qeng, in_bufs, [out_buf])
        if self.p._qlastbuf.get(id(sem)) not in (None, out_buf):
            k = (qeng, id(sem))
            if self.waited.get(k, -1) < sem.n:
                self.waited[k] = sem.n
                waits.append((sem, sem.n))
        self.p._qlastbuf[id(sem)] = out_buf
        tgt = sem.inc(16)
        dep = (sem, tgt, qeng)
        self.writer[out_buf] = dep
        self.readers[out_buf] = []
        for ib in in_bufs:
            self.readers.setdefault(ib, []).append(dep)

        def run(eng, waits=waits, sem=sem, out_ap=out_ap, in_ap=in_ap, kw=kw,
                _ob=out_buf, _ib=in_bufs):
            for ws, wt in waits:
                eng.wait_ge(ws.h, wt)
            try:
                eng.dma_start(out=out_ap, in_=in_ap, **kw).then_inc(sem.h, 16)
            except Exception:
                import traceback
                print("DMA FAIL out_buf=", _ob, "in=", _ib)
                print("OUT:", out_ap)
                print("IN:", in_ap)
                traceback.print_exc()
                raise

        self.steps[qeng].append(run)

    def raw(self, engine, emit):
        self.steps[engine].append(emit)

    def finish(self):
        nc = self.p.nc
        qsems = {id(s): s for (q, b), s in self.p._qmap.items()}
        esems = [self.esem[e] for e in ("scalar", "vector", "tensor", "gpsimd", "sync")]

        def drain(eng):
            for s in qsems.values():
                eng.wait_ge(s.h, s.n)
            for s in esems:
                eng.wait_ge(s.h, s.n)

        with nc.Block() as b:
            for ename, reg in (("sync", b.sync), ("gpsimd", b.gpsimd),
                               ("scalar", b.scalar), ("vector", b.vector),
                               ("tensor", b.tensor)):
                def body(eng, ename=ename):
                    for fn in self.steps[ename]:
                        fn(eng)
                    if ename == "gpsimd":
                        drain(eng)
                reg(body)


def build_program():
    nc = bass.Bass("TRN2", target_bir_lowering=False)
    p = P()
    p.nc = nc
    ctx = ExitStack()
    p.ctx = ctx

    def ein(name, shape, dt=F32):
        return nc.dram_tensor(name, shape, dt, kind="ExternalInput").ap()

    p.xT = ein("xT", [D, T])
    p.wqkvT = ein("wqkvT", [D, 1536])
    p.convw = ein("convw", [1536, 4])
    p.wabT = ein("wabT", [D, 8])
    p.bab = ein("bab", [8, 1])
    p.wgT = ein("wgT", [D, 512])
    p.woT = ein("woT", [D, D])
    p.ln1r = ein("ln1r", [128, NDC])
    p.ln2r = ein("ln2r", [128, NDC])
    p.wqfT = ein("wqfT", [D, 512])
    p.wkfT = ein("wkfT", [D, 512])
    p.wofs = ein("wofs", [D, 512])
    p.wvfs = ein("wvfs", [512, D])
    p.A_ = ein("A_", [D, 16])
    p.B_ = ein("B_", [16, FF])
    p.winT = ein("winT", [D, FF])
    p.wupT = ein("wupT", [D, FF])
    p.wdnT = ein("wdnT", [FF, D])
    p.tdr = ein("tdr", [128, NDC])
    p.tfr = ein("tfr", [128, NDC])
    p.mkr = ein("mkr", [128, NDC])
    p.mvr = ein("mvr", [128, NDC])
    p.mrr = ein("mrr", [128, NDC])
    p.wk1T = ein("wk1T", [D, D])
    p.wv1T = ein("wv1T", [D, D])
    p.wr1T = ein("wr1T", [D, D])
    p.wo1T = ein("wo1T", [D, D])
    p.w1T = ein("w1T", [D, FB])
    p.w2T = ein("w2T", [D, FB])
    p.w3T = ein("w3T", [FB, D])
    p.qi = nc.dram_tensor("qi", [1, 1], mybir.dt.uint32, kind="ExternalInput").ap()
    p.xq4T = ein("xq4T", [D, TQ])

    p.out = nc.dram_tensor("out", [D, TQ], F32, kind="ExternalOutput").ap()

    dbg = "ExternalOutput" if DEBUG else "Internal"

    def intl(name, shape, dt=BF16, debug_vis=False):
        kind = dbg if debug_vis else "Internal"
        return nc.dram_tensor(name, shape, dt, kind=kind).ap()

    p.sc_qd = intl("sc_qd", [512, T], debug_vis=True)
    p.sc_kd = intl("sc_kd", [512, T], debug_vis=True)
    p.sc_vd = intl("sc_vd", [512, T], debug_vis=True)
    p.sc_gg = intl("sc_gg", [512, T], debug_vis=True)
    p.sc_qf = intl("sc_qf", [512, T], debug_vis=True)
    p.sc_kf = intl("sc_kf", [512, T], debug_vis=True)
    p.sc_vt = intl("sc_vt", [T, 4 * 33], debug_vis=True)
    p.sc_xr = intl("sc_xr", [D, T])
    p.sc_xn = intl("sc_xn", [D, T], debug_vis=True)
    p.sc_ab = intl("sc_ab", [16, T], F32, debug_vis=True)
    p.sc_bnd = intl("sc_bnd", [1, 8], F32, debug_vis=True)
    p.ag_og_in = intl("ag_og_in", [512, T])
    p.ag_og_out = intl("ag_og_out", [D, T], debug_vis=True)
    p.ar_m_in = intl("ar_m_in", [16, T], F32)
    p.ar_m_out = intl("ar_m_out", [16, T], F32, debug_vis=True)
    p.sc_x1 = intl("sc_x1", [D, TQ], F32, debug_vis=True)
    p.ag_h_in = intl("ag_h_in", [1, D])
    p.ag_h_out = intl("ag_h_out", [4, D])
    p.ag_e_in = intl("ag_e_in", [2, D], F32)
    p.ag_e_out = intl("ag_e_out", [8, D], F32)
    p.sc_k1 = intl("sc_k1", [D, TQ], debug_vis=True)
    p.sc_v1 = intl("sc_v1", [D, TQ])
    p.sc_r1 = intl("sc_r1", [D, TQ])
    p.sc_Nl = intl("sc_Nl", [D, TQ], F32)
    p.sc_Dl = intl("sc_Dl", [D, TQ], F32)
    p.sc_xk = intl("sc_xk", [D, TQ])
    p.sc_xv = intl("sc_xv", [D, TQ])
    p.sc_xrm = intl("sc_xrm", [D, TQ])
    p.sc_sr = intl("sc_sr", [D, TQ])
    p.sc_rw = intl("sc_rw", [D, TQ], debug_vis=True)
    p.sc_gq = intl("sc_gq", [D, TQ])
    p.sc_stb1 = intl("sc_stb1", [2, TQ])
    p.ag_h2 = intl("ag_h2", [5, D])
    p.sc_w1q = intl("sc_w1q", [D, FB], debug_vis=True)
    p.sc_w2q = intl("sc_w2q", [D, FB])
    p.sc_w3q = intl("sc_w3q", [FB, D])
    p.dbg = intl("dbgout", [128, 2048], F32, debug_vis=True)
    p.sc_st = intl("sc_st", [8, T], F32, debug_vis=True)
    p.sc_stb = intl("sc_stb", [4, T], BF16, debug_vis=True)

    p.c_ident = nc.inline_tensor(np.eye(128, dtype=bf16), name="c_ident").ap()
    p.c_ones = nc.inline_tensor(np.ones((128, 512), dtype=bf16), name="c_ones").ap()
    s_i = np.arange(CD)[:, None]
    t_i = np.arange(CD)[None, :]
    p.c_maskUs = nc.inline_tensor((s_i < t_i).astype(bf16), name="c_maskUs").ap()
    p.c_maskUi = nc.inline_tensor((s_i <= t_i).astype(bf16), name="c_maskUi").ap()
    dm = np.concatenate(
        [((np.arange(128)[:, None] + 128 * a) <= np.arange(512)[None, :]).astype(bf16)
         for a in range(4)], axis=1)
    p.c_diag = nc.inline_tensor(dm, name="c_diag").ap()
    p.c_iota = nc.inline_tensor(
        np.tile((np.arange(TQ, dtype=np.float32) + 1.0)[None, :], (128, 1)),
        name="c_iota").ap()
    p.c_zero = nc.inline_tensor(np.zeros((128, 512), np.float32), name="c_zero").ap()
    p.c_identf = nc.inline_tensor(np.eye(64, dtype=np.float32), name="c_identf").ap()

    sb = lambda name, shape, dt: ctx.enter_context(nc.sbuf_tensor(name, shape, dt))
    p.ident = sb("ident", [128, 128], BF16)
    p.ones = sb("ones", [128, 512], BF16)
    p.maskUs = sb("maskUs", [CD, CD], BF16)
    p.maskUi = sb("maskUi", [CD, CD], BF16)
    p.diagm = sb("diagm", [128, 4 * 512], BF16)
    p.zeros = sb("zeros", [128, 512], F32)

    p.v_babn = sb("v_babn", [8, 1], F32)
    p.qit = sb("qit", [1, 1], mybir.dt.uint32)
    p.w_conv = sb("w_conv", [128, 48], F32)
    p.v_bab = sb("v_bab", [8, 1], F32)
    p.v_ln1 = sb("v_ln1", [128, NDC], F32)
    p.v_ln2 = sb("v_ln2", [128, NDC], F32)
    p.v_td = sb("v_td", [128, NDC], F32)
    p.v_tf = sb("v_tf", [128, NDC], F32)
    p.v_mk = sb("v_mk", [128, NDC], F32)
    p.v_mv = sb("v_mv", [128, NDC], F32)
    p.v_mr = sb("v_mr", [128, NDC], F32)
    p.v_w = sb("v_w", [128, NDC], F32)
    p.v_ew = sb("v_ew", [128, NDC], F32)
    p.v_decq = sb("v_decq", [128, NDC], F32)
    p.v_mk1 = sb("v_mk1", [128, NDC], F32)
    p.v_mv1 = sb("v_mv1", [128, NDC], F32)
    p.v_mr1 = sb("v_mr1", [128, NDC], F32)
    p.v_wtq = sb("v_wtq", [128, NDC], F32)

    p.w_pall = sb("w_pall", [128, NDC * 64], BF16)
    p.w_a2 = sb("w_a2", [128, NDC * 16], BF16)

    p.ps = [ctx.enter_context(nc.psum_tensor(f"ps{i}", [128, 512], F32)) for i in range(6)]
    p.psb = [ctx.enter_context(nc.psum_tensor(f"psb{i}", [128, 1024], BF16)) for i in range(2)]

    p.constpool = sb("constpool", [128, 8], F32)
    p.onesf32 = sb("onesf32", [1, 128], F32)
    p.CONST_VALS = [1e-5, 1e-6, 0.5, -0.5, 1.0, 0.0, 0.0, 0.0]
    p.cst = lambda v: p.constpool[:, p.CONST_VALS.index(v):p.CONST_VALS.index(v) + 1]

    p.s_ld = Sem(nc, ctx, "s_ld")
    p.s_st = Sem(nc, ctx, "s_st")
    p.s_l2 = Sem(nc, ctx, "s_l2")
    p.s_mm = Sem(nc, ctx, "s_mm")
    p.s_ve = Sem(nc, ctx, "s_ve")
    p.s_ac = Sem(nc, ctx, "s_ac")
    p.s_gp = Sem(nc, ctx, "s_gp")
    p.sqs = [Sem(nc, ctx, f"sqs{i}") for i in range(8)]
    p.sqg = [Sem(nc, ctx, f"sqg{i}") for i in range(4)]
    p.sql = [Sem(nc, ctx, f"sql{i}") for i in range(4)]

    phases = [ph0, ph1, ph2, ph3, ph4, ph5, ph6, ph7, ph8, ph9][:N_PHASES]
    p.phase_marks = []
    for ph in phases:
        ph(p)
        nc.all_engine_barrier()
        p.phase_marks.append((ph.__name__, nc.n_instructions()))
    global LAST_MARKS
    LAST_MARKS = list(p.phase_marks)
    ctx.close()
    p.nc_ref = nc
    return nc


def ph0(p):
    nc = p.nc
    ctx0 = ExitStack()
    wofs_sb = ctx0.enter_context(nc.sbuf_tensor("wofs_sb", [128, NDC * 512], BF16))
    wvfs_sb = ctx0.enter_context(nc.sbuf_tensor("wvfs_sb", [128, 4 * D], BF16))
    w2_sb = ctx0.enter_context(nc.sbuf_tensor("w2_sb", [128, 64], BF16))
    aw_sb = ctx0.enter_context(nc.sbuf_tensor("aw_sb", [128, NDC * 16], BF16))

    ld0, st0, ve00 = p.s_ld.n, p.s_st.n, p.s_ve.n
    with nc.Block() as b:
        @b.sync
        def _(e):
            e.dma_start(out=p.ident[:], in_=p.c_ident).then_inc(p.s_ld.h, 16); p.s_ld.inc(16)
            e.dma_start(out=p.ones[:], in_=p.c_ones).then_inc(p.s_ld.h, 16); p.s_ld.inc(16)
            e.dma_start(out=p.maskUs[:], in_=p.c_maskUs).then_inc(p.s_ld.h, 16); p.s_ld.inc(16)
            e.dma_start(out=p.maskUi[:], in_=p.c_maskUi).then_inc(p.s_ld.h, 16); p.s_ld.inc(16)
            e.dma_start(out=p.diagm[:], in_=p.c_diag).then_inc(p.s_ld.h, 16); p.s_ld.inc(16)
            e.dma_start(out=p.zeros[:], in_=p.c_zero).then_inc(p.s_ld.h, 16); p.s_ld.inc(16)
            for dst, src in [(p.v_bab, p.bab), (p.v_ln1, p.ln1r), (p.v_ln2, p.ln2r),
                             (p.v_td, p.tdr), (p.v_tf, p.tfr), (p.v_mk, p.mkr),
                             (p.v_mv, p.mvr), (p.v_mr, p.mrr)]:
                e.dma_start(out=dst[:], in_=src).then_inc(p.s_ld.h, 16); p.s_ld.inc(16)
            e.dma_start(out=p.qit[:], in_=p.qi).then_inc(p.s_ld.h, 16); p.s_ld.inc(16)
            e.dma_start(out=p.w_conv[:].rearrange("q (r f) -> q r f", f=4),
                        in_=p.convw.rearrange("(r q) f -> q r f", q=128)
                        ).then_inc(p.s_ld.h, 16); p.s_ld.inc(16)
            e.wait_ge(p.s_ld.h, p.s_ld.n)

        @b.gpsimd
        def _(g):
            for dst, src, oc in [
                (wofs_sb, p.wofs, 512), (aw_sb, p.A_, 16),
            ]:
                g.dma_start(out=dst[:, 0:NDC * oc].rearrange("q (dc oc) -> q dc oc", oc=oc),
                            in_=src.rearrange("(dc q) oc -> q dc oc", q=128)
                            ).then_inc(p.s_st.h, 16); p.s_st.inc(16)
            g.dma_start(out=wvfs_sb[:].rearrange("q (hc oc) -> q hc oc", oc=D),
                        in_=p.wvfs.rearrange("(hc q) oc -> q hc oc", q=128)
                        ).then_inc(p.s_st.h, 16); p.s_st.inc(16)
            g.wait_ge(p.s_st.h, p.s_st.n)

        @b.scalar
        def _(sc):
            sc.wait_ge(p.s_ld.h, ld0 + 16 * 16)
            sc.wait_ge(p.s_st.h, st0 + 3 * 16)
            sc.wait_ge(p.s_ve.h, ve00 + 1)
            for dc in range(NDC):
                sc.mul(out=p.w_a2[:, dc * 16:(dc + 1) * 16],
                       in_=aw_sb[:, dc * 16:(dc + 1) * 16], mul=p.v_ln2[:, dc:dc + 1])
            sc.mul(out=p.v_babn[:], in_=p.v_bab[:], mul=-1.0)
            sc.activation(out=p.v_mk1[:], in_=p.v_mk[:], func=AF.Identity, scale=-1.0,
                          bias=p.cst(1.0))
            sc.activation(out=p.v_mv1[:], in_=p.v_mv[:], func=AF.Identity, scale=-1.0,
                          bias=p.cst(1.0))
            sc.activation(out=p.v_mr1[:], in_=p.v_mr[:], func=AF.Identity, scale=-1.0,
                          bias=p.cst(1.0))
            sc.activation(out=p.v_w[:], in_=p.v_td[:], func=AF.Exp
                          ).then_inc(p.s_ac.h, 1); _ = p.s_ac.inc()
            sc.wait_ge(p.s_ac.h, p.s_ac.n)
            sc.mul(out=p.v_w[:], in_=p.v_w[:], mul=-1.0).then_inc(p.s_ac.h, 1); _ = p.s_ac.inc()
            sc.wait_ge(p.s_ac.h, p.s_ac.n)
            sc.activation(out=p.v_ew[:], in_=p.v_w[:], func=AF.Exp)
            sc.activation(out=p.v_decq[:], in_=p.v_w[:], func=AF.Exp, scale=float(TQ))
            sc.mul(out=p.v_wtq[:], in_=p.v_w[:], mul=float(TQ)
                   ).then_inc(p.s_ac.h, 1); _ = p.s_ac.inc()

        @b.vector
        def _(v):
            for j, val in enumerate(p.CONST_VALS):
                v.memset(p.constpool[:, j:j + 1], float(val))
            ins = v.memset(p.onesf32[:], 1.0)
            ins.then_inc(p.s_ve.h, 1); _ = p.s_ve.inc()

        @b.tensor
        def _(t):
            t.wait_ge(p.s_st.h, st0 + 3 * 16)
            for ic in range(4):
                for dc in range(NDC):
                    ins = t.matmul(p.ps[0][0:128, ic * 16:(ic + 1) * 16],
                                   wofs_sb[:, dc * 512 + ic * 128: dc * 512 + ic * 128 + 128],
                                   aw_sb[:, dc * 16:(dc + 1) * 16],
                                   start=(dc == 0), stop=(dc == NDC - 1))
            ins.then_inc(p.s_mm.h, 1); _ = p.s_mm.inc()

    nc.all_engine_barrier()
    ve0, mm0, ac0 = p.s_ve.n, p.s_mm.n, p.s_ac.n
    with nc.Block() as b2:
        @b2.vector
        def _(v):
            v.tensor_copy(w2_sb[:], p.ps[0][0:128, 0:64]).then_inc(p.s_ve.h, 1); _ = p.s_ve.inc()

        @b2.tensor
        def _(t):
            t.wait_ge(p.s_ve.h, ve0 + 1)
            for dc in range(NDC):
                if dc >= 2:
                    t.wait_ge(p.s_ac.h, ac0 + dc - 1)
                bank = p.ps[1 + dc % 2]
                for h in range(NHL):
                    ins = t.matmul(bank[0:128, h * 16:(h + 1) * 16],
                                   wvfs_sb[:, h * D + dc * 128: h * D + dc * 128 + 128],
                                   w2_sb[:, h * 16:(h + 1) * 16], start=True, stop=True)
                ins.then_inc(p.s_mm.h, 1); _ = p.s_mm.inc()

        @b2.scalar
        def _(sc):
            for dc in range(NDC):
                sc.wait_ge(p.s_mm.h, mm0 + dc + 1)
                sc.mul(out=p.w_pall[:, dc * 64:(dc + 1) * 64],
                       in_=p.ps[1 + dc % 2][0:128, 0:64], mul=p.v_ln1[:, dc:dc + 1]
                       ).then_inc(p.s_ac.h, 1); _ = p.s_ac.inc()

        @b2.gpsimd
        def _(g):
            p.qv = nc.values_load(p.qit[:], engines=[mybir.EngineType.Pool])
            p.qcol = [nc.snap(p.qv * TQ + u * 512, engines=[mybir.EngineType.Pool],
                              min_val=0, max_val=T - 512) for u in range(4)]
            p.qrow = nc.snap(p.qv * D, engines=[mybir.EngineType.Pool],
                             min_val=0, max_val=3 * D)
    ctx0.close()


def ph1(p):
    nc = p.nc
    ctx1 = ExitStack()
    xf = [ctx1.enter_context(nc.sbuf_tensor(f"p1xf{i}", [128, 512], F32)) for i in range(2)]
    xb = [ctx1.enter_context(nc.sbuf_tensor(f"p1xb{i}", [128, 512], BF16)) for i in range(2)]
    xq = [ctx1.enter_context(nc.sbuf_tensor(f"p1xq{i}", [128, 512], BF16)) for i in range(2)]
    sx = ctx1.enter_context(nc.sbuf_tensor("p1sx", [33, T], F32))
    st = ctx1.enter_context(nc.sbuf_tensor("p1st", [128, 64 * 14], F32))

    NIT = NTT * NDC
    sx0, sx1 = p.sqs[0], p.sqs[1]
    x0_0, x1_0 = sx0.n, sx1.n
    ve0, mm0, ac0 = p.s_ve.n, p.s_mm.n, p.s_ac.n
    with nc.Block() as b:
        @b.sync
        def _(e):
            for k in range(NIT):
                tt, dc = divmod(k, NDC)
                sm = sx0 if k % 2 == 0 else sx1
                if k >= 2:
                    e.wait_ge(p.s_ve.h, ve0 + 2 * (k - 1))
                e.dma_start(out=xf[k % 2][:],
                            in_=p.xT[dc * 128:(dc + 1) * 128, tt * 512:(tt + 1) * 512]
                            ).then_inc(sm.h, 16)
                sm.inc(16)

        @b.vector
        def _(v):
            for k in range(NIT):
                base = x0_0 if k % 2 == 0 else x1_0
                v.wait_ge((sx0 if k % 2 == 0 else sx1).h, base + (k // 2 + 1) * 16)
                if k >= 2:
                    v.wait_ge(p.s_mm.h, mm0 + 2 * (k - 1))
                chain(v, p.s_ve, v.tensor_copy(xb[k % 2][:], xf[k % 2][:]))
                v.tensor_mul(xq[k % 2][:], xb[k % 2][:], xb[k % 2][:]
                             ).then_inc(p.s_ve.h, 1)
                p.s_ve.inc()

        @b.tensor
        def _(t):
            for k in range(NIT):
                tt, dc = divmod(k, NDC)
                t.wait_ge(p.s_ve.h, ve0 + 2 * k + 2)
                if dc == 0 and tt >= 2:
                    t.wait_ge(p.s_ac.h, ac0 + tt - 1)
                if k >= 2:
                    pass
                ba, bq = p.ps[2 * (tt % 2)], p.ps[2 * (tt % 2) + 1]
                t.matmul(ba[0:1, 0:512], p.ones[:, 0:1], xb[k % 2][:],
                         start=(dc == 0), stop=(dc == NDC - 1)).then_inc(p.s_mm.h, 1)
                t.matmul(bq[0:1, 0:512], p.ones[:, 0:1], xq[k % 2][:],
                         start=(dc == 0), stop=(dc == NDC - 1)
                         ).then_inc(p.s_mm.h, 1)
                p.s_mm.inc(2)

        @b.scalar
        def _(sc):
            for tt in range(NTT):
                sc.wait_ge(p.s_mm.h, mm0 + (tt + 1) * NDC * 2)
                sc.copy(out=sx[0:1, tt * 512:(tt + 1) * 512], in_=p.ps[2 * (tt % 2)][0:1, 0:512])
                sc.copy(out=sx[32:33, tt * 512:(tt + 1) * 512], in_=p.ps[2 * (tt % 2) + 1][0:1, 0:512]
                        ).then_inc(p.s_ac.h, 1); _ = p.s_ac.inc()

    nc.all_engine_barrier()
    c = lambda i: st[:, 64 * i:64 * (i + 1)]
    ld0, ve0, ac0 = p.s_ld.n, p.s_ve.n, p.s_ac.n
    with nc.Block() as b2:
        @b2.sync
        def _(e):
            e.dma_start(out=p.sc_st[0:1, :], in_=sx[0:1, :]).then_inc(p.s_ld.h, 16); p.s_ld.inc(16)
            e.dma_start(out=p.sc_st[1:2, :], in_=sx[32:33, :]).then_inc(p.s_ld.h, 16); p.s_ld.inc(16)
            e.wait_ge(p.s_ld.h, p.s_ld.n)
            e.dma_start(out=c(0), in_=p.sc_st.rearrange("o (q w) -> o q w", q=128)[0]
                        ).then_inc(p.s_ld.h, 16); p.s_ld.inc(16)
            e.dma_start(out=c(1), in_=p.sc_st.rearrange("o (q w) -> o q w", q=128)[1]
                        ).then_inc(p.s_ld.h, 16); p.s_ld.inc(16)
            e.wait_ge(p.s_ld.h, p.s_ld.n)

        @b2.vector
        def _(v):
            v.wait_ge(p.s_ld.h, ld0 + 64)
            chain(v, p.s_ve, v.tensor_scalar_mul(c(2), c(0), 1.0 / D))
            chain(v, p.s_ve, v.tensor_scalar_mul(c(3), c(1), 1.0 / D))
            chain(v, p.s_ve, v.tensor_mul(c(5), c(2), c(2)))
            chain(v, p.s_ve, v.tensor_sub(c(4), c(3), c(5)))
            v.tensor_scalar_max(c(4), c(4), 0.0).then_inc(p.s_ve.h, 1); _ = p.s_ve.inc()
            v.wait_ge(p.s_ac.h, ac0 + 1)
            chain(v, p.s_ve, v.reciprocal(c(6), c(5)))
            chain(v, p.s_ve, v.tensor_mul(c(7), c(4), c(6)))
            chain(v, p.s_ve, v.tensor_mul(c(7), c(7), c(6)))
            v.tensor_scalar_add(c(7), c(7), 1e-6).then_inc(p.s_ve.h, 1); _ = p.s_ve.inc()
            v.wait_ge(p.s_ac.h, ac0 + 2)
            chain(v, p.s_ve, v.reciprocal(c(9), c(8)))
            chain(v, p.s_ve, v.tensor_mul(c(10), c(6), c(9)))
            chain(v, p.s_ve, v.tensor_mul(c(11), c(2), c(6)))
            chain(v, p.s_ve, v.tensor_scalar_mul(c(11), c(11), -1.0))
            chain(v, p.s_ve, v.tensor_mul(c(12), c(2), c(10)))
            v.tensor_scalar_mul(c(12), c(12), -1.0).then_inc(p.s_ve.h, 1); _ = p.s_ve.inc()

        @b2.scalar
        def _(sc):
            sc.wait_ge(p.s_ve.h, ve0 + 5)
            sc.activation(out=c(5), in_=c(4), func=AF.Sqrt, bias=p.cst(1e-5)
                          ).then_inc(p.s_ac.h, 1); _ = p.s_ac.inc()
            sc.wait_ge(p.s_ve.h, ve0 + 9)
            sc.activation(out=c(8), in_=c(7), func=AF.Sqrt).then_inc(p.s_ac.h, 1); _ = p.s_ac.inc()

        @b2.gpsimd
        def _(g):
            g.wait_ge(p.s_ve.h, ve0 + 15)
            for i, cc in [(0, 6), (1, 11), (2, 10), (3, 12)]:
                g.dma_start(out=p.sc_st.rearrange("o (q w) -> o q w", q=128)[4 + i],
                            in_=c(cc)).then_inc(p.s_st.h, 16)
                p.s_st.inc(16)
                g.dma_start(out=p.sc_stb.rearrange("o (q w) -> o q w", q=128)[i],
                            in_=c(cc)).then_inc(p.s_st.h, 16)
                p.s_st.inc(16)
            g.wait_ge(p.s_st.h, p.s_st.n)
    ctx1.close()


def ph2(p):
    ph2a(p)
    p.nc.all_engine_barrier()
    ph2b(p)
    p.nc.all_engine_barrier()
    ph2c(p)
    p.nc.all_engine_barrier()
    ph2d(p)


def dyn_off(ap0, sym):
    import concourse.ap as _ap
    return _ap.AP(tensor=ap0.tensor, offset=sym, ap=[list(x) for x in ap0.ap])


def xn_tile_dram(p, dram, tt):
    return dram.rearrange("(dc q) w -> q dc w", q=128)[:, :, tt * 512:(tt + 1) * 512]


def ph2a(p):
    nc = p.nc
    with ExitStack() as ctx:
        sbt = lambda n, s, d: ctx.enter_context(nc.sbuf_tensor(n, s, d))
        xf = [sbt(f"a_xf{i}", [128, 512], F32) for i in range(2)]
        xbt = [sbt(f"a_xb{i}", [128, 512], BF16) for i in range(2)]
        bc = sbt("a_bc", [128, 4 * 512], BF16)
        xnt = [sbt(f"a_xnt{i}", [128, NDC * 512], BF16) for i in range(2)]
        xrt = [sbt(f"a_xrt{i}", [128, NDC * 512], BF16) for i in range(2)]
        tmp = sbt("a_tmp", [128, 512], BF16)
        srow = [sbt(f"a_srow{i}", [1, 4 * 512], BF16) for i in range(2)]
        q = AutoQ(p)
        for tt in range(NTT):
            pb = tt % 2
            q.dma("sync", srow[pb][:], p.sc_stb[:, tt * 512:(tt + 1) * 512],
                  ("srow", pb), None)
            for j in range(4):
                q.do("tensor", lambda t, j=j, pb=pb: t.matmul(
                    p.ps[2 + j][0:128, 0:512], p.ones[0:1, 0:128],
                    srow[pb][0:1, j * 512:(j + 1) * 512], start=True, stop=True),
                    reads=[("srow", pb)], writes=[("ps", 2 + j)])
                q.do("vector", lambda v, j=j: v.tensor_copy(
                    bc[:, j * 512:(j + 1) * 512], p.ps[2 + j][0:128, 0:512]),
                    reads=[("ps", 2 + j)], writes=[("bc", j)])
            for dc in range(NDC):
                k = tt * NDC + dc
                q.dma("sync", xf[k % 2][:],
                      p.xT[dc * 128:(dc + 1) * 128, tt * 512:(tt + 1) * 512],
                      ("xf", k % 2), None)
                q.do("vector", lambda v, k=k: v.tensor_copy(xbt[k % 2][:], xf[k % 2][:]),
                     reads=[("xf", k % 2)], writes=[("xb", k % 2)])
                o = dc * 512
                q.do("vector", lambda v, k=k: v.tensor_mul(tmp[:], xbt[k % 2][:], bc[:, 0:512]),
                     reads=[("xb", k % 2), ("bc", 0)], writes=[("tmp",)])
                q.do("vector", lambda v, o=o, pb=pb: v.tensor_add(
                    xnt[pb][:, o:o + 512], tmp[:], bc[:, 512:1024]),
                    reads=[("tmp",), ("bc", 1)], writes=[("xnt", pb)])
                q.do("vector", lambda v, k=k: v.tensor_mul(tmp[:], xbt[k % 2][:], bc[:, 1024:1536]),
                     reads=[("xb", k % 2), ("bc", 2)], writes=[("tmp",)])
                q.do("vector", lambda v, o=o, pb=pb: v.tensor_add(
                    xrt[pb][:, o:o + 512], tmp[:], bc[:, 1536:2048]),
                    reads=[("tmp",), ("bc", 3)], writes=[("xrt", pb)])
            q.dma("gpsimd", xn_tile_dram(p, p.sc_xn, tt),
                  xnt[pb][:].rearrange("q (dc w) -> q dc w", w=512),
                  ("sc_xn",), [("xnt", pb)])
            q.dma("gpsimd", xn_tile_dram(p, p.sc_xr, tt),
                  xrt[pb][:].rearrange("q (dc w) -> q dc w", w=512),
                  ("sc_xr",), [("xrt", pb)])
        q.finish()


def ph2b(p):
    nc = p.nc
    with ExitStack() as ctx:
        sbt = lambda n, s, d: ctx.enter_context(nc.sbuf_tensor(n, s, d))
        wq = sbt("b_wqkv", [128, NDC * 12 * 128], BF16)
        xn = [sbt(f"b_xn{i}", [128, NDC * 512], BF16) for i in range(2)]
        zb = sbt("b_zb", [128, 12 * 516], BF16)
        cv = sbt("b_cv", [128, 512], BF16)
        cv2 = sbt("b_cv2", [128, 512], BF16)
        sq = sbt("b_sq", [128, 512], BF16)
        nrow = sbt("b_nrow", [1, 8 * 512], F32)
        nrowb = sbt("b_nrowb", [1, 8 * 512], BF16)
        stg = sbt("b_stg", [128, 12 * 512], BF16)
        q = AutoQ(p)
        q.dma("gpsimd", wq[:].rearrange("q (dc oc) -> q dc oc", oc=12 * 128),
              p.wqkvT.rearrange("(dc q) oc -> q dc oc", q=128), ("wq",), None)
        for oc in range(12):
            q.do("vector", lambda v, oc=oc: v.memset(zb[:, oc * 516:oc * 516 + 3], 0.0),
                 reads=[], writes=[("zb", oc)])
        for tt in range(NTT):
            q.dma("sync", xn[tt % 2][:].rearrange("q (dc w) -> q dc w", w=512),
                  xn_tile_dram(p, p.sc_xn, tt), ("xn", tt % 2), None)
            for oc in range(12):
                zo = oc * 516

                def grp(t, oc=oc, tt=tt):
                    bank = p.ps[oc % 2]
                    for dc in range(NDC):
                        ins = t.matmul(bank[0:128, 0:512],
                                       wq[:, (dc * 12 + oc) * 128:(dc * 12 + oc) * 128 + 128],
                                       xn[tt % 2][:, dc * 512:(dc + 1) * 512],
                                       start=(dc == 0), stop=(dc == NDC - 1))
                    return ins
                q.do("tensor", grp, reads=[("wq",), ("xn", tt % 2)], writes=[("ps", oc % 2)])
                q.do("vector", lambda v, zo=zo, oc=oc: v.tensor_copy(
                    zb[:, zo + 3:zo + 515], p.ps[oc % 2][0:128, 0:512]),
                    reads=[("ps", oc % 2)], writes=[("zb", oc)])
                q.do("scalar", lambda sc, zo=zo, oc=oc: sc.mul(
                    out=cv2[:], in_=zb[:, zo + 3:zo + 515], mul=p.w_conv[:, oc * 4 + 3:oc * 4 + 4]),
                    reads=[("zb", oc)], writes=[("cv2",)])
                q.do("vector", lambda v, zo=zo, oc=oc: v.scalar_tensor_tensor(
                    cv[:], zb[:, zo + 2:zo + 514], p.w_conv[:, oc * 4 + 2:oc * 4 + 3],
                    cv2[:], ALU.mult, ALU.add),
                    reads=[("zb", oc), ("cv2",)], writes=[("cv",)])
                q.do("vector", lambda v, zo=zo, oc=oc: v.scalar_tensor_tensor(
                    cv[:], zb[:, zo + 1:zo + 513], p.w_conv[:, oc * 4 + 1:oc * 4 + 2],
                    cv[:], ALU.mult, ALU.add),
                    reads=[("zb", oc), ("cv",)], writes=[("cv",)])
                q.do("vector", lambda v, zo=zo, oc=oc: v.scalar_tensor_tensor(
                    cv[:], zb[:, zo + 0:zo + 512], p.w_conv[:, oc * 4 + 0:oc * 4 + 1],
                    cv[:], ALU.mult, ALU.add),
                    reads=[("zb", oc), ("cv",)], writes=[("cv",)])
                q.do("vector", lambda v, zo=zo: v.tensor_copy(zb[:, zo:zo + 3], zb[:, zo + 512:zo + 515]),
                     reads=[("zb", oc)], writes=[("zb", oc)])
                q.do("scalar", lambda sc, oc=oc: sc.activation(
                    out=stg[:, oc * 512:(oc + 1) * 512], in_=cv[:], func=AF.Silu),
                    reads=[("cv",)], writes=[("stg", oc)])
                if oc < 8:
                    q.do("vector", lambda v, oc=oc: v.tensor_mul(
                        sq[:], stg[:, oc * 512:(oc + 1) * 512], stg[:, oc * 512:(oc + 1) * 512]),
                        reads=[("stg", oc)], writes=[("sq",)])
                    q.do("tensor", lambda t: t.matmul(
                        p.ps[2][0:1, 0:512], p.ones[:, 0:1], sq[:], start=True, stop=True),
                        reads=[("sq",)], writes=[("ps", 2)])
                    q.do("scalar", lambda sc, oc=oc: sc.copy(
                        out=nrow[0:1, oc * 512:(oc + 1) * 512], in_=p.ps[2][0:1, 0:512]),
                        reads=[("ps", 2)], writes=[("nrm", oc)])
            rws = [("nrm", i) for i in range(8)]
            q.do("vector", lambda v: v.tensor_scalar_max(nrow[:], nrow[:], 1e-24),
                 reads=rws, writes=rws)
            q.do("scalar", lambda sc: sc.activation(out=nrow[:], in_=nrow[:], func=AF.Sqrt),
                 reads=rws, writes=rws)
            q.do("vector", lambda v: v.reciprocal(nrow[:], nrow[:]),
                 reads=rws, writes=rws)
            q.do("vector", lambda v: v.tensor_copy(nrowb[:], nrow[:]),
                 reads=rws, writes=[("nrowb",)])
            for oc in range(8):
                q.do("tensor", lambda t, oc=oc: t.matmul(
                    p.ps[3][0:128, 0:512], p.ones[0:1, 0:128],
                    nrowb[0:1, oc * 512:(oc + 1) * 512], start=True, stop=True),
                    reads=[("nrowb",)], writes=[("ps", 3)])
                q.do("vector", lambda v, oc=oc: v.tensor_mul(
                    stg[:, oc * 512:(oc + 1) * 512], stg[:, oc * 512:(oc + 1) * 512],
                    p.ps[3][0:128, 0:512]),
                    reads=[("ps", 3), ("stg", oc)], writes=[("stg", oc)])
            for dst, lo in ((p.sc_qd, 0), (p.sc_kd, 4), (p.sc_vd, 8)):
                q.dma("gpsimd",
                      dst.rearrange("(oc q) w -> q oc w", q=128)[:, :, tt * 512:(tt + 1) * 512],
                      stg[:, lo * 512:(lo + 4) * 512].rearrange("q (oc w) -> q oc w", w=512),
                      ("spill", dst.tensor.name), [("stg", lo + i) for i in range(4)])
        q.finish()


def ph2c(p):
    nc = p.nc
    with ExitStack() as ctx:
        sbt = lambda n, s, d: ctx.enter_context(nc.sbuf_tensor(n, s, d))
        wab = sbt("c_wab", [128, NDC * 8], BF16)
        wg = sbt("c_wg", [128, NDC * 512], BF16)
        xn = [sbt(f"c_xn{i}", [128, NDC * 512], BF16) for i in range(2)]
        sig = sbt("c_sig", [8, 512], F32)
        la = sbt("c_la", [4, 512], F32)
        stg = sbt("c_stg", [128, 4 * 512], BF16)
        q = AutoQ(p)
        q.dma("gpsimd", wab[:].rearrange("q (dc o) -> q dc o", o=8),
              p.wabT.rearrange("(dc q) o -> q dc o", q=128), ("wab",), None)
        q.dma("gpsimd", wg[:].rearrange("q (dc o) -> q dc o", o=512),
              p.wgT.rearrange("(dc q) o -> q dc o", q=128), ("wg",), None)
        for tt in range(NTT):
            q.dma("sync", xn[tt % 2][:].rearrange("q (dc w) -> q dc w", w=512),
                  xn_tile_dram(p, p.sc_xn, tt), ("xn", tt % 2), None)

            def abgrp(t, tt=tt):
                for dc in range(NDC):
                    ins = t.matmul(p.ps[2][0:8, 0:512], wab[:, dc * 8:(dc + 1) * 8],
                                   xn[tt % 2][:, dc * 512:(dc + 1) * 512],
                                   start=(dc == 0), stop=(dc == NDC - 1))
                return ins
            q.do("tensor", abgrp, reads=[("wab",), ("xn", tt % 2)], writes=[("ps", 2)])
            q.do("scalar", lambda sc: sc.activation(
                out=sig[:], in_=p.ps[2][0:8, 0:512], func=AF.Sigmoid, bias=p.v_bab[:]),
                reads=[("ps", 2)], writes=[("sig",)])
            q.do("scalar", lambda sc: sc.activation(
                out=la[:], in_=sig[0:4, :], func=AF.Ln),
                reads=[("sig",)], writes=[("la",)])
            q.dma("gpsimd", p.sc_ab[0:8, tt * 512:(tt + 1) * 512], sig[:], ("sc_ab8",), [("sig",)])
            q.dma("gpsimd", p.sc_ab[8:12, tt * 512:(tt + 1) * 512], la[:], ("sc_ab4",), [("la",)])
            for oc in range(4):
                def ggrp(t, oc=oc, tt=tt):
                    bank = p.ps[oc % 2]
                    for dc in range(NDC):
                        ins = t.matmul(bank[0:128, 0:512],
                                       wg[:, (dc * 4 + oc) * 128:(dc * 4 + oc) * 128 + 128],
                                       xn[tt % 2][:, dc * 512:(dc + 1) * 512],
                                       start=(dc == 0), stop=(dc == NDC - 1))
                    return ins
                q.do("tensor", ggrp, reads=[("wg",), ("xn", tt % 2)], writes=[("ps", oc % 2)])
                q.do("scalar", lambda sc, oc=oc: sc.activation(
                    out=stg[:, oc * 512:(oc + 1) * 512], in_=p.ps[oc % 2][0:128, 0:512],
                    func=AF.Silu),
                    reads=[("ps", oc % 2)], writes=[("stg", oc)])
            q.dma("gpsimd",
                  p.sc_gg.rearrange("(oc q) w -> q oc w", q=128)[:, :, tt * 512:(tt + 1) * 512],
                  stg[:].rearrange("q (oc w) -> q oc w", w=512),
                  ("spill_gg",), [("stg", i) for i in range(4)])
        q.finish()


def ph2d(p):
    nc = p.nc
    with ExitStack() as ctx:
        sbt = lambda n, s, d: ctx.enter_context(nc.sbuf_tensor(n, s, d))
        wqf = sbt("d_wqf", [128, NDC * 512], BF16)
        wkf = sbt("d_wkf", [128, NDC * 512], BF16)
        xr = [sbt(f"d_xr{i}", [128, NDC * 512], BF16) for i in range(2)]
        stg = sbt("d_stg", [128, 8 * 512], BF16)
        sq = sbt("d_sq", [128, 512], BF16)
        nrow = sbt("d_nrow", [1, 8 * 512], F32)
        brow = sbt("d_brow", [1, 8 * 512], F32)
        bfin = sbt("d_bfin", [1, 8], F32)
        vt = sbt("d_vt", [128, 512], BF16)
        stv = sbt("d_stv", [128, 4 * 132], BF16)
        q = AutoQ(p)
        q.dma("gpsimd", wqf[:].rearrange("q (dc o) -> q dc o", o=512),
              p.wqfT.rearrange("(dc q) o -> q dc o", q=128), ("wqf",), None)
        q.dma("gpsimd", wkf[:].rearrange("q (dc o) -> q dc o", o=512),
              p.wkfT.rearrange("(dc q) o -> q dc o", q=128), ("wkf",), None)
        for dc in range(NDC):
            q.do("scalar", lambda sc, dc=dc: sc.mul(
                out=wqf[:, dc * 512:(dc + 1) * 512], in_=wqf[:, dc * 512:(dc + 1) * 512],
                mul=p.v_ln1[:, dc:dc + 1]),
                reads=[("wqf",)], writes=[("wqf",)])
            q.do("scalar", lambda sc, dc=dc: sc.mul(
                out=wkf[:, dc * 512:(dc + 1) * 512], in_=wkf[:, dc * 512:(dc + 1) * 512],
                mul=p.v_ln1[:, dc:dc + 1]),
                reads=[("wkf",)], writes=[("wkf",)])
        q.do("vector", lambda v: v.memset(brow[:], 0.0), reads=[], writes=[("brow",)])
        q.do("vector", lambda v: v.memset(
            stv[:].rearrange("q (g c) -> q g c", c=33)[:, :, 16:32], 0.0),
            reads=[], writes=[("stv_ones",)])
        q.do("vector", lambda v: v.memset(
            stv[:].rearrange("q (g c) -> q g c", c=33)[:, :, 32:33], 1.0),
            reads=[("stv_ones",)], writes=[("stv_ones",)])
        for tt in range(NTT):
            q.dma("sync", xr[tt % 2][:].rearrange("q (dc w) -> q dc w", w=512),
                  xn_tile_dram(p, p.sc_xr, tt), ("xr", tt % 2), None)
            for oc in range(8):
                w = wqf if oc < 4 else wkf
                wb = ("wqf",) if oc < 4 else ("wkf",)

                def grp(t, oc=oc, tt=tt, w=w):
                    bank = p.ps[oc % 2]
                    for dc in range(NDC):
                        ins = t.matmul(bank[0:128, 0:512],
                                       w[:, (dc * 4 + oc % 4) * 128:(dc * 4 + oc % 4) * 128 + 128],
                                       xr[tt % 2][:, dc * 512:(dc + 1) * 512],
                                       start=(dc == 0), stop=(dc == NDC - 1))
                    return ins
                q.do("tensor", grp, reads=[wb, ("xr", tt % 2)], writes=[("ps", oc % 2)])
                q.do("scalar", lambda sc, oc=oc: sc.copy(
                    out=stg[:, oc * 512:(oc + 1) * 512], in_=p.ps[oc % 2][0:128, 0:512]),
                    reads=[("ps", oc % 2)], writes=[("stg", oc)])
                q.do("vector", lambda v, oc=oc: v.tensor_mul(
                    sq[:], stg[:, oc * 512:(oc + 1) * 512], stg[:, oc * 512:(oc + 1) * 512]),
                    reads=[("stg", oc)], writes=[("sq",)])
                q.do("tensor", lambda t: t.matmul(
                    p.ps[2][0:1, 0:512], p.ones[:, 0:1], sq[:], start=True, stop=True),
                    reads=[("sq",)], writes=[("ps", 2)])
                q.do("scalar", lambda sc, oc=oc: sc.copy(
                    out=nrow[0:1, oc * 512:(oc + 1) * 512], in_=p.ps[2][0:1, 0:512]),
                    reads=[("ps", 2)], writes=[("nrm", oc)])
            for dst, lo in ((p.sc_qf, 0), (p.sc_kf, 4)):
                q.dma("gpsimd",
                      dst.rearrange("(oc q) w -> q oc w", q=128)[:, :, tt * 512:(tt + 1) * 512],
                      stg[:, lo * 512:(lo + 4) * 512].rearrange("q (oc w) -> q oc w", w=512),
                      ("spill", dst.tensor.name), [("stg", lo + i) for i in range(4)])
            q.do("vector", lambda v: v.tensor_max(brow[:], brow[:], nrow[:]),
                 reads=[("brow",)] + [("nrm", i) for i in range(8)], writes=[("brow",)])
            def vgrp(t, tt=tt):
                for dc in range(NDC):
                    ins = t.matmul(p.ps[3][0:64, 0:512], p.w_pall[:, dc * 64:(dc + 1) * 64],
                                   xr[tt % 2][:, dc * 512:(dc + 1) * 512],
                                   start=(dc == 0), stop=(dc == NDC - 1))
                return ins
            q.do("tensor", vgrp, reads=[("xr", tt % 2)], writes=[("ps", 3)])
            q.do("vector", lambda v: v.tensor_copy(vt[0:64, :], p.ps[3][0:64, 0:512]),
                 reads=[("ps", 3)], writes=[("vt",)])
            for j in range(4):
                q.do("tensor", lambda t, j=j: t.transpose(
                    p.psb[0][0:128, j * 64:(j + 1) * 64], vt[0:64, j * 128:(j + 1) * 128],
                    p.ident[0:64, 0:64]),
                    reads=[("vt",)], writes=[("psb0",)])
                q.do("vector", lambda v, j=j: v.tensor_copy(
                    stv[:, j * 132:j * 132 + 132].rearrange("q (h c) -> q h c", c=33)[:, :, 0:16],
                    p.psb[0][0:128, j * 64:(j + 1) * 64].rearrange("q (h c) -> q h c", c=16)),
                    reads=[("psb0",)], writes=[("stv", j)])
            q.dma("gpsimd",
                  p.sc_vt[tt * 512:(tt + 1) * 512, :].rearrange("(j q) c -> q j c", q=128),
                  stv[:].rearrange("q (j c) -> q j c", c=132),
                  ("spill_vt",), [("stv", j) for j in range(4)] + [("stv_ones",)])
        for oc in range(8):
            q.do("vector", lambda v, oc=oc: v.reduce_max(
                bfin[0:1, oc:oc + 1], brow[0:1, oc * 512:(oc + 1) * 512],
                axis=mybir.AxisListType.X),
                reads=[("brow",)], writes=[("bfin",)])
        q.dma("gpsimd", p.sc_bnd[0:1, 0:8], bfin[:], ("sc_bnd0",), [("bfin",)])
        q.finish()


def ph3(p):
    nc = p.nc
    with ExitStack() as ctx:
        sbt = lambda n, s, d: ctx.enter_context(nc.sbuf_tensor(n, s, d))
        qh = sbt("e_qh", [128, T], BF16)
        kh = sbt("e_kh", [128, T], BF16)
        vh = sbt("e_vh", [128, T], BF16)
        gh = sbt("e_gh", [128, T], BF16)
        ogh = sbt("e_ogh", [128, T], BF16)
        lach = sbt("e_lach", [64, 128], F32)
        Lch = sbt("e_Lch", [64, 128], F32)
        bch = sbt("e_bch", [64, 128], F32)
        LT = sbt("e_LT", [128, 64], F32)
        bT = sbt("e_bT", [128, 64], F32)
        gT = sbt("e_gT", [128, 64], F32)
        nbgT = sbt("e_nbgT", [128, 64], F32)
        rrT = sbt("e_rrT", [128, 64], F32)
        gcb = sbt("e_gcb", [128, 64], F32)
        lcrow = sbt("e_lcrow", [1, 64], F32)
        Lrow = sbt("e_Lrow", [1, T], F32)
        brow = sbt("e_brow", [1, T], BF16)
        onesf = sbt("e_onesf", [1, 128], F32)
        identf = sbt("e_identf", [64, 64], F32)
        gbc = sbt("e_gbc", [128, 128], BF16)
        dtile = sbt("e_dtile", [128, 128], F32)
        d2t = sbt("e_d2t", [128, 128], F32)
        g2 = sbt("e_g2", [128, 128], BF16)
        bm = sbt("e_bm", [128, 128], BF16)
        tp1 = sbt("e_tp1", [128, 128], BF16)
        tpr = sbt("e_tpr", [128, 128], BF16)
        tun = sbt("e_tun", [128, 128], BF16)
        aqk = sbt("e_aqk", [128, 128], BF16)
        powu = [sbt(f"e_powu{i}", [128, 128], BF16) for i in range(6)]
        powp = [sbt(f"e_powp{i}", [128, 128], BF16) for i in range(5)]
        rch = [sbt(f"e_rch{i}", [128, 128], BF16) for i in range(2)]
        vtb = sbt("e_vtb", [128, 128], BF16)
        btile = sbt("e_btile", [128, 128], BF16)
        ut = sbt("e_ut", [128, 128], BF16)
        qt2 = sbt("e_qt2", [128, 128], BF16)
        ktr = sbt("e_ktr", [128, 128], BF16)
        stf = sbt("e_stf", [128, 128], F32)
        stb = sbt("e_stb", [128, 128], BF16)
        q = AutoQ(p)
        q.do("vector", lambda v: v.memset(onesf[:], 1.0), writes=[("onesf",)])
        q.dma("sync", identf[:], p.c_identf, ("identf",), None)
        for h in range(NHL):
            q.dma("sync", qh[:], p.sc_qd[h * 128:(h + 1) * 128, :], ("qh",), None)
            q.dma("sync", kh[:], p.sc_kd[h * 128:(h + 1) * 128, :], ("kh",), None)
            q.dma("sync", vh[:], p.sc_vd[h * 128:(h + 1) * 128, :], ("vh",), None)
            q.dma("sync", gh[:], p.sc_gg[h * 128:(h + 1) * 128, :], ("gh",), None)
            q.dma("sync", lach[:], p.sc_ab.rearrange("o (c w) -> o c w", w=128)[8 + h],
                  ("lach",), None)
            q.dma("sync", bch[:], p.sc_ab.rearrange("o (c w) -> o c w", w=128)[4 + h],
                  ("bch",), None)
            q.dma("gpsimd", brow[:], p.sc_ab[4 + h:5 + h, :], ("brow",), None)
            q.do("vector", lambda v: v.tensor_tensor_scan(
                Lch[:], lach[:], p.zeros[0:64, 0:128], 0.0, ALU.add, ALU.add),
                reads=[("lach",)], writes=[("Lch",)])
            q.dma("gpsimd", p.sc_st.rearrange("o (c w) -> o c w", w=128)[2], Lch[:],
                  ("strow2",), [("Lch",)])
            q.dma("sync", Lrow[:], p.sc_st[2:3, :], ("Lrow",), [("strow2",)])
            q.do("tensor", lambda t: t.transpose(p.ps[4][0:128, 0:64], Lch[:], identf[:]),
                 reads=[("Lch",), ("identf",)], writes=[("ps", 4)])
            q.do("vector", lambda v: v.tensor_copy(LT[:], p.ps[4][0:128, 0:64]),
                 reads=[("ps", 4)], writes=[("LT",)])
            q.do("tensor", lambda t: t.transpose(p.ps[4][0:128, 0:64], bch[:], identf[:]),
                 reads=[("bch",), ("identf",)], writes=[("ps", 4)])
            q.do("vector", lambda v: v.tensor_copy(bT[:], p.ps[4][0:128, 0:64]),
                 reads=[("ps", 4)], writes=[("bT",)])
            q.do("scalar", lambda sc: sc.activation(out=gT[:], in_=LT[:], func=AF.Exp),
                 reads=[("LT",)], writes=[("gT",)])
            q.do("vector", lambda v: v.tensor_mul(nbgT[:], bT[:], gT[:]),
                 reads=[("bT",), ("gT",)], writes=[("nbgT",)])
            q.do("vector", lambda v: v.tensor_scalar_mul(nbgT[:], nbgT[:], -1.0),
                 reads=[("nbgT",)], writes=[("nbgT",)])
            q.do("tensor", lambda t: t.transpose(p.ps[4][0:1, 0:64], Lch[:, 127:128], identf[:]),
                 reads=[("Lch",), ("identf",)], writes=[("ps", 4)])
            q.do("vector", lambda v: v.tensor_copy(lcrow[:], p.ps[4][0:1, 0:64]),
                 reads=[("ps", 4)], writes=[("lcrow",)])
            q.do("tensor", lambda t: t.matmul(p.ps[4][0:128, 0:64], onesf[:],
                                              lcrow[:], start=True, stop=True),
                 reads=[("lcrow",), ("onesf",)], writes=[("ps", 4)])
            q.do("vector", lambda v: v.tensor_sub(rrT[:], p.ps[4][0:128, 0:64], LT[:]),
                 reads=[("ps", 4), ("LT",)], writes=[("rrT",)])
            q.do("scalar", lambda sc: sc.activation(out=rrT[:], in_=rrT[:], func=AF.Exp),
                 reads=[("rrT",)], writes=[("rrT",)])
            q.do("scalar", lambda sc: sc.activation(out=gcb[:], in_=p.ps[4][0:128, 0:64],
                                                    func=AF.Exp),
                 reads=[("ps", 4)], writes=[("gcb",)])
            q.do("vector", lambda v: v.memset(stf[:], 0.0), writes=[("stf",)])
            q.do("vector", lambda v: v.memset(stb[:], 0.0), writes=[("stb",)])
            for c in range(NCH):
                cs = slice(c * CD, (c + 1) * CD)
                q.do("tensor", lambda t, cs=cs: t.matmul(
                    p.ps[4][0:128, 0:128], onesf[:], Lrow[0:1, cs], start=True, stop=True),
                    reads=[("Lrow",), ("onesf",)], writes=[("ps", 4)])
                q.do("scalar", lambda sc: sc.activation(out=gbc[:], in_=p.ps[4][0:128, 0:128],
                                                        func=AF.Exp),
                     reads=[("ps", 4)], writes=[("gbc",)])
                q.do("tensor", lambda t, cs=cs: t.matmul(
                    p.ps[5][0:128, 0:128], Lrow[0:1, cs], onesf[:], start=True, stop=True),
                    reads=[("Lrow",), ("onesf",)], writes=[("ps", 5)])
                q.do("vector", lambda v: v.tensor_copy(d2t[:], p.ps[5][0:128, 0:128]),
                     reads=[("ps", 5)], writes=[("d2t",)])
                q.do("vector", lambda v: v.tensor_sub(dtile[:], p.ps[4][0:128, 0:128],
                                                      d2t[:]),
                     reads=[("ps", 4), ("d2t",)], writes=[("dtile",)])
                q.do("vector", lambda v: v.tensor_scalar_min(dtile[:], dtile[:], 0.0),
                     reads=[("dtile",)], writes=[("dtile",)])
                q.do("scalar", lambda sc: sc.activation(out=g2[:], in_=dtile[:], func=AF.Exp),
                     reads=[("dtile",)], writes=[("g2",)])
                q.do("tensor", lambda t, cs=cs: t.matmul(
                    p.ps[0][0:128, 0:128], kh[:, cs], kh[:, cs], start=True, stop=True),
                    reads=[("kh",)], writes=[("ps", 0)])
                q.do("tensor", lambda t, cs=cs: t.matmul(
                    p.ps[1][0:128, 0:128], kh[:, cs], qh[:, cs], start=True, stop=True),
                    reads=[("kh",), ("qh",)], writes=[("ps", 1)])
                q.do("tensor", lambda t, cs=cs: t.matmul(
                    p.ps[2][0:128, 0:128], p.ones[0:1, 0:128], brow[0:1, cs],
                    start=True, stop=True),
                    reads=[("brow",)], writes=[("ps", 2)])
                q.do("vector", lambda v: v.tensor_mul(bm[:], p.maskUs[:], p.ps[2][0:128, 0:128]),
                     reads=[("ps", 2)], writes=[("bm",)])
                q.do("vector", lambda v: v.tensor_mul(tp1[:], p.ps[0][0:128, 0:128], g2[:]),
                     reads=[("ps", 0), ("g2",)], writes=[("tp1",)])
                q.do("vector", lambda v: v.tensor_mul(tpr[:], tp1[:], bm[:]),
                     reads=[("tp1",), ("bm",)], writes=[("tpr",)])
                q.do("vector", lambda v: v.tensor_mul(tp1[:], p.ps[1][0:128, 0:128], g2[:]),
                     reads=[("ps", 1), ("g2",)], writes=[("tp1",)])
                q.do("vector", lambda v: v.tensor_mul(aqk[:], tp1[:], p.maskUi[:]),
                     reads=[("tp1",)], writes=[("aqk",)])
                q.do("tensor", lambda t: t.transpose(p.psb[0][0:128, 0:128], tpr[:], p.ident[:]),
                     reads=[("tpr",)], writes=[("psb0",)])
                q.do("vector", lambda v: v.tensor_copy(tun[:], p.psb[0][0:128, 0:128]),
                     reads=[("psb0",)], writes=[("tun",)])
                pu, pp = [("tun",)], [("tpr",)]
                for i in range(6):
                    lhs_p = tpr if i == 0 else powp[i - 1]
                    lhs_u = tun if i == 0 else powu[i - 1]
                    rb_p = pp[-1]
                    rb_u = pu[-1]
                    q.do("tensor", lambda t, lhs_p=lhs_p, lhs_u=lhs_u: t.matmul(
                        p.ps[3][0:128, 0:128], lhs_p[:], lhs_u[:], start=True, stop=True),
                        reads=[rb_p, rb_u], writes=[("ps", 3)])
                    q.do("vector", lambda v, i=i: v.tensor_copy(powu[i][:], p.ps[3][0:128, 0:128]),
                         reads=[("ps", 3)], writes=[("powu", i)])
                    pu.append(("powu", i))
                    if i < 5:
                        q.do("tensor", lambda t, lhs_p=lhs_p, lhs_u=lhs_u: t.matmul(
                            p.ps[3][0:128, 0:128], lhs_u[:], lhs_p[:], start=True, stop=True),
                            reads=[rb_p, rb_u], writes=[("ps", 3)])
                        q.do("vector", lambda v, i=i: v.tensor_copy(powp[i][:],
                                                                    p.ps[3][0:128, 0:128]),
                             reads=[("ps", 3)], writes=[("powp", i)])
                        pp.append(("powp", i))
                q.do("vector", lambda v: v.tensor_sub(rch[0][:], p.ident[:], tpr[:]),
                     reads=[("tpr",)], writes=[("rch", 0)])
                for i in range(6):
                    src_r = rch[i % 2]
                    dst_r = rch[(i + 1) % 2]

                    def rstep(t, src_r=src_r, i=i):
                        t.matmul(p.ps[3][0:128, 0:128], p.ident[:], src_r[:],
                                 start=True, stop=False)
                        return t.matmul(p.ps[3][0:128, 0:128], powu[i][:], src_r[:],
                                        start=False, stop=True)
                    q.do("tensor", rstep, reads=[("rch", i % 2), ("powu", i)],
                         writes=[("ps", 3)])
                    q.do("vector", lambda v, dst_r=dst_r: v.tensor_copy(dst_r[:],
                                                                        p.ps[3][0:128, 0:128]),
                         reads=[("ps", 3)], writes=[("rch", (i + 1) % 2)])
                invb = ("rch", 0)
                q.do("tensor", lambda t, cs=cs: t.matmul(
                    p.ps[3][0:128, 0:128], kh[:, cs], stb[:], start=True, stop=True),
                    reads=[("kh",), ("stb",)], writes=[("ps", 3)])
                q.do("tensor", lambda t, cs=cs: t.transpose(
                    p.psb[1][0:128, 0:128], vh[:, cs], p.ident[:]),
                    reads=[("vh",)], writes=[("psb1",)])
                q.do("vector", lambda v, c=c: v.tensor_scalar(
                    vtb[:], p.psb[1][0:128, 0:128], bT[:, c:c + 1], None, ALU.mult),
                    reads=[("psb1",), ("bT",)], writes=[("vtb",)])
                q.do("vector", lambda v, c=c: v.scalar_tensor_tensor(
                    btile[:], p.ps[3][0:128, 0:128], nbgT[:, c:c + 1], vtb[:],
                    ALU.mult, ALU.add),
                    reads=[("ps", 3), ("nbgT",), ("vtb",)], writes=[("btile",)])
                q.do("tensor", lambda t: t.matmul(
                    p.ps[5][0:128, 0:128], rch[0][:], btile[:], start=True, stop=True),
                    reads=[invb, ("btile",)], writes=[("ps", 5)])
                q.do("vector", lambda v: v.tensor_copy(ut[:], p.ps[5][0:128, 0:128]),
                     reads=[("ps", 5)], writes=[("ut",)])
                q.do("vector", lambda v, cs=cs: v.tensor_mul(qt2[:], qh[:, cs], gbc[:]),
                     reads=[("qh",), ("gbc",)], writes=[("qt2",)])

                def ogrp(t):
                    t.matmul(p.ps[0][0:128, 0:128], stb[:], qt2[:], start=True, stop=False)
                    return t.matmul(p.ps[0][0:128, 0:128], ut[:], aqk[:],
                                    start=False, stop=True)
                q.do("tensor", ogrp, reads=[("stb",), ("qt2",), ("ut",), ("aqk",)],
                     writes=[("ps", 0)])
                q.do("vector", lambda v, cs=cs: v.tensor_mul(
                    ogh[:, cs], p.ps[0][0:128, 0:128], gh[:, cs]),
                    reads=[("ps", 0), ("gh",)], writes=[("ogh",)])
                q.do("tensor", lambda t, cs=cs: t.transpose(
                    p.psb[1][0:128, 0:128], kh[:, cs], p.ident[:]),
                    reads=[("kh",)], writes=[("psb1",)])
                q.do("vector", lambda v, c=c: v.tensor_scalar(
                    ktr[:], p.psb[1][0:128, 0:128], rrT[:, c:c + 1], None, ALU.mult),
                    reads=[("psb1",), ("rrT",)], writes=[("ktr",)])
                q.do("tensor", lambda t: t.matmul(
                    p.ps[1][0:128, 0:128], ktr[:], ut[:], start=True, stop=True),
                    reads=[("ktr",), ("ut",)], writes=[("ps", 1)])
                q.do("vector", lambda v, c=c: v.scalar_tensor_tensor(
                    stf[:], stf[:], gcb[:, c:c + 1], p.ps[1][0:128, 0:128],
                    ALU.mult, ALU.add),
                    reads=[("stf",), ("gcb",), ("ps", 1)], writes=[("stf",)])
                q.do("vector", lambda v: v.tensor_copy(stb[:], stf[:]),
                     reads=[("stf",)], writes=[("stb",)])
            q.dma("gpsimd", p.ag_og_in[h * 128:(h + 1) * 128, :], ogh[:],
                  ("og_sp", h), [("ogh",)])
        q.do("gpsimd", lambda g: g.collective_compute(
            "AllGather", ALU.bypass, replica_groups=[[0, 1, 2, 3], [4, 5, 6, 7]],
            ins=[p.ag_og_in], outs=[p.ag_og_out]),
            reads=[("og_sp", 0), ("og_sp", 1), ("og_sp", 2), ("og_sp", 3)],
            writes=[("og_out",)])
        q.finish()


def ph4(p):
    nc = p.nc
    ISR = 1.0 / math.sqrt(128.0)
    with ExitStack() as ctx:
        sbt = lambda n, s, d: ctx.enter_context(nc.sbuf_tensor(n, s, d))
        qf = sbt("f_qf", [128, T], BF16)
        kf = sbt("f_kf", [128, T], BF16)
        vt_sb = sbt("f_vt", [128, 64 * 132], BF16)
        probs = [sbt(f"f_pr{i}", [128, 512], BF16) for i in range(3)]
        macc = sbt("f_macc", [16, T], F32)
        bnd = sbt("f_bnd", [1, 8], F32)
        bmh = sbt("f_bmh", [128, 1], F32)
        rl = sbt("f_rl", [1, 512], F32)
        rlb = sbt("f_rlb", [1, 512], BF16)
        rlbc = sbt("f_rlbc", [16, 512], F32)
        mtmp = sbt("f_mtmp", [16, 512], F32)
        onesf = sbt("f_onesf", [1, 128], F32)
        q = AutoQ(p)
        q.do("vector", lambda v: v.memset(onesf[:], 1.0), writes=[("onesf",)])
        q.do("vector", lambda v: v.memset(macc[:], 0.0), writes=[("macc",)])
        q.dma("sync", vt_sb[:].rearrange("q (kc c) -> q kc c", c=132),
              p.sc_vt.rearrange("(kc q) c -> q kc c", q=128), ("vt",), None)
        q.dma("sync", bnd[:], p.sc_bnd, ("bnd",), None)
        q.do("vector", lambda v: v.tensor_mul(bnd[0:1, 4:8], bnd[0:1, 0:4], bnd[0:1, 4:8]),
             reads=[("bnd",)], writes=[("bnd",)])
        q.do("vector", lambda v: v.tensor_scalar_mul(bnd[0:1, 4:8], bnd[0:1, 4:8], 1.0 / 128.0),
             reads=[("bnd",)], writes=[("bnd",)])
        q.do("scalar", lambda sc: sc.activation(out=bnd[0:1, 4:8], in_=bnd[0:1, 4:8],
                                                func=AF.Sqrt),
             reads=[("bnd",)], writes=[("bnd",)])
        q.do("vector", lambda v: v.tensor_scalar_mul(bnd[0:1, 4:8], bnd[0:1, 4:8], -1.0),
             reads=[("bnd",)], writes=[("bnd",)])
        for h in range(NHL):
            if h < 3:
                ph7_record(p, q, ctx, h)
            q.dma("sync", qf[:], p.sc_qf[h * 128:(h + 1) * 128, :], ("qf",), None)
            q.dma("sync", kf[:], p.sc_kf[h * 128:(h + 1) * 128, :], ("kf",), None)
            q.do("tensor", lambda t, h=h: t.matmul(
                p.ps[3][0:128, 0:1], p.onesf32[:], bnd[0:1, 4 + h:5 + h],
                start=True, stop=True),
                reads=[("bnd",)], writes=[("ps", 3)])
            q.do("vector", lambda v: v.tensor_copy(bmh[:], p.ps[3][0:128, 0:1]),
                 reads=[("ps", 3)], writes=[("bmh",)])
            for qt in range(NTT):
                nkc = 4 * qt + 4

                def pvstep(kc, h=h, nkc=nkc):
                    pb3 = kc % 3
                    q.do("tensor", lambda t, kc=kc, pb3=pb3, s=(kc == 0), e=(kc == nkc - 1): t.matmul(
                        p.ps[2][0:33, 0:512],
                        vt_sb[:, kc * 132 + 33 * h:kc * 132 + 33 * h + 33],
                        probs[pb3][:], start=s, stop=e),
                        reads=[("probs", pb3), ("vt",)], writes=[("pspv",)])
                for kc in range(nkc):
                    pb = kc % 2
                    pb3 = kc % 3
                    q.do("tensor", lambda t, kc=kc, qt=qt, pb=pb: t.matmul(
                        p.ps[pb][0:128, 0:512], kf[:, kc * 128:(kc + 1) * 128],
                        qf[:, qt * 512:(qt + 1) * 512], start=True, stop=True),
                        reads=[("kf",), ("qf",)], writes=[("ps", pb)])
                    q.do("scalar", lambda sc, pb=pb, pb3=pb3: sc.activation(
                        out=probs[pb3][:], in_=p.ps[pb][0:128, 0:512], func=AF.Exp,
                        bias=bmh[:], scale=ISR),
                        reads=[("ps", pb), ("bmh",)], writes=[("probs", pb3)])
                    if kc >= 4 * qt:
                        a = kc - 4 * qt
                        q.do("vector", lambda v, pb3=pb3, a=a: v.tensor_mul(
                            probs[pb3][:], probs[pb3][:], p.diagm[:, a * 512:(a + 1) * 512]),
                            reads=[("probs", pb3)], writes=[("probs", pb3)])
                    if kc >= 1:
                        pvstep(kc - 1)
                pvstep(nkc - 1)
                q.do("vector", lambda v: v.reciprocal(rl[:], p.ps[2][32:33, 0:512]),
                     reads=[("pspv",)], writes=[("rl",)])
                q.do("vector", lambda v: v.tensor_copy(rlb[:], rl[:]),
                     reads=[("rl",)], writes=[("rlb",)])
                q.do("tensor", lambda t: t.matmul(
                    p.ps[3][0:16, 0:512], p.ones[0:1, 0:16], rlb[:], start=True, stop=True),
                    reads=[("rlb",)], writes=[("ps", 3)])
                q.do("vector", lambda v: v.tensor_copy(rlbc[:], p.ps[3][0:16, 0:512]),
                     reads=[("ps", 3)], writes=[("rlbc",)])
                q.do("vector", lambda v: v.tensor_mul(
                    mtmp[:], p.ps[2][0:16, 0:512], rlbc[:]),
                    reads=[("pspv",), ("rlbc",)], writes=[("mtmp",)])
                q.do("vector", lambda v, qt=qt: v.tensor_add(
                    macc[:, qt * 512:(qt + 1) * 512], macc[:, qt * 512:(qt + 1) * 512],
                    mtmp[:]),
                    reads=[("mtmp",), ("macc",)], writes=[("macc",)])
        q.dma("gpsimd", p.ar_m_in, macc[:], ("arin",), [("macc",)])
        q.do("gpsimd", lambda g: g.collective_compute(
            "AllReduce", ALU.add, replica_groups=[[0, 1, 2, 3], [4, 5, 6, 7]],
            ins=[p.ar_m_in], outs=[p.ar_m_out]),
            reads=[("arin",)], writes=[("arout",)])
        q.finish()


def ph5(p):
    nc = p.nc
    with ExitStack() as ctx0:
        Bsb = ctx0.enter_context(nc.sbuf_tensor("g_B", [16, FF], BF16))
        qz = AutoQ(p)
        qz.dma("gpsimd", Bsb[:], p.B_, ("Bsb",), None)
        qz.finish()
        nc.all_engine_barrier()
        for u in range(4):
            with ExitStack() as s1:
                sbt = lambda n, s, d: s1.enter_context(nc.sbuf_tensor(n, s, d))
                xq = sbt(f"g_xq{u}", [128, NDC * 512], F32)
                gsb = sbt(f"g_g{u}", [128, NDC * 512], BF16)
                prod = sbt(f"g_prod{u}", [128, 64 * 512], BF16)
                bcs = sbt(f"g_bcs{u}", [128, 2 * 512], BF16)
                srow = sbt(f"g_srow{u}", [1, 2 * 512], BF16)
                modt = sbt(f"g_modt{u}", [16, 512], F32)
                mh = sbt(f"g_mh{u}", [16, 512], BF16)
                tmp = sbt(f"g_tmp{u}", [128, 512], F32)
                tmp2 = sbt(f"g_tmp2{u}", [128, 512], F32)
                qa = AutoQ(p)
                qa.dma("sync", xq[:].rearrange("q (dc w) -> q dc w", w=512),
                       p.xq4T.rearrange("(dc q) w -> q dc w", q=128)[:, :, u * 512:(u + 1) * 512],
                       ("xq",), None)
                qa.dma("gpsimd", srow[:], dyn_off(p.sc_stb[0:2, 0:512], p.qcol[u]),
                       ("srow",), None)
                for j in range(2):
                    qa.do("tensor", lambda t, j=j: t.matmul(
                        p.ps[2 + j][0:128, 0:512], p.ones[0:1, 0:128],
                        srow[0:1, j * 512:(j + 1) * 512], start=True, stop=True),
                        reads=[("srow",)], writes=[("ps", 2 + j)])
                    qa.do("vector", lambda v, j=j: v.tensor_copy(
                        bcs[:, j * 512:(j + 1) * 512], p.ps[2 + j][0:128, 0:512]),
                        reads=[("ps", 2 + j)], writes=[("bcs",)])
                qa.dma("gpsimd", modt[:], dyn_off(p.ar_m_out[:, 0:512], p.qcol[u]),
                       ("modt",), None)
                qa.do("scalar", lambda sc: sc.activation(out=modt[:], in_=modt[:], func=AF.Tanh),
                      reads=[("modt",)], writes=[("modt",)])
                qa.do("vector", lambda v: v.tensor_scalar_mul(modt[:], modt[:], 0.1),
                      reads=[("modt",)], writes=[("modt",)])
                qa.finish()
                nc.all_engine_barrier()
                with ExitStack() as s2:
                    sb2 = lambda n, s, d: s2.enter_context(nc.sbuf_tensor(n, s, d))
                    hb = sb2(f"g_hb{u}", [128, NDC * 512], BF16)
                    win = [sb2(f"g_win{u}_{i}", [128, NDC * 128], BF16) for i in range(3)]
                    wup = [sb2(f"g_wup{u}_{i}", [128, NDC * 128], BF16) for i in range(3)]
                    gt = [sb2(f"g_gt{u}_{i}", [128, 512], BF16) for i in range(2)]
                    qc = AutoQ(p)
                    qc.dma("gpsimd", hb[:].rearrange("q (dc w) -> q dc w", w=512),
                           dyn_off(p.sc_xr.rearrange("(dc q) w -> q dc w", q=128)
                                   [:, :, 0:512], p.qcol[u]), ("hbx",), None)
                    for dc in range(NDC):
                        qc.do("vector", lambda v, dc=dc: v.tensor_scalar(
                            hb[:, dc * 512:(dc + 1) * 512], hb[:, dc * 512:(dc + 1) * 512],
                            p.v_ln2[:, dc:dc + 1], None, ALU.mult),
                            reads=[("hbx",)], writes=[("hbx",)])
                    def hagrp(t):
                        for dc in range(NDC):
                            ins = t.matmul(p.ps[4][0:16, 0:512],
                                           p.w_a2[:, dc * 16:(dc + 1) * 16],
                                           hb[:, dc * 512:(dc + 1) * 512],
                                           start=(dc == 0), stop=(dc == NDC - 1))
                        return ins
                    qc.do("tensor", hagrp, reads=[("hbx",)], writes=[("ps", 4)])
                    qc.do("vector", lambda v: v.tensor_mul(mh[:], modt[:], p.ps[4][0:16, 0:512]),
                          reads=[("ps", 4), ("modt",)], writes=[("mh",)])
                    for fc in range(64):
                        wb = fc % 3
                        gb = fc % 2
                        qc.dma("gpsimd", win[wb][:].rearrange("q (dc w) -> q dc w", w=128),
                               p.winT.rearrange("(dc q) w -> q dc w", q=128)
                               [:, :, fc * 128:(fc + 1) * 128], ("win", wb), None)
                        qc.dma("gpsimd", wup[wb][:].rearrange("q (dc w) -> q dc w", w=128),
                               p.wupT.rearrange("(dc q) w -> q dc w", q=128)
                               [:, :, fc * 128:(fc + 1) * 128], ("wup", wb), None)

                        def gategrp(t, wb=wb, fc=fc, gb=gb):
                            for dc in range(NDC):
                                t.matmul(p.ps[2 * gb][0:128, 0:512],
                                         win[wb][:, dc * 128:(dc + 1) * 128],
                                         hb[:, dc * 512:(dc + 1) * 512],
                                         start=(dc == 0), stop=False)
                            return t.matmul(p.ps[2 * gb][0:128, 0:512],
                                            Bsb[:, fc * 128:(fc + 1) * 128], mh[:],
                                            start=False, stop=True)
                        qc.do("tensor", gategrp,
                              reads=[("win", wb), ("hbx",), ("mh",), ("Bsb",)],
                              writes=[("ps", 2 * gb)])

                        def upgrp(t, wb=wb, gb=gb):
                            for dc in range(NDC):
                                ins = t.matmul(p.ps[2 * gb + 1][0:128, 0:512],
                                               wup[wb][:, dc * 128:(dc + 1) * 128],
                                               hb[:, dc * 512:(dc + 1) * 512],
                                               start=(dc == 0), stop=(dc == NDC - 1))
                            return ins
                        qc.do("tensor", upgrp, reads=[("wup", wb), ("hbx",)],
                              writes=[("ps", 2 * gb + 1)])
                        qc.do("scalar", lambda sc, gb=gb: sc.activation(
                            out=gt[gb][:], in_=p.ps[2 * gb][0:128, 0:512], func=AF.Silu),
                            reads=[("ps", 2 * gb)], writes=[("gt", gb)])
                        qc.do("vector", lambda v, fc=fc, gb=gb: v.tensor_mul(
                            prod[:, fc * 512:(fc + 1) * 512], gt[gb][:],
                            p.ps[2 * gb + 1][0:128, 0:512]),
                            reads=[("gt", gb), ("ps", 2 * gb + 1)], writes=[("prod",)])
                    qc.finish()
                    nc.all_engine_barrier()
                with ExitStack() as s3:
                    sb3 = lambda n, s, d: s3.enter_context(nc.sbuf_tensor(n, s, d))
                    wdn = [sb3(f"g_wdn{u}_{i}", [128, 64 * 128], BF16) for i in range(3)]
                    tmpd = [sb3(f"g_tmpd{u}_{i}", [128, 512], F32) for i in range(2)]
                    qd = AutoQ(p)
                    for dc in range(NDC):
                        wb = dc % 3
                        db = dc % 2
                        qd.dma("gpsimd", wdn[wb][:].rearrange("q (fc w) -> q fc w", w=128),
                               p.wdnT.rearrange("(fc q) w -> q fc w", q=128)
                               [:, :, dc * 128:(dc + 1) * 128], ("wdn", wb), None)

                        def dgrp(t, wb=wb, db=db):
                            for fc in range(64):
                                ins = t.matmul(p.ps[2 + db][0:128, 0:512],
                                               wdn[wb][:, fc * 128:(fc + 1) * 128],
                                               prod[:, fc * 512:(fc + 1) * 512],
                                               start=(fc == 0), stop=(fc == 63))
                            return ins
                        qd.do("tensor", dgrp, reads=[("wdn", wb), ("prod",)],
                              writes=[("ps", 2 + db)])
                        qd.do("vector", lambda v, dc=dc, db=db: v.tensor_mul(
                            tmpd[db][:], xq[:, dc * 512:(dc + 1) * 512], bcs[:, 0:512]),
                            reads=[("xq",), ("bcs",)], writes=[("tmpd", db)])
                        qd.do("vector", lambda v, db=db: v.tensor_add(
                            tmpd[db][:], tmpd[db][:], bcs[:, 512:1024]),
                            reads=[("tmpd", db), ("bcs",)], writes=[("tmpd", db)])
                        qd.do("vector", lambda v, db=db: v.tensor_add(
                            tmpd[db][:], tmpd[db][:], p.ps[2 + db][0:128, 0:512]),
                            reads=[("tmpd", db), ("ps", 2 + db)], writes=[("tmpd", db)])
                        qd.do("scalar", lambda sc, dc=dc, db=db: sc.activation(
                            out=gsb[:, dc * 512:(dc + 1) * 512], in_=tmpd[db][:], func=AF.Silu),
                            reads=[("tmpd", db)], writes=[("gsb",)])
                    qd.finish()
                    nc.all_engine_barrier()
                with ExitStack() as s4:
                    sb4 = lambda n, s, d: s4.enter_context(nc.sbuf_tensor(n, s, d))
                    ogq = sb4(f"g_ogq{u}", [128, NDC * 512], BF16)
                    wo = [sb4(f"g_wo{u}_{i}", [128, NDC * 128], BF16) for i in range(3)]
                    x1t = [sb4(f"g_x1t{u}_{i}", [128, 512], F32) for i in range(2)]
                    tmf = [sb4(f"g_tmf{u}_{i}", [128, 512], F32) for i in range(2)]
                    qf = AutoQ(p)
                    qf.dma("gpsimd", ogq[:].rearrange("q (dc w) -> q dc w", w=512),
                           dyn_off(p.ag_og_out.rearrange("(dc q) w -> q dc w", q=128)
                                   [:, :, 0:512], p.qcol[u]), ("ogq",), None)
                    for dc in range(NDC):
                        wb = dc % 3
                        db = dc % 2
                        qf.dma("gpsimd", wo[wb][:].rearrange("q (d2 w) -> q d2 w", w=128),
                               p.woT.rearrange("(d2 q) w -> q d2 w", q=128)
                               [:, :, dc * 128:(dc + 1) * 128], ("wo", wb), None)

                        def ygrp(t, wb=wb, db=db):
                            for d2 in range(NDC):
                                ins = t.matmul(p.ps[5 * db][0:128, 0:512],
                                               wo[wb][:, d2 * 128:(d2 + 1) * 128],
                                               ogq[:, d2 * 512:(d2 + 1) * 512],
                                               start=(d2 == 0), stop=(d2 == NDC - 1))
                            return ins
                        qf.do("tensor", ygrp, reads=[("wo", wb), ("ogq",)],
                              writes=[("ps", 5 * db)])
                        qf.do("vector", lambda v, dc=dc, db=db: v.tensor_mul(
                            tmf[db][:], p.ps[5 * db][0:128, 0:512],
                            gsb[:, dc * 512:(dc + 1) * 512]),
                            reads=[("ps", 5 * db), ("gsb",)], writes=[("tmf", db)])
                        qf.do("vector", lambda v, dc=dc, db=db: v.tensor_add(
                            x1t[db][:], tmf[db][:], xq[:, dc * 512:(dc + 1) * 512]),
                            reads=[("tmf", db), ("xq",)], writes=[("x1t", db)])
                        qf.dma("gpsimd",
                               p.sc_x1[dc * 128:(dc + 1) * 128, u * 512:(u + 1) * 512],
                               x1t[db][:], ("x1sp", db), [("x1t", db)])
                    qf.finish()
                    nc.all_engine_barrier()


def ph6(p):
    nc = p.nc
    p.xn1sb = p.ctx.enter_context(nc.sbuf_tensor("xn1sb", [128, NDC * TQ], BF16))
    p.halocol = p.ctx.enter_context(nc.sbuf_tensor("halocol", [128, NDC], BF16))
    with ExitStack() as ctx:
        sbt = lambda n, s, d: ctx.enter_context(nc.sbuf_tensor(n, s, d))
        xf = [sbt(f"h_xf{i}", [128, 512], F32) for i in range(2)]
        xb = [sbt(f"h_xb{i}", [128, 512], BF16) for i in range(2)]
        xsq = sbt("h_xsq", [128, 512], BF16)
        sx = sbt("h_sx", [33, TQ], F32)
        st = sbt("h_st", [128, 16 * 14], F32)
        srow = sbt("h_srow", [1, 2 * 512], BF16)
        bc = sbt("h_bc", [128, 2 * 512], BF16)
        tmp = sbt("h_tmp", [128, 512], BF16)
        halo = sbt("h_halo", [1, D], BF16)
        q = AutoQ(p)
        NIT = 4 * NDC
        for k in range(NIT):
            tt, dc = divmod(k, NDC)
            q.dma("sync", xf[k % 2][:],
                  p.sc_x1[dc * 128:(dc + 1) * 128, tt * 512:(tt + 1) * 512],
                  ("xf", k % 2), None)
            q.do("vector", lambda v, k=k: v.tensor_copy(xb[k % 2][:], xf[k % 2][:]),
                 reads=[("xf", k % 2)], writes=[("xb", k % 2)])
            q.do("vector", lambda v, k=k: v.tensor_mul(xsq[:], xb[k % 2][:], xb[k % 2][:]),
                 reads=[("xb", k % 2)], writes=[("xsq",)])

            def stats2(t, k=k, tt=tt, dc=dc):
                t.matmul(p.ps[2 + tt % 2][0:1, 0:512], p.ones[:, 0:1], xb[k % 2][:],
                         start=(dc == 0), stop=(dc == NDC - 1))
                return t.matmul(p.ps[4 + tt % 2][0:1, 0:512],
                                p.ones[:, 0:1], xsq[:],
                                start=(dc == 0), stop=(dc == NDC - 1))
            q.do("tensor", stats2, reads=[("xb", k % 2), ("xsq",)],
                 writes=[("pssa", tt % 2), ("pssb", tt % 2)])
            if dc == NDC - 1:
                q.do("scalar", lambda sc, tt=tt: sc.copy(
                    out=sx[0:1, tt * 512:(tt + 1) * 512],
                    in_=p.ps[2 + tt % 2][0:1, 0:512]),
                    reads=[("pssa", tt % 2)], writes=[("sx0",)])
                q.do("scalar", lambda sc, tt=tt: sc.copy(
                    out=sx[32:33, tt * 512:(tt + 1) * 512],
                    in_=p.ps[4 + tt % 2][0:1, 0:512]),
                    reads=[("pssb", tt % 2)], writes=[("sx1",)])
        q.dma("gpsimd", p.sc_st[2:3, 0:TQ], sx[0:1, :], ("st2",), [("sx0",)])
        q.dma("gpsimd", p.sc_st[3:4, 0:TQ], sx[32:33, :], ("st3",), [("sx1",)])
        c = lambda i: st[:, 16 * i:16 * (i + 1)]
        q.dma("sync", c(0), p.sc_st[2:3, 0:TQ].rearrange("o (q w) -> o q w", q=128)[0],
              ("c0",), [("st2",)])
        q.dma("sync", c(1), p.sc_st[3:4, 0:TQ].rearrange("o (q w) -> o q w", q=128)[0],
              ("c1",), [("st3",)])
        q.do("vector", lambda v: v.tensor_scalar_mul(c(2), c(0), 1.0 / D),
             reads=[("c0",)], writes=[("mu",)])
        q.do("vector", lambda v: v.tensor_scalar_mul(c(3), c(1), 1.0 / D),
             reads=[("c1",)], writes=[("m2",)])
        q.do("vector", lambda v: v.tensor_mul(c(5), c(2), c(2)),
             reads=[("mu",)], writes=[("t5",)])
        q.do("vector", lambda v: v.tensor_sub(c(4), c(3), c(5)),
             reads=[("m2",), ("t5",)], writes=[("var",)])
        q.do("vector", lambda v: v.tensor_scalar_max(c(4), c(4), 0.0),
             reads=[("var",)], writes=[("var",)])
        q.do("scalar", lambda sc: sc.activation(out=c(5), in_=c(4), func=AF.Sqrt,
                                                bias=p.cst(1e-5)),
             reads=[("var",)], writes=[("t5",)])
        q.do("vector", lambda v: v.reciprocal(c(6), c(5)),
             reads=[("t5",)], writes=[("a1",)])
        q.do("vector", lambda v: v.tensor_mul(c(7), c(2), c(6)),
             reads=[("mu",), ("a1",)], writes=[("b1",)])
        q.do("vector", lambda v: v.tensor_scalar_mul(c(7), c(7), -1.0),
             reads=[("b1",)], writes=[("b1",)])
        q.dma("gpsimd", p.sc_st[6:7, 0:TQ].rearrange("o (q w) -> o q w", q=128)[0], c(6),
              ("st6",), [("a1",)])
        q.dma("gpsimd", p.sc_st[7:8, 0:TQ].rearrange("o (q w) -> o q w", q=128)[0], c(7),
              ("st7",), [("b1",)])
        q.dma("gpsimd", p.sc_stb1[0:1, :], p.sc_st[6:7, 0:TQ], ("stb10",), [("st6",)])
        q.dma("gpsimd", p.sc_stb1[1:2, :], p.sc_st[7:8, 0:TQ], ("stb11",), [("st7",)])
        for k in range(NIT):
            tt, dc = divmod(k, NDC)
            if dc == 0:
                q.dma("gpsimd", srow[:], p.sc_stb1[:, tt * 512:(tt + 1) * 512],
                      ("srow",), [("stb10",), ("stb11",)])
                for j in range(2):
                    q.do("tensor", lambda t, j=j: t.matmul(
                        p.ps[j][0:128, 0:512], p.ones[0:1, 0:128],
                        srow[0:1, j * 512:(j + 1) * 512], start=True, stop=True),
                        reads=[("srow",)], writes=[("ps", j)])
                    q.do("vector", lambda v, j=j: v.tensor_copy(
                        bc[:, j * 512:(j + 1) * 512], p.ps[j][0:128, 0:512]),
                        reads=[("ps", j)], writes=[("bc",)])
            q.dma("sync", xf[k % 2][:],
                  p.sc_x1[dc * 128:(dc + 1) * 128, tt * 512:(tt + 1) * 512],
                  ("xf", k % 2), None)
            q.do("vector", lambda v, k=k: v.tensor_copy(xb[k % 2][:], xf[k % 2][:]),
                 reads=[("xf", k % 2)], writes=[("xb", k % 2)])
            q.do("vector", lambda v, k=k: v.tensor_mul(tmp[:], xb[k % 2][:], bc[:, 0:512]),
                 reads=[("xb", k % 2), ("bc",)], writes=[("tmpx",)])
            q.do("vector", lambda v, tt=tt, dc=dc: v.tensor_add(
                p.xn1sb[:, dc * TQ + tt * 512:dc * TQ + (tt + 1) * 512], tmp[:],
                bc[:, 512:1024]),
                reads=[("tmpx",), ("bc",)], writes=[("xn1",)])
        for dc in range(NDC):
            q.dma("gpsimd", p.ag_h_in[0:1, dc * 128:(dc + 1) * 128],
                  p.xn1sb[:, dc * TQ + TQ - 1:dc * TQ + TQ], ("haloin",), [("xn1",)])
        q.do("gpsimd", lambda g: g.collective_compute(
            "AllGather", ALU.bypass, replica_groups=[[0, 1, 2, 3], [4, 5, 6, 7]],
            ins=[p.ag_h_in], outs=[p.ag_h_out]),
            reads=[("haloin",)], writes=[("haloout",)])
        for j in range(4):
            q.dma("gpsimd", p.ag_h2[0:1, j * 512:(j + 1) * 512], p.c_zero[0:1, 0:512],
                  ("h2z", j), None)
        q.dma("gpsimd", p.ag_h2[1:5, :], p.ag_h_out, ("h2c",), [("haloout",)])
        q.dma("gpsimd", halo[:], dyn_off(p.ag_h2[0:1, :], p.qrow), ("halosb",),
              [("h2c",)] + [("h2z", j) for j in range(4)])
        for dc in range(NDC):
            q.dma("gpsimd", p.halocol[:, dc:dc + 1], halo[0:1, dc * 128:(dc + 1) * 128],
                  ("halocol",), [("halosb",)])
        q.finish()


def ph7(p):
    pass


def ph7_record(p, q, ctx, part):
    nc = p.nc
    if True:
        sbt = lambda n, s, d: ctx.enter_context(nc.sbuf_tensor(n, s, d))
        wt = [sbt(f"i_wt_{part}_{i}", [128, 512], F32) for i in range(2)]
        ab = sbt(f"i_ab_{part}", [128, 512], F32)
        acc = sbt(f"i_acc_{part}", [128, 176 * 2], F32)
        accr = sbt(f"i_accr_{part}", [128, 1], F32)
        tot = sbt(f"i_tot_{part}", [1, 4], F32)
        thr = sbt(f"i_thr_{part}", [128, 1], F32)
        msk = sbt(f"i_msk_{part}", [128, 512], BF16)
        sgn = sbt(f"i_sgn_{part}", [128, 512], BF16)
        qt = sbt(f"i_qt_{part}", [128, 512], BF16)
        onesfc = sbt(f"i_onesf_{part}", [128, 1], F32)
        q.do("vector", lambda v: v.memset(onesfc[:], 1.0), writes=[("onesfc",)])
        for wi, (wsrc, wdst, rows, cols) in [(part, [
                (p.w1T, p.sc_w1q, D, FB), (p.w2T, p.sc_w2q, D, FB),
                (p.w3T, p.sc_w3q, FB, D)][part])]:
            nrc = rows // 128
            ncj = cols // 512
            nt = nrc * ncj
            for ti in range(nt):
                rc, cj = divmod(ti, ncj)
                q.dma("sync", wt[ti % 2][:],
                      wsrc[rc * 128:(rc + 1) * 128, cj * 512:(cj + 1) * 512],
                      ("wt", ti % 2), None)
                q.do("scalar", lambda sc, ti=ti: sc.activation(
                    out=ab[:], in_=wt[ti % 2][:], func=AF.Abs,
                    accum_out=acc[:, ti:ti + 1]),
                    reads=[("wt", ti % 2)], writes=[("ab",), ("acc",)])
            q.do("vector", lambda v, nt=nt: v.reduce_sum(
                accr[:], acc[:, 0:nt], axis=mybir.AxisListType.X),
                reads=[("acc",)], writes=[("accr",)])
            q.do("tensor", lambda t: t.matmul(p.ps[4][0:1, 0:1], onesfc[:], accr[:],
                                              start=True, stop=True),
                 reads=[("accr",), ("onesfc",)], writes=[("ps", 4)])
            q.do("vector", lambda v, wi=wi, rows=rows, cols=cols: v.tensor_scalar(
                tot[0:1, wi:wi + 1], p.ps[4][0:1, 0:1], 0.5 / (rows * cols), 5e-9,
                ALU.mult, ALU.add),
                reads=[("ps", 4)], writes=[("tot",)])
            q.do("tensor", lambda t, wi=wi: t.matmul(
                p.ps[5][0:128, 0:1], p.onesf32[:], tot[0:1, wi:wi + 1],
                start=True, stop=True),
                reads=[("tot",)], writes=[("ps", 5)])
            q.do("vector", lambda v: v.tensor_copy(thr[:], p.ps[5][0:128, 0:1]),
                 reads=[("ps", 5)], writes=[("thr",)])
            for ti in range(nt):
                rc, cj = divmod(ti, ncj)
                q.dma("sync", wt[ti % 2][:],
                      wsrc[rc * 128:(rc + 1) * 128, cj * 512:(cj + 1) * 512],
                      ("wt", ti % 2), None)
                q.do("scalar", lambda sc, ti=ti: sc.activation(
                    out=ab[:], in_=wt[ti % 2][:], func=AF.Abs),
                    reads=[("wt", ti % 2)], writes=[("ab",)])
                q.do("vector", lambda v: v.tensor_scalar(
                    msk[:], ab[:], thr[:], None, ALU.is_ge),
                    reads=[("ab",), ("thr",)], writes=[("msk",)])
                q.do("scalar", lambda sc, ti=ti: sc.activation(
                    out=sgn[:], in_=wt[ti % 2][:], func=AF.Sign),
                    reads=[("wt", ti % 2)], writes=[("sgn",)])
                q.do("vector", lambda v: v.tensor_mul(qt[:], msk[:], sgn[:]),
                     reads=[("msk",), ("sgn",)], writes=[("qt",)])
                q.dma("gpsimd", wdst[rc * 128:(rc + 1) * 128, cj * 512:(cj + 1) * 512],
                      qt[:], ("qsp", wi), [("qt",)])


def ph8(p):
    nc = p.nc
    with ExitStack() as ctx:
        sbt = lambda n, s, d: ctx.enter_context(nc.sbuf_tensor(n, s, d))
        sh = sbt("j_sh", [128, TQ], BF16)
        t1 = sbt("j_t1", [128, TQ], BF16)
        xkb = sbt("j_xkb", [128, TQ], BF16)
        q = AutoQ(p)
        for dc in range(NDC):
            q.do("vector", lambda v, dc=dc: v.tensor_copy(
                sh[:, 1:TQ], p.xn1sb[:, dc * TQ:dc * TQ + TQ - 1]),
                reads=[("xn1g",)], writes=[("sh",)])
            q.do("vector", lambda v, dc=dc: v.tensor_copy(
                sh[:, 0:1], p.halocol[:, dc:dc + 1]),
                reads=[("halocol",)], writes=[("sh",)])
            for (mv, mv1, dst, db) in [(p.v_mk, p.v_mk1, p.sc_xk, "xk"),
                                       (p.v_mv, p.v_mv1, p.sc_xv, "xv"),
                                       (p.v_mr, p.v_mr1, p.sc_xrm, "xr")]:
                q.do("vector", lambda v, mv1=mv1, dc=dc: v.tensor_scalar(
                    t1[:], sh[:], mv1[:, dc:dc + 1], None, ALU.mult),
                    reads=[("sh",)], writes=[("t1",)])
                q.do("vector", lambda v, mv=mv, dc=dc: v.scalar_tensor_tensor(
                    xkb[:], p.xn1sb[:, dc * TQ:(dc + 1) * TQ], mv[:, dc:dc + 1], t1[:],
                    ALU.mult, ALU.add),
                    reads=[("t1",), ("xn1g",)], writes=[("xkb",)])
                q.dma("gpsimd", dst[dc * 128:(dc + 1) * 128, :], xkb[:],
                      ("mx", db), [("xkb",)])
        q.finish()
    nc.all_engine_barrier()
    with ExitStack() as ctx:
        sbt = lambda n, s, d: ctx.enter_context(nc.sbuf_tensor(n, s, d))
        wk = [sbt(f"j_wk{i}", [128, NDC * 128], BF16) for i in range(2)]
        xsrc = sbt("j_xsrc", [128, NDC * 512], BF16)
        kv = sbt("j_kv", [128, 512], BF16)
        q = AutoQ(p)
        for (wT, xsc, dst, sig, tag) in [(p.wk1T, p.sc_xk, p.sc_k1, False, "k"),
                                         (p.wv1T, p.sc_xv, p.sc_v1, False, "v"),
                                         (p.wr1T, p.sc_xrm, p.sc_sr, True, "r")]:
            for tt in range(4):
                q.dma("gpsimd", xsrc[:].rearrange("q (dc w) -> q dc w", w=512),
                      xsc.rearrange("(dc q) w -> q dc w", q=128)[:, :, tt * 512:(tt + 1) * 512],
                      ("xsrc", tag), None)
                for oc in range(NDC):
                    wb = oc % 2
                    q.dma("gpsimd", wk[wb][:].rearrange("q (dc w) -> q dc w", w=128),
                          wT.rearrange("(dc q) w -> q dc w", q=128)
                          [:, :, oc * 128:(oc + 1) * 128], ("wk1", tag, wb), None)

                    def grp(t, wb=wb):
                        for dc in range(NDC):
                            ins = t.matmul(p.ps[wb][0:128, 0:512],
                                           wk[wb][:, dc * 128:(dc + 1) * 128],
                                           xsrc[:, dc * 512:(dc + 1) * 512],
                                           start=(dc == 0), stop=(dc == NDC - 1))
                        return ins
                    q.do("tensor", grp, reads=[("wk1", tag, wb), ("xsrc", tag)],
                         writes=[("ps", wb)])
                    if sig:
                        q.do("scalar", lambda sc, wb=wb: sc.activation(
                            out=kv[:], in_=p.ps[wb][0:128, 0:512], func=AF.Sigmoid),
                            reads=[("ps", wb)], writes=[("kv",)])
                    else:
                        q.do("vector", lambda v, wb=wb: v.tensor_copy(
                            kv[:], p.ps[wb][0:128, 0:512]),
                            reads=[("ps", wb)], writes=[("kv",)])
                    q.dma("gpsimd", dst[oc * 128:(oc + 1) * 128, tt * 512:(tt + 1) * 512],
                          kv[:], ("kvsp", tag), [("kv",)])
        q.finish()
    nc.all_engine_barrier()
    with ExitStack() as ctx:
        sbt = lambda n, s, d: ctx.enter_context(nc.sbuf_tensor(n, s, d))
        kb = [sbt(f"j_kb{i}", [128, TQ], BF16) for i in range(2)]
        vb = [sbt(f"j_vb{i}", [128, TQ], BF16) for i in range(2)]
        pk = sbt("j_pk", [128, TQ], F32)
        pkv = sbt("j_pkv", [128, TQ], F32)
        Nl = sbt("j_Nl", [128, TQ], F32)
        Dl = sbt("j_Dl", [128, TQ], F32)
        ewb = sbt("j_ewb", [128, TQ], F32)
        endsNl = sbt("j_eNl", [128, NDC], F32)
        endsDl = sbt("j_eDl", [128, NDC], F32)
        q = AutoQ(p)
        for dc in range(NDC):
            db = dc % 2
            if dc % 4 == 0:
                bitffn_record(p, q, ctx, dc // 4)
            q.dma("sync", kb[db][:], p.sc_k1[dc * 128:(dc + 1) * 128, :], ("kb", db), None)
            q.dma("sync", vb[db][:], p.sc_v1[dc * 128:(dc + 1) * 128, :], ("vb", db), None)
            q.do("scalar", lambda sc, db=db: sc.activation(out=pk[:], in_=kb[db][:], func=AF.Exp),
                 reads=[("kb", db)], writes=[("pk",)])
            q.do("vector", lambda v, db=db: v.tensor_mul(pkv[:], pk[:], vb[db][:]),
                 reads=[("pk",), ("vb", db)], writes=[("pkv",)])
            q.do("vector", lambda v: v.memset(ewb[:], 0.0),
                 reads=[("ewb",)], writes=[("ewb",)])
            q.do("scalar", lambda sc, dc=dc: sc.activation(
                out=ewb[:], in_=ewb[:], func=AF.Exp, bias=p.v_w[:, dc:dc + 1]),
                reads=[("ewb",)], writes=[("ewb",)])
            q.do("vector", lambda v: v.tensor_tensor_scan(
                Nl[:], ewb[:], pkv[:], 0.0, ALU.mult, ALU.add),
                reads=[("ewb",), ("pkv",)], writes=[("Nl",)])
            q.do("vector", lambda v: v.tensor_tensor_scan(
                Dl[:], ewb[:], pk[:], 0.0, ALU.mult, ALU.add),
                reads=[("ewb",), ("pk",)], writes=[("Dl",)])
            q.dma("gpsimd", p.sc_Nl[dc * 128:(dc + 1) * 128, :], Nl[:], ("Nlsp",), [("Nl",)])
            q.dma("gpsimd", p.sc_Dl[dc * 128:(dc + 1) * 128, :], Dl[:], ("Dlsp",), [("Dl",)])
            q.do("vector", lambda v, dc=dc: v.tensor_copy(endsNl[:, dc:dc + 1],
                                                          Nl[:, TQ - 1:TQ]),
                 reads=[("Nl",)], writes=[("eNl",)])
            q.do("vector", lambda v, dc=dc: v.tensor_copy(endsDl[:, dc:dc + 1],
                                                          Dl[:, TQ - 1:TQ]),
                 reads=[("Dl",)], writes=[("eDl",)])
        q.dma("gpsimd", p.ag_e_in[0:1, :], endsNl[:], ("ein0",), [("eNl",)])
        q.dma("gpsimd", p.ag_e_in[1:2, :], endsDl[:], ("ein1",), [("eDl",)])
        q.do("gpsimd", lambda g: g.collective_compute(
            "AllGather", ALU.bypass, replica_groups=[[0, 1, 2, 3], [4, 5, 6, 7]],
            ins=[p.ag_e_in], outs=[p.ag_e_out]),
            reads=[("ein0",), ("ein1",)], writes=[("eout",)])
        q.finish()
    nc.all_engine_barrier()
    with ExitStack() as ctx:
        sbt = lambda n, s, d: ctx.enter_context(nc.sbuf_tensor(n, s, d))
        kb = [sbt(f"j2_kb{i}", [128, TQ], BF16) for i in range(2)]
        vb = [sbt(f"j2_vb{i}", [128, TQ], BF16) for i in range(2)]
        srb = [sbt(f"j2_srb{i}", [128, TQ], BF16) for i in range(2)]
        pku = sbt("j2_pku", [128, TQ], F32)
        Nl = [sbt(f"j2_Nl{i}", [128, TQ], F32) for i in range(2)]
        Dl = [sbt(f"j2_Dl{i}", [128, TQ], F32) for i in range(2)]
        num = sbt("j2_num", [128, TQ], F32)
        den = sbt("j2_den", [128, TQ], F32)
        Em = sbt("j2_Em", [128, TQ], F32)
        corr = sbt("j2_corr", [128, TQ], F32)
        rwv = sbt("j2_rwv", [128, TQ], BF16)
        iot = sbt("j2_iot", [128, TQ], F32)
        endsN = sbt("j2_eN", [128, NDC], F32)
        endsD = sbt("j2_eD", [128, NDC], F32)
        wp = sbt("j2_wp", [128, NDC], F32)
        mcol = sbt("j2_mcol", [128, 1], F32)
        enN = sbt("j2_enN", [128, NDC], F32)
        enD = sbt("j2_enD", [128, NDC], F32)
        msc = sbt("j2_msc", [1, 4], F32)
        msci = sbt("j2_msci", [1, 1], mybir.dt.int32)
        q = AutoQ(p)
        q.dma("sync", iot[:], p.c_iota, ("iot",), None)
        q.do("vector", lambda v: v.memset(enN[:], 0.0), writes=[("enN",)])
        q.do("vector", lambda v: v.memset(enD[:], 0.0), writes=[("enD",)])
        q.do("vector", lambda v: v.memset(msc[:], 0.0), writes=[("msc",)])
        q.do("gpsimd", lambda g: g.store(msci[0:1, 0:1], p.qv),
             reads=[], writes=[("msci",)])
        q.do("vector", lambda v: v.tensor_copy(msc[0:1, 0:1], msci[0:1, 0:1]),
             reads=[("msci",), ("msc",)], writes=[("mscq",)])
        for p_ in range(4):
            q.do("vector", lambda v, p_=p_: v.tensor_scalar(
                msc[0:1, 1:2], msc[0:1, 0:1], float(-1 - p_), None, ALU.add),
                reads=[("mscq",)], writes=[("mscm", p_)])
            q.do("tensor", lambda t: t.matmul(p.ps[3][0:128, 0:1], p.onesf32[:],
                                              msc[0:1, 1:2], start=True, stop=True),
                 reads=[("mscm", p_)], writes=[("ps", 3)])
            q.do("vector", lambda v: v.tensor_copy(mcol[:], p.ps[3][0:128, 0:1]),
                 reads=[("ps", 3)], writes=[("mcol",)])
            q.do("vector", lambda v: v.tensor_scalar(
                wp[:], p.v_wtq[:], mcol[:], None, ALU.mult),
                reads=[("mcol",)], writes=[("wp",)])
            q.do("vector", lambda v: v.tensor_scalar_min(wp[:], wp[:], 0.0),
                 reads=[("wp",)], writes=[("wp",)])
            q.do("scalar", lambda sc: sc.activation(out=wp[:], in_=wp[:], func=AF.Exp),
                 reads=[("wp",)], writes=[("wp",)])
            q.do("vector", lambda v: v.tensor_scalar(
                mcol[:], mcol[:], 0.0, None, ALU.is_ge),
                reads=[("mcol",)], writes=[("mcol",)])
            q.do("vector", lambda v: v.tensor_scalar(wp[:], wp[:], mcol[:], None, ALU.mult),
                 reads=[("wp",), ("mcol",)], writes=[("wp",)])
            q.dma("sync", endsN[:], p.ag_e_out.rearrange("o (q dc) -> o q dc", q=128)[2 * p_],
                  ("endsN",), None)
            q.dma("sync", endsD[:],
                  p.ag_e_out.rearrange("o (q dc) -> o q dc", q=128)[2 * p_ + 1],
                  ("endsD",), None)
            q.do("vector", lambda v: v.scalar_tensor_tensor(
                enN[:], endsN[:], 1.0, enN[:], ALU.mult, ALU.add)
                if False else v.tensor_mul(endsN[:], endsN[:], wp[:]),
                reads=[("endsN",), ("wp",)], writes=[("endsN",)])
            q.do("vector", lambda v: v.tensor_add(enN[:], enN[:], endsN[:]),
                 reads=[("endsN",), ("enN",)], writes=[("enN",)])
            q.do("vector", lambda v: v.tensor_mul(endsD[:], endsD[:], wp[:]),
                 reads=[("endsD",), ("wp",)], writes=[("endsD",)])
            q.do("vector", lambda v: v.tensor_add(enD[:], enD[:], endsD[:]),
                 reads=[("endsD",), ("enD",)], writes=[("enD",)])
        for dc in range(NDC):
            db = dc % 2
            q.dma("sync", kb[db][:], p.sc_k1[dc * 128:(dc + 1) * 128, :], ("kb", db), None)
            q.dma("sync", vb[db][:], p.sc_v1[dc * 128:(dc + 1) * 128, :], ("vb", db), None)
            q.dma("sync", srb[db][:], p.sc_sr[dc * 128:(dc + 1) * 128, :], ("srb", db), None)
            q.dma("sync", Nl[db][:], p.sc_Nl[dc * 128:(dc + 1) * 128, :], ("Nlr", db), None)
            q.dma("sync", Dl[db][:], p.sc_Dl[dc * 128:(dc + 1) * 128, :], ("Dlr", db), None)
            q.do("scalar", lambda sc, dc=dc, db=db: sc.activation(
                out=pku[:], in_=kb[db][:], func=AF.Exp, bias=p.v_tf[:, dc:dc + 1]),
                reads=[("kb", db)], writes=[("pku",)])
            q.do("scalar", lambda sc, dc=dc: sc.activation(
                out=Em[:], in_=iot[:], func=AF.Exp, scale=p.v_w[:, dc:dc + 1]),
                reads=[("iot",)], writes=[("Em",)])
            q.do("vector", lambda v, db=db: v.tensor_mul(num[:], pku[:], vb[db][:]),
                 reads=[("pku",), ("vb", db)], writes=[("num",)])
            q.do("vector", lambda v, db=db: v.tensor_add(num[:, 1:TQ], num[:, 1:TQ],
                                                         Nl[db][:, 0:TQ - 1]),
                 reads=[("num",), ("Nlr", db)], writes=[("num",)])
            q.do("vector", lambda v, dc=dc: v.tensor_scalar(
                corr[:], Em[:], enN[:, dc:dc + 1], None, ALU.mult),
                reads=[("Em",), ("enN",)], writes=[("corr",)])
            q.do("vector", lambda v: v.tensor_add(num[:, 1:TQ], num[:, 1:TQ],
                                                  corr[:, 0:TQ - 1]),
                 reads=[("corr",), ("num",)], writes=[("num",)])
            q.do("vector", lambda v, dc=dc: v.tensor_scalar(
                num[:, 0:1], enN[:, dc:dc + 1], 1.0, None, ALU.mult)
                if False else v.tensor_add(num[:, 0:1], num[:, 0:1],
                                           enN[:, dc:dc + 1]),
                reads=[("enN",), ("num",)], writes=[("num",)])
            q.do("vector", lambda v: v.tensor_copy(den[:], pku[:]),
                 reads=[("pku",)], writes=[("den",)])
            q.do("vector", lambda v, db=db: v.tensor_add(den[:, 1:TQ], den[:, 1:TQ],
                                                          Dl[db][:, 0:TQ - 1]),
                 reads=[("den",), ("Dlr", db)], writes=[("den",)])
            q.do("vector", lambda v, dc=dc: v.tensor_scalar(
                corr[:], Em[:], enD[:, dc:dc + 1], None, ALU.mult),
                reads=[("Em",), ("enD",)], writes=[("corr",)])
            q.do("vector", lambda v: v.tensor_add(den[:, 1:TQ], den[:, 1:TQ],
                                                  corr[:, 0:TQ - 1]),
                 reads=[("corr",), ("den",)], writes=[("den",)])
            q.do("vector", lambda v, dc=dc: v.tensor_add(den[:, 0:1], den[:, 0:1],
                                                         enD[:, dc:dc + 1]),
                 reads=[("enD",), ("den",)], writes=[("den",)])
            q.do("vector", lambda v: v.reciprocal(den[:], den[:]),
                 reads=[("den",)], writes=[("den",)])
            q.do("vector", lambda v: v.tensor_mul(num[:], num[:], den[:]),
                 reads=[("num",), ("den",)], writes=[("num",)])
            q.do("vector", lambda v, db=db: v.tensor_mul(rwv[:], num[:], srb[db][:]),
                 reads=[("num",), ("srb", db)], writes=[("rwv",)])
            q.dma("gpsimd", p.sc_rw[dc * 128:(dc + 1) * 128, :], rwv[:], ("rwsp",), [("rwv",)])
        q.finish()


def bitffn_record(p, q, ctx, tt):
    nc = p.nc
    sbt = lambda n, s, d: ctx.enter_context(nc.sbuf_tensor(n, s, d))
    if not hasattr(p, "_bfb"):
        p._bfb = {
            "hid": sbt("bf_hid", [128, 44 * 512], BF16),
            "w1b": [sbt(f"bf_w1b{i}", [128, NDC * 128], BF16) for i in range(2)],
            "w2b": [sbt(f"bf_w2b{i}", [128, NDC * 128], BF16) for i in range(2)],
            "w3b": [sbt("bf_w3b0", [128, 44 * 128], BF16)],
            "sg": sbt("bf_sg", [128, 512], BF16),
            "gq": sbt("bf_gq", [128, 512], BF16),
        }
    hid = p._bfb["hid"]; w1b = p._bfb["w1b"]; w2b = p._bfb["w2b"]
    w3b = p._bfb["w3b"]; sg = p._bfb["sg"]; gq = p._bfb["gq"]
    for fc in range(44):
        wb = fc % 2
        q.dma("sync", w1b[wb][:].rearrange("q (dc w) -> q dc w", w=128),
              p.sc_w1q.rearrange("(dc q) w -> q dc w", q=128)
              [:, :, fc * 128:(fc + 1) * 128], ("kw1", wb), None)
        q.dma("sync", w2b[wb][:].rearrange("q (dc w) -> q dc w", w=128),
              p.sc_w2q.rearrange("(dc q) w -> q dc w", q=128)
              [:, :, fc * 128:(fc + 1) * 128], ("kw2", wb), None)

        def h1grp(t, wb=wb, tt=tt, fb2=fc % 2):
            for dc in range(NDC):
                ins = t.matmul(p.ps[2 * fb2][0:128, 0:512],
                               w1b[wb][:, dc * 128:(dc + 1) * 128],
                               p.xn1sb[:, dc * TQ + tt * 512:dc * TQ + (tt + 1) * 512],
                               start=(dc == 0), stop=(dc == NDC - 1))
            return ins
        q.do("tensor", h1grp, reads=[("kw1", wb), ("xn1g",)],
             writes=[("ps", 2 * (fc % 2))])

        def h2grp(t, wb=wb, tt=tt, fb2=fc % 2):
            for dc in range(NDC):
                ins = t.matmul(p.ps[2 * fb2 + 1][0:128, 0:512],
                               w2b[wb][:, dc * 128:(dc + 1) * 128],
                               p.xn1sb[:, dc * TQ + tt * 512:dc * TQ + (tt + 1) * 512],
                               start=(dc == 0), stop=(dc == NDC - 1))
            return ins
        q.do("tensor", h2grp, reads=[("kw2", wb), ("xn1g",)],
             writes=[("ps", 2 * (fc % 2) + 1)])
        q.do("scalar", lambda sc, fb2=fc % 2: sc.activation(
            out=sg[:], in_=p.ps[2 * fb2][0:128, 0:512], func=AF.Sigmoid),
            reads=[("ps", 2 * (fc % 2))], writes=[("sg",)])
        q.do("vector", lambda v, fc=fc, fb2=fc % 2: v.tensor_mul(
            hid[:, fc * 512:(fc + 1) * 512], sg[:], p.ps[2 * fb2 + 1][0:128, 0:512]),
            reads=[("sg",), ("ps", 2 * (fc % 2) + 1)], writes=[("hid",)])
    for oc in range(NDC):
        wb3 = 0
        q.dma("sync", w3b[wb3][:].rearrange("q (fc w) -> q fc w", w=128),
              p.sc_w3q.rearrange("(fc q) w -> q fc w", q=128)
              [:, :, oc * 128:(oc + 1) * 128], ("kw3", wb3), None)

        def w3grp(t, ob2=oc % 2, wb3=wb3):
            for fc in range(44):
                ins = t.matmul(p.ps[4 + ob2][0:128, 0:512],
                               w3b[wb3][:, fc * 128:(fc + 1) * 128],
                               hid[:, fc * 512:(fc + 1) * 512],
                               start=(fc == 0), stop=(fc == 43))
            return ins
        q.do("tensor", w3grp, reads=[("kw3", wb3), ("hid",)],
             writes=[("ps", 4 + oc % 2)])
        q.do("scalar", lambda sc, ob2=oc % 2: sc.activation(
            out=gq[:], in_=p.ps[4 + ob2][0:128, 0:512], func=AF.Silu),
            reads=[("ps", 4 + oc % 2)], writes=[("gq",)])
        q.dma("gpsimd", p.sc_gq[oc * 128:(oc + 1) * 128, tt * 512:(tt + 1) * 512],
              gq[:], ("gqsp",), [("gq",)])


def ph9(p):
    nc = p.nc
    with ExitStack() as ctx:
        sbt = lambda n, s, d: ctx.enter_context(nc.sbuf_tensor(n, s, d))
        rw = sbt("k_rw", [128, NDC * 512], BF16)
        wo = [sbt(f"k_wo{i}", [128, NDC * 128], BF16) for i in range(3)]
        y1 = sbt("k_y1", [128, NDC * 512], BF16)
        gq = [sbt(f"k_gq{i}", [128, 512], BF16) for i in range(2)]
        xt = [sbt(f"k_xt{i}", [128, 512], F32) for i in range(2)]
        fin = [sbt(f"k_fin{i}", [128, 512], F32) for i in range(2)]
        q = AutoQ(p)
        for tt in range(4):
            q.dma("sync", rw[:].rearrange("q (dc w) -> q dc w", w=512),
                  p.sc_rw.rearrange("(dc q) w -> q dc w", q=128)[:, :, tt * 512:(tt + 1) * 512],
                  ("rw",), None)
            for oc in range(NDC):
                wb = oc % 3
                q.dma("gpsimd", wo[wb][:].rearrange("q (dc w) -> q dc w", w=128),
                      p.wo1T.rearrange("(dc q) w -> q dc w", q=128)
                      [:, :, oc * 128:(oc + 1) * 128], ("kwo", wb), None)

                def ygrp(t, wb=wb):
                    for dc in range(NDC):
                        ins = t.matmul(p.ps[wb][0:128, 0:512],
                                       wo[wb][:, dc * 128:(dc + 1) * 128],
                                       rw[:, dc * 512:(dc + 1) * 512],
                                       start=(dc == 0), stop=(dc == NDC - 1))
                    return ins
                q.do("tensor", ygrp, reads=[("kwo", wb), ("rw",)], writes=[("ps", wb)])
                q.do("vector", lambda v, oc=oc, wb=wb: v.tensor_copy(
                    y1[:, oc * 512:(oc + 1) * 512], p.ps[wb][0:128, 0:512]),
                    reads=[("ps", wb)], writes=[("y1",)])
            for oc in range(NDC):
                ob = oc % 2
                q.dma("sync", gq[ob][:],
                      p.sc_gq[oc * 128:(oc + 1) * 128, tt * 512:(tt + 1) * 512],
                      ("kgq", ob), None)
                q.dma("sync", xt[ob][:],
                      p.sc_x1[oc * 128:(oc + 1) * 128, tt * 512:(tt + 1) * 512],
                      ("kxt", ob), None)
                q.do("vector", lambda v, oc=oc, ob=ob: v.tensor_mul(
                    fin[ob][:], y1[:, oc * 512:(oc + 1) * 512], gq[ob][:]),
                    reads=[("y1",), ("kgq", ob)], writes=[("fin", ob)])
                q.do("vector", lambda v, ob=ob: v.tensor_add(fin[ob][:], fin[ob][:], xt[ob][:]),
                     reads=[("fin", ob), ("kxt", ob)], writes=[("fin", ob)])
                q.dma("gpsimd", p.out[oc * 128:(oc + 1) * 128, tt * 512:(tt + 1) * 512],
                      fin[ob][:], ("outsp", ob), [("fin", ob)])
        q.finish()


def make_in_maps(I):
    def C(x):
        return np.ascontiguousarray(np.asarray(x, dtype=np.float32))

    in_maps = []
    for c in range(8):
        b, g = c // 4, c % 4
        qr = slice(512 * g, 512 * (g + 1))
        m = {}
        m["xT"] = C(I["L0_attn_x"][b].T)
        m["qi"] = np.array([[g]], np.uint32)
        m["xq4T"] = C(I["L0_attn_x"][b, g * TQ:(g + 1) * TQ, :].T)
        rows = np.r_[np.arange(512 * g, 512 * (g + 1)),
                     2048 + np.arange(512 * g, 512 * (g + 1)),
                     4096 + np.arange(512 * g, 512 * (g + 1))]
        m["wqkvT"] = C(I["L0_attn_W_qkv"][rows].T)
        m["convw"] = C(I["L0_attn_conv_w"][rows])
        absel = np.r_[np.arange(g * 4, g * 4 + 4), 16 + np.arange(g * 4, g * 4 + 4)]
        m["wabT"] = C(I["L0_attn_W_ab"][absel].T)
        m["bab"] = C(I["L0_attn_b_ab"][absel][:, None])
        m["wgT"] = C(I["L0_attn_W_g"][qr].T)
        m["woT"] = C(I["L0_attn_W_o"].T)
        m["ln1r"] = C(I["L0_ffn_ln1_w"].reshape(NDC, 128).T)
        m["ln2r"] = C(I["L0_ffn_ln2_w"].reshape(NDC, 128).T)
        m["wqfT"] = C(I["L0_ffn_Wq"][qr].T)
        m["wkfT"] = C(I["L0_ffn_Wk"][qr].T)
        m["wofs"] = C(I["L0_ffn_Wo"][:, qr])
        m["wvfs"] = C(I["L0_ffn_Wv"][qr, :])
        m["A_"] = C(I["L0_ffn_A"])
        m["B_"] = C(I["L0_ffn_B"])
        m["winT"] = C(I["L0_ffn_W_in_base"].T)
        m["wupT"] = C(I["L0_ffn_W_up_base"].T)
        m["wdnT"] = C(I["L0_ffn_W_down"].T)
        m["tdr"] = C(I["L1_attn_time_decay"].reshape(NDC, 128).T)
        m["tfr"] = C(I["L1_attn_time_first"].reshape(NDC, 128).T)
        m["mkr"] = C(np.asarray(I["L1_attn_time_mix_k"]).reshape(-1).reshape(NDC, 128).T)
        m["mvr"] = C(np.asarray(I["L1_attn_time_mix_v"]).reshape(-1).reshape(NDC, 128).T)
        m["mrr"] = C(np.asarray(I["L1_attn_time_mix_r"]).reshape(-1).reshape(NDC, 128).T)
        m["wk1T"] = C(I["L1_attn_Wk"].T)
        m["wv1T"] = C(I["L1_attn_Wv"].T)
        m["wr1T"] = C(I["L1_attn_Wr"].T)
        m["wo1T"] = C(I["L1_attn_Wo"].T)
        m["w1T"] = C(I["L1_ffn_w1"].T)
        m["w2T"] = C(I["L1_ffn_w2"].T)
        m["w3T"] = C(I["L1_ffn_w3"].T)
        in_maps.append(m)
    return in_maps


def kernel(**I):
    in_maps = make_in_maps(I)
    nc = build_program()
    res = run_bass_kernel_spmd(nc, in_maps, core_ids=list(range(8)))
    out = np.empty((2, T, D), np.float32)
    for c in range(8):
        b, g = c // 4, c % 4
        out[b, g * TQ:(g + 1) * TQ, :] = res.results[c]["out"].T
    return out


# version 10 (final)
# speedup vs baseline: 1.0004x; 1.0004x over previous
import math
import numpy as np
from contextlib import ExitStack

import concourse.bass as bass
import concourse.mybir as mybir
from concourse.bass_utils import run_bass_kernel_spmd
import ml_dtypes

F32 = mybir.dt.float32
BF16 = mybir.dt.bfloat16
AF = mybir.ActivationFunctionType
ALU = mybir.AluOpType
bf16 = ml_dtypes.bfloat16

D = 2048
T = 8192
TQ = 2048
HD = 128
NHL = 4
CD = 128
NCH = T // CD
FF = 8192
FB = 5632
NDC = D // 128
RT = 512
NTT = T // RT

DEBUG = False
LAST_MARKS = None
N_PHASES = 10


class Sem:
    def __init__(self, nc, ctx, name):
        self.h = ctx.enter_context(nc.semaphore(name))
        self.n = 0

    def inc(self, k=1):
        self.n += k
        return self.n


class P:
    pass


def chain(eng, sem, ins):
    ins.then_inc(sem.h, 1)
    sem.inc()
    eng.wait_ge(sem.h, sem.n)


class AutoQ:

    QUEUES = ("sync", "gpsimd", "squeue")

    def __init__(self, p):
        self.p = p
        self.steps = {e: [] for e in ("sync", "gpsimd", "scalar", "vector", "tensor")}
        self.esem = {"scalar": p.s_ac, "vector": p.s_ve, "tensor": p.s_mm,
                     "gpsimd": p.s_gp, "sync": p.s_ld}
        self.writer = {}
        self.readers = {}
        self.waited = {}
        if not hasattr(p, "_qmap"):
            p._qmap = {}
            p._qcount = {"sync": 0, "gpsimd": 0, "squeue": 0}
            p._qlastbuf = {}

    def _qsem(self, q, buf):
        key = (q, buf)
        if key not in self.p._qmap:
            pool = {"sync": self.p.sqs, "gpsimd": self.p.sqg, "squeue": self.p.sql}[q]
            self.p._qmap[key] = pool[self.p._qcount[q] % len(pool)]
            self.p._qcount[q] += 1
        return self.p._qmap[key]

    def _dep_waits(self, engine, reads, writes):
        waits = []

        def add(dep):
            if dep is None:
                return
            sem, tgt, oeng = dep
            k = (engine, id(sem))
            if self.waited.get(k, -1) >= tgt:
                return
            self.waited[k] = tgt
            waits.append((sem, tgt))

        for r in reads:
            add(self.writer.get(r))
        for w in writes:
            add(self.writer.get(w))
            for dep in self.readers.get(w, ()):
                add(dep)
        return waits

    def do(self, engine, emit, reads=(), writes=(), n_inc=1):
        sem = self.esem[engine]
        waits = self._dep_waits(engine, reads, writes)
        tgt = sem.inc(n_inc)
        dep = (sem, tgt, engine)
        for w in writes:
            self.writer[w] = dep
            self.readers[w] = []
        for r in reads:
            self.readers.setdefault(r, []).append(dep)

        def run(eng, waits=waits, emit=emit, sem=sem, n_inc=n_inc,
                warm=(engine == "tensor" and bool(waits))):
            if warm:
                eng.ldweights(self.p.ident[:])
            for ws, wt in waits:
                eng.wait_ge(ws.h, wt)
            ins = emit(eng)
            ins.then_inc(sem.h, n_inc)

        self.steps[engine].append(run)

    def dma(self, q, out_ap, in_ap, out_buf, in_buf, **kw):
        qeng = {"sync": "sync", "gpsimd": "gpsimd", "squeue": "scalar"}[q]
        sem = self._qsem(q, out_buf)
        in_bufs = ([] if in_buf is None else
                   (list(in_buf) if isinstance(in_buf, list) else [in_buf]))
        waits = self._dep_waits(qeng, in_bufs, [out_buf])
        if self.p._qlastbuf.get(id(sem)) not in (None, out_buf):
            k = (qeng, id(sem))
            if self.waited.get(k, -1) < sem.n:
                self.waited[k] = sem.n
                waits.append((sem, sem.n))
        self.p._qlastbuf[id(sem)] = out_buf
        tgt = sem.inc(16)
        dep = (sem, tgt, qeng)
        self.writer[out_buf] = dep
        self.readers[out_buf] = []
        for ib in in_bufs:
            self.readers.setdefault(ib, []).append(dep)

        def run(eng, waits=waits, sem=sem, out_ap=out_ap, in_ap=in_ap, kw=kw,
                _ob=out_buf, _ib=in_bufs):
            for ws, wt in waits:
                eng.wait_ge(ws.h, wt)
            try:
                eng.dma_start(out=out_ap, in_=in_ap, **kw).then_inc(sem.h, 16)
            except Exception:
                import traceback
                print("DMA FAIL out_buf=", _ob, "in=", _ib)
                print("OUT:", out_ap)
                print("IN:", in_ap)
                traceback.print_exc()
                raise

        self.steps[qeng].append(run)

    def raw(self, engine, emit):
        self.steps[engine].append(emit)

    def finish(self):
        nc = self.p.nc
        qsems = {id(s): s for (q, b), s in self.p._qmap.items()}
        esems = [self.esem[e] for e in ("scalar", "vector", "tensor", "gpsimd", "sync")]

        def drain(eng):
            for s in qsems.values():
                eng.wait_ge(s.h, s.n)
            for s in esems:
                eng.wait_ge(s.h, s.n)

        with nc.Block() as b:
            for ename, reg in (("sync", b.sync), ("gpsimd", b.gpsimd),
                               ("scalar", b.scalar), ("vector", b.vector),
                               ("tensor", b.tensor)):
                def body(eng, ename=ename):
                    for fn in self.steps[ename]:
                        fn(eng)
                    if ename == "gpsimd":
                        drain(eng)
                reg(body)


def build_program():
    nc = bass.Bass("TRN2", target_bir_lowering=False)
    p = P()
    p.nc = nc
    ctx = ExitStack()
    p.ctx = ctx

    def ein(name, shape, dt=F32):
        return nc.dram_tensor(name, shape, dt, kind="ExternalInput").ap()

    p.xT = ein("xT", [D, T])
    p.wqkvT = ein("wqkvT", [D, 1536])
    p.convw = ein("convw", [1536, 4])
    p.wabT = ein("wabT", [D, 8])
    p.bab = ein("bab", [8, 1])
    p.wgT = ein("wgT", [D, 512])
    p.woT = ein("woT", [D, D])
    p.ln1r = ein("ln1r", [128, NDC])
    p.ln2r = ein("ln2r", [128, NDC])
    p.wqfT = ein("wqfT", [D, 512])
    p.wkfT = ein("wkfT", [D, 512])
    p.wofs = ein("wofs", [D, 512])
    p.wvfs = ein("wvfs", [512, D])
    p.A_ = ein("A_", [D, 16])
    p.B_ = ein("B_", [16, FF])
    p.winT = ein("winT", [D, FF])
    p.wupT = ein("wupT", [D, FF])
    p.wdnT = ein("wdnT", [FF, D])
    p.tdr = ein("tdr", [128, NDC])
    p.tfr = ein("tfr", [128, NDC])
    p.mkr = ein("mkr", [128, NDC])
    p.mvr = ein("mvr", [128, NDC])
    p.mrr = ein("mrr", [128, NDC])
    p.wk1T = ein("wk1T", [D, D])
    p.wv1T = ein("wv1T", [D, D])
    p.wr1T = ein("wr1T", [D, D])
    p.wo1T = ein("wo1T", [D, D])
    p.w1T = ein("w1T", [D, FB])
    p.w2T = ein("w2T", [D, FB])
    p.w3T = ein("w3T", [FB, D])
    p.qi = nc.dram_tensor("qi", [1, 1], mybir.dt.uint32, kind="ExternalInput").ap()
    p.xq4T = ein("xq4T", [D, TQ])

    p.out = nc.dram_tensor("out", [D, TQ], F32, kind="ExternalOutput").ap()

    dbg = "ExternalOutput" if DEBUG else "Internal"

    def intl(name, shape, dt=BF16, debug_vis=False):
        kind = dbg if debug_vis else "Internal"
        return nc.dram_tensor(name, shape, dt, kind=kind).ap()

    p.sc_qd = intl("sc_qd", [512, T], debug_vis=True)
    p.sc_kd = intl("sc_kd", [512, T], debug_vis=True)
    p.sc_vd = intl("sc_vd", [512, T], debug_vis=True)
    p.sc_gg = intl("sc_gg", [512, T], debug_vis=True)
    p.sc_qf = intl("sc_qf", [512, T], debug_vis=True)
    p.sc_kf = intl("sc_kf", [512, T], debug_vis=True)
    p.sc_vt = intl("sc_vt", [T, 4 * 33], debug_vis=True)
    p.sc_xr = intl("sc_xr", [D, T])
    p.sc_xn = intl("sc_xn", [D, T], debug_vis=True)
    p.sc_ab = intl("sc_ab", [16, T], F32, debug_vis=True)
    p.sc_bnd = intl("sc_bnd", [1, 8], F32, debug_vis=True)
    p.ag_og_in = intl("ag_og_in", [512, T])
    p.ag_og_out = intl("ag_og_out", [D, T], debug_vis=True)
    p.ar_m_in = intl("ar_m_in", [16, T], F32)
    p.ar_m_out = intl("ar_m_out", [16, T], F32, debug_vis=True)
    p.sc_x1 = intl("sc_x1", [D, TQ], F32, debug_vis=True)
    p.ag_h_in = intl("ag_h_in", [1, D])
    p.ag_h_out = intl("ag_h_out", [4, D])
    p.ag_e_in = intl("ag_e_in", [2, D], F32)
    p.ag_e_out = intl("ag_e_out", [8, D], F32)
    p.sc_k1 = intl("sc_k1", [D, TQ], debug_vis=True)
    p.sc_v1 = intl("sc_v1", [D, TQ])
    p.sc_r1 = intl("sc_r1", [D, TQ])
    p.sc_Nl = intl("sc_Nl", [D, TQ], F32)
    p.sc_Dl = intl("sc_Dl", [D, TQ], F32)
    p.sc_xk = intl("sc_xk", [D, TQ])
    p.sc_xv = intl("sc_xv", [D, TQ])
    p.sc_xrm = intl("sc_xrm", [D, TQ])
    p.sc_sr = intl("sc_sr", [D, TQ])
    p.sc_rw = intl("sc_rw", [D, TQ], debug_vis=True)
    p.sc_gq = intl("sc_gq", [D, TQ])
    p.sc_stb1 = intl("sc_stb1", [2, TQ])
    p.ag_h2 = intl("ag_h2", [5, D])
    p.sc_w1q = intl("sc_w1q", [D, FB], debug_vis=True)
    p.sc_w2q = intl("sc_w2q", [D, FB])
    p.sc_w3q = intl("sc_w3q", [FB, D])
    p.dbg = intl("dbgout", [128, 2048], F32, debug_vis=True)
    p.sc_st = intl("sc_st", [8, T], F32, debug_vis=True)
    p.sc_stb = intl("sc_stb", [4, T], BF16, debug_vis=True)

    p.c_ident = nc.inline_tensor(np.eye(128, dtype=bf16), name="c_ident").ap()
    p.c_ones = nc.inline_tensor(np.ones((128, 512), dtype=bf16), name="c_ones").ap()
    s_i = np.arange(CD)[:, None]
    t_i = np.arange(CD)[None, :]
    p.c_maskUs = nc.inline_tensor((s_i < t_i).astype(bf16), name="c_maskUs").ap()
    p.c_maskUi = nc.inline_tensor((s_i <= t_i).astype(bf16), name="c_maskUi").ap()
    dm = np.concatenate(
        [((np.arange(128)[:, None] + 128 * a) <= np.arange(512)[None, :]).astype(bf16)
         for a in range(4)], axis=1)
    p.c_diag = nc.inline_tensor(dm, name="c_diag").ap()
    p.c_iota = nc.inline_tensor(
        np.tile((np.arange(TQ, dtype=np.float32) + 1.0)[None, :], (128, 1)),
        name="c_iota").ap()
    p.c_zero = nc.inline_tensor(np.zeros((128, 512), np.float32), name="c_zero").ap()
    p.c_identf = nc.inline_tensor(np.eye(64, dtype=np.float32), name="c_identf").ap()

    sb = lambda name, shape, dt: ctx.enter_context(nc.sbuf_tensor(name, shape, dt))
    p.ident = sb("ident", [128, 128], BF16)
    p.ones = sb("ones", [128, 512], BF16)
    p.maskUs = sb("maskUs", [CD, CD], BF16)
    p.maskUi = sb("maskUi", [CD, CD], BF16)
    p.diagm = sb("diagm", [128, 4 * 512], BF16)
    p.zeros = sb("zeros", [128, 512], F32)

    p.v_babn = sb("v_babn", [8, 1], F32)
    p.qit = sb("qit", [1, 1], mybir.dt.uint32)
    p.w_conv = sb("w_conv", [128, 48], F32)
    p.v_bab = sb("v_bab", [8, 1], F32)
    p.v_ln1 = sb("v_ln1", [128, NDC], F32)
    p.v_ln2 = sb("v_ln2", [128, NDC], F32)
    p.v_td = sb("v_td", [128, NDC], F32)
    p.v_tf = sb("v_tf", [128, NDC], F32)
    p.v_mk = sb("v_mk", [128, NDC], F32)
    p.v_mv = sb("v_mv", [128, NDC], F32)
    p.v_mr = sb("v_mr", [128, NDC], F32)
    p.v_w = sb("v_w", [128, NDC], F32)
    p.v_ew = sb("v_ew", [128, NDC], F32)
    p.v_decq = sb("v_decq", [128, NDC], F32)
    p.v_mk1 = sb("v_mk1", [128, NDC], F32)
    p.v_mv1 = sb("v_mv1", [128, NDC], F32)
    p.v_mr1 = sb("v_mr1", [128, NDC], F32)
    p.v_wtq = sb("v_wtq", [128, NDC], F32)

    p.w_pall = sb("w_pall", [128, NDC * 64], BF16)
    p.w_a2 = sb("w_a2", [128, NDC * 16], BF16)

    p.ps = [ctx.enter_context(nc.psum_tensor(f"ps{i}", [128, 512], F32)) for i in range(6)]
    p.psb = [ctx.enter_context(nc.psum_tensor(f"psb{i}", [128, 1024], BF16)) for i in range(2)]

    p.constpool = sb("constpool", [128, 8], F32)
    p.onesf32 = sb("onesf32", [1, 128], F32)
    p.CONST_VALS = [1e-5, 1e-6, 0.5, -0.5, 1.0, 0.0, 0.0, 0.0]
    p.cst = lambda v: p.constpool[:, p.CONST_VALS.index(v):p.CONST_VALS.index(v) + 1]

    p.s_ld = Sem(nc, ctx, "s_ld")
    p.s_st = Sem(nc, ctx, "s_st")
    p.s_l2 = Sem(nc, ctx, "s_l2")
    p.s_mm = Sem(nc, ctx, "s_mm")
    p.s_ve = Sem(nc, ctx, "s_ve")
    p.s_ac = Sem(nc, ctx, "s_ac")
    p.s_gp = Sem(nc, ctx, "s_gp")
    p.sqs = [Sem(nc, ctx, f"sqs{i}") for i in range(8)]
    p.sqg = [Sem(nc, ctx, f"sqg{i}") for i in range(4)]
    p.sql = [Sem(nc, ctx, f"sql{i}") for i in range(4)]

    phases = [ph0, ph1, ph2, ph3, ph4, ph5, ph6, ph7, ph8, ph9][:N_PHASES]
    p.phase_marks = []
    for ph in phases:
        ph(p)
        nc.all_engine_barrier()
        p.phase_marks.append((ph.__name__, nc.n_instructions()))
    global LAST_MARKS
    LAST_MARKS = list(p.phase_marks)
    ctx.close()
    p.nc_ref = nc
    return nc


def ph0(p):
    nc = p.nc
    ctx0 = ExitStack()
    wofs_sb = ctx0.enter_context(nc.sbuf_tensor("wofs_sb", [128, NDC * 512], BF16))
    wvfs_sb = ctx0.enter_context(nc.sbuf_tensor("wvfs_sb", [128, 4 * D], BF16))
    w2_sb = ctx0.enter_context(nc.sbuf_tensor("w2_sb", [128, 64], BF16))
    aw_sb = ctx0.enter_context(nc.sbuf_tensor("aw_sb", [128, NDC * 16], BF16))

    ld0, st0, ve00 = p.s_ld.n, p.s_st.n, p.s_ve.n
    with nc.Block() as b:
        @b.sync
        def _(e):
            e.dma_start(out=p.ident[:], in_=p.c_ident).then_inc(p.s_ld.h, 16); p.s_ld.inc(16)
            e.dma_start(out=p.ones[:], in_=p.c_ones).then_inc(p.s_ld.h, 16); p.s_ld.inc(16)
            e.dma_start(out=p.maskUs[:], in_=p.c_maskUs).then_inc(p.s_ld.h, 16); p.s_ld.inc(16)
            e.dma_start(out=p.maskUi[:], in_=p.c_maskUi).then_inc(p.s_ld.h, 16); p.s_ld.inc(16)
            e.dma_start(out=p.diagm[:], in_=p.c_diag).then_inc(p.s_ld.h, 16); p.s_ld.inc(16)
            e.dma_start(out=p.zeros[:], in_=p.c_zero).then_inc(p.s_ld.h, 16); p.s_ld.inc(16)
            for dst, src in [(p.v_bab, p.bab), (p.v_ln1, p.ln1r), (p.v_ln2, p.ln2r),
                             (p.v_td, p.tdr), (p.v_tf, p.tfr), (p.v_mk, p.mkr),
                             (p.v_mv, p.mvr), (p.v_mr, p.mrr)]:
                e.dma_start(out=dst[:], in_=src).then_inc(p.s_ld.h, 16); p.s_ld.inc(16)
            e.dma_start(out=p.qit[:], in_=p.qi).then_inc(p.s_ld.h, 16); p.s_ld.inc(16)
            e.dma_start(out=p.w_conv[:].rearrange("q (r f) -> q r f", f=4),
                        in_=p.convw.rearrange("(r q) f -> q r f", q=128)
                        ).then_inc(p.s_ld.h, 16); p.s_ld.inc(16)
            e.wait_ge(p.s_ld.h, p.s_ld.n)

        @b.gpsimd
        def _(g):
            for dst, src, oc in [
                (wofs_sb, p.wofs, 512), (aw_sb, p.A_, 16),
            ]:
                g.dma_start(out=dst[:, 0:NDC * oc].rearrange("q (dc oc) -> q dc oc", oc=oc),
                            in_=src.rearrange("(dc q) oc -> q dc oc", q=128)
                            ).then_inc(p.s_st.h, 16); p.s_st.inc(16)
            g.dma_start(out=wvfs_sb[:].rearrange("q (hc oc) -> q hc oc", oc=D),
                        in_=p.wvfs.rearrange("(hc q) oc -> q hc oc", q=128)
                        ).then_inc(p.s_st.h, 16); p.s_st.inc(16)
            g.wait_ge(p.s_st.h, p.s_st.n)

        @b.scalar
        def _(sc):
            sc.wait_ge(p.s_ld.h, ld0 + 16 * 16)
            sc.wait_ge(p.s_st.h, st0 + 3 * 16)
            sc.wait_ge(p.s_ve.h, ve00 + 1)
            for dc in range(NDC):
                sc.mul(out=p.w_a2[:, dc * 16:(dc + 1) * 16],
                       in_=aw_sb[:, dc * 16:(dc + 1) * 16], mul=p.v_ln2[:, dc:dc + 1])
            sc.mul(out=p.v_babn[:], in_=p.v_bab[:], mul=-1.0)
            sc.activation(out=p.v_mk1[:], in_=p.v_mk[:], func=AF.Identity, scale=-1.0,
                          bias=p.cst(1.0))
            sc.activation(out=p.v_mv1[:], in_=p.v_mv[:], func=AF.Identity, scale=-1.0,
                          bias=p.cst(1.0))
            sc.activation(out=p.v_mr1[:], in_=p.v_mr[:], func=AF.Identity, scale=-1.0,
                          bias=p.cst(1.0))
            sc.activation(out=p.v_w[:], in_=p.v_td[:], func=AF.Exp
                          ).then_inc(p.s_ac.h, 1); _ = p.s_ac.inc()
            sc.wait_ge(p.s_ac.h, p.s_ac.n)
            sc.mul(out=p.v_w[:], in_=p.v_w[:], mul=-1.0).then_inc(p.s_ac.h, 1); _ = p.s_ac.inc()
            sc.wait_ge(p.s_ac.h, p.s_ac.n)
            sc.activation(out=p.v_ew[:], in_=p.v_w[:], func=AF.Exp)
            sc.activation(out=p.v_decq[:], in_=p.v_w[:], func=AF.Exp, scale=float(TQ))
            sc.mul(out=p.v_wtq[:], in_=p.v_w[:], mul=float(TQ)
                   ).then_inc(p.s_ac.h, 1); _ = p.s_ac.inc()

        @b.vector
        def _(v):
            for j, val in enumerate(p.CONST_VALS):
                v.memset(p.constpool[:, j:j + 1], float(val))
            ins = v.memset(p.onesf32[:], 1.0)
            ins.then_inc(p.s_ve.h, 1); _ = p.s_ve.inc()

        @b.tensor
        def _(t):
            t.wait_ge(p.s_st.h, st0 + 3 * 16)
            for ic in range(4):
                for dc in range(NDC):
                    ins = t.matmul(p.ps[0][0:128, ic * 16:(ic + 1) * 16],
                                   wofs_sb[:, dc * 512 + ic * 128: dc * 512 + ic * 128 + 128],
                                   aw_sb[:, dc * 16:(dc + 1) * 16],
                                   start=(dc == 0), stop=(dc == NDC - 1))
            ins.then_inc(p.s_mm.h, 1); _ = p.s_mm.inc()

    nc.all_engine_barrier()
    ve0, mm0, ac0 = p.s_ve.n, p.s_mm.n, p.s_ac.n
    with nc.Block() as b2:
        @b2.vector
        def _(v):
            v.tensor_copy(w2_sb[:], p.ps[0][0:128, 0:64]).then_inc(p.s_ve.h, 1); _ = p.s_ve.inc()

        @b2.tensor
        def _(t):
            t.wait_ge(p.s_ve.h, ve0 + 1)
            for dc in range(NDC):
                if dc >= 2:
                    t.wait_ge(p.s_ac.h, ac0 + dc - 1)
                bank = p.ps[1 + dc % 2]
                for h in range(NHL):
                    ins = t.matmul(bank[0:128, h * 16:(h + 1) * 16],
                                   wvfs_sb[:, h * D + dc * 128: h * D + dc * 128 + 128],
                                   w2_sb[:, h * 16:(h + 1) * 16], start=True, stop=True)
                ins.then_inc(p.s_mm.h, 1); _ = p.s_mm.inc()

        @b2.scalar
        def _(sc):
            for dc in range(NDC):
                sc.wait_ge(p.s_mm.h, mm0 + dc + 1)
                sc.mul(out=p.w_pall[:, dc * 64:(dc + 1) * 64],
                       in_=p.ps[1 + dc % 2][0:128, 0:64], mul=p.v_ln1[:, dc:dc + 1]
                       ).then_inc(p.s_ac.h, 1); _ = p.s_ac.inc()

        @b2.gpsimd
        def _(g):
            p.qv = nc.values_load(p.qit[:], engines=[mybir.EngineType.Pool])
            p.qcol = [nc.snap(p.qv * TQ + u * 512, engines=[mybir.EngineType.Pool],
                              min_val=0, max_val=T - 512) for u in range(4)]
            p.qrow = nc.snap(p.qv * D, engines=[mybir.EngineType.Pool],
                             min_val=0, max_val=3 * D)
    ctx0.close()


def ph1(p):
    nc = p.nc
    ctx1 = ExitStack()
    xf = [ctx1.enter_context(nc.sbuf_tensor(f"p1xf{i}", [128, 512], F32)) for i in range(2)]
    xb = [ctx1.enter_context(nc.sbuf_tensor(f"p1xb{i}", [128, 512], BF16)) for i in range(2)]
    xq = [ctx1.enter_context(nc.sbuf_tensor(f"p1xq{i}", [128, 512], BF16)) for i in range(2)]
    sx = ctx1.enter_context(nc.sbuf_tensor("p1sx", [33, T], F32))
    st = ctx1.enter_context(nc.sbuf_tensor("p1st", [128, 64 * 14], F32))

    NIT = NTT * NDC
    sx0, sx1 = p.sqs[0], p.sqs[1]
    x0_0, x1_0 = sx0.n, sx1.n
    ve0, mm0, ac0 = p.s_ve.n, p.s_mm.n, p.s_ac.n
    with nc.Block() as b:
        @b.sync
        def _(e):
            for k in range(NIT):
                tt, dc = divmod(k, NDC)
                sm = sx0 if k % 2 == 0 else sx1
                if k >= 2:
                    e.wait_ge(p.s_ve.h, ve0 + 2 * (k - 1))
                e.dma_start(out=xf[k % 2][:],
                            in_=p.xT[dc * 128:(dc + 1) * 128, tt * 512:(tt + 1) * 512]
                            ).then_inc(sm.h, 16)
                sm.inc(16)

        @b.vector
        def _(v):
            for k in range(NIT):
                base = x0_0 if k % 2 == 0 else x1_0
                v.wait_ge((sx0 if k % 2 == 0 else sx1).h, base + (k // 2 + 1) * 16)
                if k >= 2:
                    v.wait_ge(p.s_mm.h, mm0 + 2 * (k - 1))
                chain(v, p.s_ve, v.tensor_copy(xb[k % 2][:], xf[k % 2][:]))
                v.tensor_mul(xq[k % 2][:], xb[k % 2][:], xb[k % 2][:]
                             ).then_inc(p.s_ve.h, 1)
                p.s_ve.inc()

        @b.tensor
        def _(t):
            for k in range(NIT):
                tt, dc = divmod(k, NDC)
                t.wait_ge(p.s_ve.h, ve0 + 2 * k + 2)
                if dc == 0 and tt >= 2:
                    t.wait_ge(p.s_ac.h, ac0 + tt - 1)
                if k >= 2:
                    pass
                ba, bq = p.ps[2 * (tt % 2)], p.ps[2 * (tt % 2) + 1]
                t.matmul(ba[0:1, 0:512], p.ones[:, 0:1], xb[k % 2][:],
                         start=(dc == 0), stop=(dc == NDC - 1)).then_inc(p.s_mm.h, 1)
                t.matmul(bq[0:1, 0:512], p.ones[:, 0:1], xq[k % 2][:],
                         start=(dc == 0), stop=(dc == NDC - 1)
                         ).then_inc(p.s_mm.h, 1)
                p.s_mm.inc(2)

        @b.scalar
        def _(sc):
            for tt in range(NTT):
                sc.wait_ge(p.s_mm.h, mm0 + (tt + 1) * NDC * 2)
                sc.copy(out=sx[0:1, tt * 512:(tt + 1) * 512], in_=p.ps[2 * (tt % 2)][0:1, 0:512])
                sc.copy(out=sx[32:33, tt * 512:(tt + 1) * 512], in_=p.ps[2 * (tt % 2) + 1][0:1, 0:512]
                        ).then_inc(p.s_ac.h, 1); _ = p.s_ac.inc()

    nc.all_engine_barrier()
    c = lambda i: st[:, 64 * i:64 * (i + 1)]
    ld0, ve0, ac0 = p.s_ld.n, p.s_ve.n, p.s_ac.n
    with nc.Block() as b2:
        @b2.sync
        def _(e):
            e.dma_start(out=p.sc_st[0:1, :], in_=sx[0:1, :]).then_inc(p.s_ld.h, 16); p.s_ld.inc(16)
            e.dma_start(out=p.sc_st[1:2, :], in_=sx[32:33, :]).then_inc(p.s_ld.h, 16); p.s_ld.inc(16)
            e.wait_ge(p.s_ld.h, p.s_ld.n)
            e.dma_start(out=c(0), in_=p.sc_st.rearrange("o (q w) -> o q w", q=128)[0]
                        ).then_inc(p.s_ld.h, 16); p.s_ld.inc(16)
            e.dma_start(out=c(1), in_=p.sc_st.rearrange("o (q w) -> o q w", q=128)[1]
                        ).then_inc(p.s_ld.h, 16); p.s_ld.inc(16)
            e.wait_ge(p.s_ld.h, p.s_ld.n)

        @b2.vector
        def _(v):
            v.wait_ge(p.s_ld.h, ld0 + 64)
            chain(v, p.s_ve, v.tensor_scalar_mul(c(2), c(0), 1.0 / D))
            chain(v, p.s_ve, v.tensor_scalar_mul(c(3), c(1), 1.0 / D))
            chain(v, p.s_ve, v.tensor_mul(c(5), c(2), c(2)))
            chain(v, p.s_ve, v.tensor_sub(c(4), c(3), c(5)))
            v.tensor_scalar_max(c(4), c(4), 0.0).then_inc(p.s_ve.h, 1); _ = p.s_ve.inc()
            v.wait_ge(p.s_ac.h, ac0 + 1)
            chain(v, p.s_ve, v.reciprocal(c(6), c(5)))
            chain(v, p.s_ve, v.tensor_mul(c(7), c(4), c(6)))
            chain(v, p.s_ve, v.tensor_mul(c(7), c(7), c(6)))
            v.tensor_scalar_add(c(7), c(7), 1e-6).then_inc(p.s_ve.h, 1); _ = p.s_ve.inc()
            v.wait_ge(p.s_ac.h, ac0 + 2)
            chain(v, p.s_ve, v.reciprocal(c(9), c(8)))
            chain(v, p.s_ve, v.tensor_mul(c(10), c(6), c(9)))
            chain(v, p.s_ve, v.tensor_mul(c(11), c(2), c(6)))
            chain(v, p.s_ve, v.tensor_scalar_mul(c(11), c(11), -1.0))
            chain(v, p.s_ve, v.tensor_mul(c(12), c(2), c(10)))
            v.tensor_scalar_mul(c(12), c(12), -1.0).then_inc(p.s_ve.h, 1); _ = p.s_ve.inc()

        @b2.scalar
        def _(sc):
            sc.wait_ge(p.s_ve.h, ve0 + 5)
            sc.activation(out=c(5), in_=c(4), func=AF.Sqrt, bias=p.cst(1e-5)
                          ).then_inc(p.s_ac.h, 1); _ = p.s_ac.inc()
            sc.wait_ge(p.s_ve.h, ve0 + 9)
            sc.activation(out=c(8), in_=c(7), func=AF.Sqrt).then_inc(p.s_ac.h, 1); _ = p.s_ac.inc()

        @b2.gpsimd
        def _(g):
            g.wait_ge(p.s_ve.h, ve0 + 15)
            for i, cc in [(0, 6), (1, 11), (2, 10), (3, 12)]:
                g.dma_start(out=p.sc_st.rearrange("o (q w) -> o q w", q=128)[4 + i],
                            in_=c(cc)).then_inc(p.s_st.h, 16)
                p.s_st.inc(16)
                g.dma_start(out=p.sc_stb.rearrange("o (q w) -> o q w", q=128)[i],
                            in_=c(cc)).then_inc(p.s_st.h, 16)
                p.s_st.inc(16)
            g.wait_ge(p.s_st.h, p.s_st.n)
    ctx1.close()


def ph2(p):
    ph2a(p)
    p.nc.all_engine_barrier()
    ph2b(p)
    p.nc.all_engine_barrier()
    ph2c(p)
    p.nc.all_engine_barrier()
    ph2d(p)


def dyn_off(ap0, sym):
    import concourse.ap as _ap
    return _ap.AP(tensor=ap0.tensor, offset=sym, ap=[list(x) for x in ap0.ap])


def xn_tile_dram(p, dram, tt):
    return dram.rearrange("(dc q) w -> q dc w", q=128)[:, :, tt * 512:(tt + 1) * 512]


def ph2a(p):
    nc = p.nc
    with ExitStack() as ctx:
        sbt = lambda n, s, d: ctx.enter_context(nc.sbuf_tensor(n, s, d))
        xf = [sbt(f"a_xf{i}", [128, 512], F32) for i in range(2)]
        xbt = [sbt(f"a_xb{i}", [128, 512], BF16) for i in range(2)]
        bc = sbt("a_bc", [128, 4 * 512], BF16)
        xnt = [sbt(f"a_xnt{i}", [128, NDC * 512], BF16) for i in range(2)]
        xrt = [sbt(f"a_xrt{i}", [128, NDC * 512], BF16) for i in range(2)]
        tmp = sbt("a_tmp", [128, 512], BF16)
        srow = [sbt(f"a_srow{i}", [1, 4 * 512], BF16) for i in range(2)]
        q = AutoQ(p)
        for tt in range(NTT):
            pb = tt % 2
            q.dma("sync", srow[pb][:], p.sc_stb[:, tt * 512:(tt + 1) * 512],
                  ("srow", pb), None)
            for j in range(4):
                q.do("tensor", lambda t, j=j, pb=pb: t.matmul(
                    p.ps[2 + j][0:128, 0:512], p.ones[0:1, 0:128],
                    srow[pb][0:1, j * 512:(j + 1) * 512], start=True, stop=True),
                    reads=[("srow", pb)], writes=[("ps", 2 + j)])
                q.do("vector", lambda v, j=j: v.tensor_copy(
                    bc[:, j * 512:(j + 1) * 512], p.ps[2 + j][0:128, 0:512]),
                    reads=[("ps", 2 + j)], writes=[("bc", j)])
            for dc in range(NDC):
                k = tt * NDC + dc
                q.dma("sync", xf[k % 2][:],
                      p.xT[dc * 128:(dc + 1) * 128, tt * 512:(tt + 1) * 512],
                      ("xf", k % 2), None)
                q.do("vector", lambda v, k=k: v.tensor_copy(xbt[k % 2][:], xf[k % 2][:]),
                     reads=[("xf", k % 2)], writes=[("xb", k % 2)])
                o = dc * 512
                q.do("vector", lambda v, k=k: v.tensor_mul(tmp[:], xbt[k % 2][:], bc[:, 0:512]),
                     reads=[("xb", k % 2), ("bc", 0)], writes=[("tmp",)])
                q.do("vector", lambda v, o=o, pb=pb: v.tensor_add(
                    xnt[pb][:, o:o + 512], tmp[:], bc[:, 512:1024]),
                    reads=[("tmp",), ("bc", 1)], writes=[("xnt", pb)])
                q.do("vector", lambda v, k=k: v.tensor_mul(tmp[:], xbt[k % 2][:], bc[:, 1024:1536]),
                     reads=[("xb", k % 2), ("bc", 2)], writes=[("tmp",)])
                q.do("vector", lambda v, o=o, pb=pb: v.tensor_add(
                    xrt[pb][:, o:o + 512], tmp[:], bc[:, 1536:2048]),
                    reads=[("tmp",), ("bc", 3)], writes=[("xrt", pb)])
            q.dma("gpsimd", xn_tile_dram(p, p.sc_xn, tt),
                  xnt[pb][:].rearrange("q (dc w) -> q dc w", w=512),
                  ("sc_xn",), [("xnt", pb)])
            q.dma("gpsimd", xn_tile_dram(p, p.sc_xr, tt),
                  xrt[pb][:].rearrange("q (dc w) -> q dc w", w=512),
                  ("sc_xr",), [("xrt", pb)])
        q.finish()


def ph2b(p):
    nc = p.nc
    with ExitStack() as ctx:
        sbt = lambda n, s, d: ctx.enter_context(nc.sbuf_tensor(n, s, d))
        wq = sbt("b_wqkv", [128, NDC * 12 * 128], BF16)
        xn = [sbt(f"b_xn{i}", [128, NDC * 512], BF16) for i in range(2)]
        zb = sbt("b_zb", [128, 12 * 516], BF16)
        cv = sbt("b_cv", [128, 512], BF16)
        cv2 = sbt("b_cv2", [128, 512], BF16)
        sq = sbt("b_sq", [128, 512], BF16)
        nrow = sbt("b_nrow", [1, 8 * 512], F32)
        nrowb = sbt("b_nrowb", [1, 8 * 512], BF16)
        stg = sbt("b_stg", [128, 12 * 512], BF16)
        q = AutoQ(p)
        q.dma("gpsimd", wq[:].rearrange("q (dc oc) -> q dc oc", oc=12 * 128),
              p.wqkvT.rearrange("(dc q) oc -> q dc oc", q=128), ("wq",), None)
        for oc in range(12):
            q.do("vector", lambda v, oc=oc: v.memset(zb[:, oc * 516:oc * 516 + 3], 0.0),
                 reads=[], writes=[("zb", oc)])
        for tt in range(NTT):
            q.dma("sync", xn[tt % 2][:].rearrange("q (dc w) -> q dc w", w=512),
                  xn_tile_dram(p, p.sc_xn, tt), ("xn", tt % 2), None)
            for oc in range(12):
                zo = oc * 516

                def grp(t, oc=oc, tt=tt):
                    bank = p.ps[oc % 2]
                    for dc in range(NDC):
                        ins = t.matmul(bank[0:128, 0:512],
                                       wq[:, (dc * 12 + oc) * 128:(dc * 12 + oc) * 128 + 128],
                                       xn[tt % 2][:, dc * 512:(dc + 1) * 512],
                                       start=(dc == 0), stop=(dc == NDC - 1))
                    return ins
                q.do("tensor", grp, reads=[("wq",), ("xn", tt % 2)], writes=[("ps", oc % 2)])
                q.do("vector", lambda v, zo=zo, oc=oc: v.tensor_copy(
                    zb[:, zo + 3:zo + 515], p.ps[oc % 2][0:128, 0:512]),
                    reads=[("ps", oc % 2)], writes=[("zb", oc)])
                q.do("scalar", lambda sc, zo=zo, oc=oc: sc.mul(
                    out=cv2[:], in_=zb[:, zo + 3:zo + 515], mul=p.w_conv[:, oc * 4 + 3:oc * 4 + 4]),
                    reads=[("zb", oc)], writes=[("cv2",)])
                q.do("vector", lambda v, zo=zo, oc=oc: v.scalar_tensor_tensor(
                    cv[:], zb[:, zo + 2:zo + 514], p.w_conv[:, oc * 4 + 2:oc * 4 + 3],
                    cv2[:], ALU.mult, ALU.add),
                    reads=[("zb", oc), ("cv2",)], writes=[("cv",)])
                q.do("vector", lambda v, zo=zo, oc=oc: v.scalar_tensor_tensor(
                    cv[:], zb[:, zo + 1:zo + 513], p.w_conv[:, oc * 4 + 1:oc * 4 + 2],
                    cv[:], ALU.mult, ALU.add),
                    reads=[("zb", oc), ("cv",)], writes=[("cv",)])
                q.do("vector", lambda v, zo=zo, oc=oc: v.scalar_tensor_tensor(
                    cv[:], zb[:, zo + 0:zo + 512], p.w_conv[:, oc * 4 + 0:oc * 4 + 1],
                    cv[:], ALU.mult, ALU.add),
                    reads=[("zb", oc), ("cv",)], writes=[("cv",)])
                q.do("vector", lambda v, zo=zo: v.tensor_copy(zb[:, zo:zo + 3], zb[:, zo + 512:zo + 515]),
                     reads=[("zb", oc)], writes=[("zb", oc)])
                q.do("scalar", lambda sc, oc=oc: sc.activation(
                    out=stg[:, oc * 512:(oc + 1) * 512], in_=cv[:], func=AF.Silu),
                    reads=[("cv",)], writes=[("stg", oc)])
                if oc < 8:
                    q.do("vector", lambda v, oc=oc: v.tensor_mul(
                        sq[:], stg[:, oc * 512:(oc + 1) * 512], stg[:, oc * 512:(oc + 1) * 512]),
                        reads=[("stg", oc)], writes=[("sq",)])
                    q.do("tensor", lambda t: t.matmul(
                        p.ps[2][0:1, 0:512], p.ones[:, 0:1], sq[:], start=True, stop=True),
                        reads=[("sq",)], writes=[("ps", 2)])
                    q.do("scalar", lambda sc, oc=oc: sc.copy(
                        out=nrow[0:1, oc * 512:(oc + 1) * 512], in_=p.ps[2][0:1, 0:512]),
                        reads=[("ps", 2)], writes=[("nrm", oc)])
            rws = [("nrm", i) for i in range(8)]
            q.do("vector", lambda v: v.tensor_scalar_max(nrow[:], nrow[:], 1e-24),
                 reads=rws, writes=rws)
            q.do("scalar", lambda sc: sc.activation(out=nrow[:], in_=nrow[:], func=AF.Sqrt),
                 reads=rws, writes=rws)
            q.do("vector", lambda v: v.reciprocal(nrow[:], nrow[:]),
                 reads=rws, writes=rws)
            q.do("vector", lambda v: v.tensor_copy(nrowb[:], nrow[:]),
                 reads=rws, writes=[("nrowb",)])
            for oc in range(8):
                q.do("tensor", lambda t, oc=oc: t.matmul(
                    p.ps[3][0:128, 0:512], p.ones[0:1, 0:128],
                    nrowb[0:1, oc * 512:(oc + 1) * 512], start=True, stop=True),
                    reads=[("nrowb",)], writes=[("ps", 3)])
                q.do("vector", lambda v, oc=oc: v.tensor_mul(
                    stg[:, oc * 512:(oc + 1) * 512], stg[:, oc * 512:(oc + 1) * 512],
                    p.ps[3][0:128, 0:512]),
                    reads=[("ps", 3), ("stg", oc)], writes=[("stg", oc)])
            for dst, lo in ((p.sc_qd, 0), (p.sc_kd, 4), (p.sc_vd, 8)):
                q.dma("gpsimd",
                      dst.rearrange("(oc q) w -> q oc w", q=128)[:, :, tt * 512:(tt + 1) * 512],
                      stg[:, lo * 512:(lo + 4) * 512].rearrange("q (oc w) -> q oc w", w=512),
                      ("spill", dst.tensor.name), [("stg", lo + i) for i in range(4)])
        q.finish()


def ph2c(p):
    nc = p.nc
    with ExitStack() as ctx:
        sbt = lambda n, s, d: ctx.enter_context(nc.sbuf_tensor(n, s, d))
        wab = sbt("c_wab", [128, NDC * 8], BF16)
        wg = sbt("c_wg", [128, NDC * 512], BF16)
        xn = [sbt(f"c_xn{i}", [128, NDC * 512], BF16) for i in range(2)]
        sig = sbt("c_sig", [8, 512], F32)
        la = sbt("c_la", [4, 512], F32)
        stg = sbt("c_stg", [128, 4 * 512], BF16)
        q = AutoQ(p)
        q.dma("gpsimd", wab[:].rearrange("q (dc o) -> q dc o", o=8),
              p.wabT.rearrange("(dc q) o -> q dc o", q=128), ("wab",), None)
        q.dma("gpsimd", wg[:].rearrange("q (dc o) -> q dc o", o=512),
              p.wgT.rearrange("(dc q) o -> q dc o", q=128), ("wg",), None)
        for tt in range(NTT):
            q.dma("sync", xn[tt % 2][:].rearrange("q (dc w) -> q dc w", w=512),
                  xn_tile_dram(p, p.sc_xn, tt), ("xn", tt % 2), None)

            def abgrp(t, tt=tt):
                for dc in range(NDC):
                    ins = t.matmul(p.ps[2][0:8, 0:512], wab[:, dc * 8:(dc + 1) * 8],
                                   xn[tt % 2][:, dc * 512:(dc + 1) * 512],
                                   start=(dc == 0), stop=(dc == NDC - 1))
                return ins
            q.do("tensor", abgrp, reads=[("wab",), ("xn", tt % 2)], writes=[("ps", 2)])
            q.do("scalar", lambda sc: sc.activation(
                out=sig[:], in_=p.ps[2][0:8, 0:512], func=AF.Sigmoid, bias=p.v_bab[:]),
                reads=[("ps", 2)], writes=[("sig",)])
            q.do("scalar", lambda sc: sc.activation(
                out=la[:], in_=sig[0:4, :], func=AF.Ln),
                reads=[("sig",)], writes=[("la",)])
            q.dma("gpsimd", p.sc_ab[0:8, tt * 512:(tt + 1) * 512], sig[:], ("sc_ab8",), [("sig",)])
            q.dma("gpsimd", p.sc_ab[8:12, tt * 512:(tt + 1) * 512], la[:], ("sc_ab4",), [("la",)])
            for oc in range(4):
                def ggrp(t, oc=oc, tt=tt):
                    bank = p.ps[oc % 2]
                    for dc in range(NDC):
                        ins = t.matmul(bank[0:128, 0:512],
                                       wg[:, (dc * 4 + oc) * 128:(dc * 4 + oc) * 128 + 128],
                                       xn[tt % 2][:, dc * 512:(dc + 1) * 512],
                                       start=(dc == 0), stop=(dc == NDC - 1))
                    return ins
                q.do("tensor", ggrp, reads=[("wg",), ("xn", tt % 2)], writes=[("ps", oc % 2)])
                q.do("scalar", lambda sc, oc=oc: sc.activation(
                    out=stg[:, oc * 512:(oc + 1) * 512], in_=p.ps[oc % 2][0:128, 0:512],
                    func=AF.Silu),
                    reads=[("ps", oc % 2)], writes=[("stg", oc)])
            q.dma("gpsimd",
                  p.sc_gg.rearrange("(oc q) w -> q oc w", q=128)[:, :, tt * 512:(tt + 1) * 512],
                  stg[:].rearrange("q (oc w) -> q oc w", w=512),
                  ("spill_gg",), [("stg", i) for i in range(4)])
        q.finish()


def ph2d(p):
    nc = p.nc
    with ExitStack() as ctx:
        sbt = lambda n, s, d: ctx.enter_context(nc.sbuf_tensor(n, s, d))
        wqf = sbt("d_wqf", [128, NDC * 512], BF16)
        wkf = sbt("d_wkf", [128, NDC * 512], BF16)
        xr = [sbt(f"d_xr{i}", [128, NDC * 512], BF16) for i in range(2)]
        stg = sbt("d_stg", [128, 8 * 512], BF16)
        sq = sbt("d_sq", [128, 512], BF16)
        nrow = sbt("d_nrow", [1, 8 * 512], F32)
        brow = sbt("d_brow", [1, 8 * 512], F32)
        bfin = sbt("d_bfin", [1, 8], F32)
        vt = sbt("d_vt", [128, 512], BF16)
        stv = sbt("d_stv", [128, 4 * 132], BF16)
        q = AutoQ(p)
        q.dma("gpsimd", wqf[:].rearrange("q (dc o) -> q dc o", o=512),
              p.wqfT.rearrange("(dc q) o -> q dc o", q=128), ("wqf",), None)
        q.dma("gpsimd", wkf[:].rearrange("q (dc o) -> q dc o", o=512),
              p.wkfT.rearrange("(dc q) o -> q dc o", q=128), ("wkf",), None)
        for dc in range(NDC):
            q.do("scalar", lambda sc, dc=dc: sc.mul(
                out=wqf[:, dc * 512:(dc + 1) * 512], in_=wqf[:, dc * 512:(dc + 1) * 512],
                mul=p.v_ln1[:, dc:dc + 1]),
                reads=[("wqf",)], writes=[("wqf",)])
            q.do("scalar", lambda sc, dc=dc: sc.mul(
                out=wkf[:, dc * 512:(dc + 1) * 512], in_=wkf[:, dc * 512:(dc + 1) * 512],
                mul=p.v_ln1[:, dc:dc + 1]),
                reads=[("wkf",)], writes=[("wkf",)])
        q.do("vector", lambda v: v.memset(brow[:], 0.0), reads=[], writes=[("brow",)])
        q.do("vector", lambda v: v.memset(
            stv[:].rearrange("q (g c) -> q g c", c=33)[:, :, 16:32], 0.0),
            reads=[], writes=[("stv_ones",)])
        q.do("vector", lambda v: v.memset(
            stv[:].rearrange("q (g c) -> q g c", c=33)[:, :, 32:33], 1.0),
            reads=[("stv_ones",)], writes=[("stv_ones",)])
        for tt in range(NTT):
            q.dma("sync", xr[tt % 2][:].rearrange("q (dc w) -> q dc w", w=512),
                  xn_tile_dram(p, p.sc_xr, tt), ("xr", tt % 2), None)
            for oc in range(8):
                w = wqf if oc < 4 else wkf
                wb = ("wqf",) if oc < 4 else ("wkf",)

                def grp(t, oc=oc, tt=tt, w=w):
                    bank = p.ps[oc % 2]
                    for dc in range(NDC):
                        ins = t.matmul(bank[0:128, 0:512],
                                       w[:, (dc * 4 + oc % 4) * 128:(dc * 4 + oc % 4) * 128 + 128],
                                       xr[tt % 2][:, dc * 512:(dc + 1) * 512],
                                       start=(dc == 0), stop=(dc == NDC - 1))
                    return ins
                q.do("tensor", grp, reads=[wb, ("xr", tt % 2)], writes=[("ps", oc % 2)])
                q.do("scalar", lambda sc, oc=oc: sc.copy(
                    out=stg[:, oc * 512:(oc + 1) * 512], in_=p.ps[oc % 2][0:128, 0:512]),
                    reads=[("ps", oc % 2)], writes=[("stg", oc)])
                q.do("vector", lambda v, oc=oc: v.tensor_mul(
                    sq[:], stg[:, oc * 512:(oc + 1) * 512], stg[:, oc * 512:(oc + 1) * 512]),
                    reads=[("stg", oc)], writes=[("sq",)])
                q.do("tensor", lambda t: t.matmul(
                    p.ps[2][0:1, 0:512], p.ones[:, 0:1], sq[:], start=True, stop=True),
                    reads=[("sq",)], writes=[("ps", 2)])
                q.do("scalar", lambda sc, oc=oc: sc.copy(
                    out=nrow[0:1, oc * 512:(oc + 1) * 512], in_=p.ps[2][0:1, 0:512]),
                    reads=[("ps", 2)], writes=[("nrm", oc)])
            for dst, lo in ((p.sc_qf, 0), (p.sc_kf, 4)):
                q.dma("gpsimd",
                      dst.rearrange("(oc q) w -> q oc w", q=128)[:, :, tt * 512:(tt + 1) * 512],
                      stg[:, lo * 512:(lo + 4) * 512].rearrange("q (oc w) -> q oc w", w=512),
                      ("spill", dst.tensor.name), [("stg", lo + i) for i in range(4)])
            q.do("vector", lambda v: v.tensor_max(brow[:], brow[:], nrow[:]),
                 reads=[("brow",)] + [("nrm", i) for i in range(8)], writes=[("brow",)])
            def vgrp(t, tt=tt):
                for dc in range(NDC):
                    ins = t.matmul(p.ps[3][0:64, 0:512], p.w_pall[:, dc * 64:(dc + 1) * 64],
                                   xr[tt % 2][:, dc * 512:(dc + 1) * 512],
                                   start=(dc == 0), stop=(dc == NDC - 1))
                return ins
            q.do("tensor", vgrp, reads=[("xr", tt % 2)], writes=[("ps", 3)])
            q.do("vector", lambda v: v.tensor_copy(vt[0:64, :], p.ps[3][0:64, 0:512]),
                 reads=[("ps", 3)], writes=[("vt",)])
            for j in range(4):
                q.do("tensor", lambda t, j=j: t.transpose(
                    p.psb[0][0:128, j * 64:(j + 1) * 64], vt[0:64, j * 128:(j + 1) * 128],
                    p.ident[0:64, 0:64]),
                    reads=[("vt",)], writes=[("psb0",)])
                q.do("vector", lambda v, j=j: v.tensor_copy(
                    stv[:, j * 132:j * 132 + 132].rearrange("q (h c) -> q h c", c=33)[:, :, 0:16],
                    p.psb[0][0:128, j * 64:(j + 1) * 64].rearrange("q (h c) -> q h c", c=16)),
                    reads=[("psb0",)], writes=[("stv", j)])
            q.dma("gpsimd",
                  p.sc_vt[tt * 512:(tt + 1) * 512, :].rearrange("(j q) c -> q j c", q=128),
                  stv[:].rearrange("q (j c) -> q j c", c=132),
                  ("spill_vt",), [("stv", j) for j in range(4)] + [("stv_ones",)])
        for oc in range(8):
            q.do("vector", lambda v, oc=oc: v.reduce_max(
                bfin[0:1, oc:oc + 1], brow[0:1, oc * 512:(oc + 1) * 512],
                axis=mybir.AxisListType.X),
                reads=[("brow",)], writes=[("bfin",)])
        q.dma("gpsimd", p.sc_bnd[0:1, 0:8], bfin[:], ("sc_bnd0",), [("bfin",)])
        q.finish()


def ph3(p):
    nc = p.nc
    with ExitStack() as ctx:
        sbt = lambda n, s, d: ctx.enter_context(nc.sbuf_tensor(n, s, d))
        qh = sbt("e_qh", [128, T], BF16)
        kh = sbt("e_kh", [128, T], BF16)
        vh = sbt("e_vh", [128, T], BF16)
        gh = sbt("e_gh", [128, T], BF16)
        ogh = sbt("e_ogh", [128, T], BF16)
        lach = sbt("e_lach", [64, 128], F32)
        Lch = sbt("e_Lch", [64, 128], F32)
        bch = sbt("e_bch", [64, 128], F32)
        LT = sbt("e_LT", [128, 64], F32)
        bT = sbt("e_bT", [128, 64], F32)
        gT = sbt("e_gT", [128, 64], F32)
        nbgT = sbt("e_nbgT", [128, 64], F32)
        rrT = sbt("e_rrT", [128, 64], F32)
        gcb = sbt("e_gcb", [128, 64], F32)
        lcrow = sbt("e_lcrow", [1, 64], F32)
        Lrow = sbt("e_Lrow", [1, T], F32)
        brow = sbt("e_brow", [1, T], BF16)
        onesf = sbt("e_onesf", [1, 128], F32)
        identf = sbt("e_identf", [64, 64], F32)
        gbc = sbt("e_gbc", [128, 128], BF16)
        dtile = sbt("e_dtile", [128, 128], F32)
        d2t = sbt("e_d2t", [128, 128], F32)
        g2 = sbt("e_g2", [128, 128], BF16)
        bm = sbt("e_bm", [128, 128], BF16)
        tp1 = sbt("e_tp1", [128, 128], BF16)
        tpr = sbt("e_tpr", [128, 128], BF16)
        tun = sbt("e_tun", [128, 128], BF16)
        aqk = sbt("e_aqk", [128, 128], BF16)
        powu = [sbt(f"e_powu{i}", [128, 128], BF16) for i in range(6)]
        powp = [sbt(f"e_powp{i}", [128, 128], BF16) for i in range(5)]
        rch = [sbt(f"e_rch{i}", [128, 128], BF16) for i in range(2)]
        vtb = sbt("e_vtb", [128, 128], BF16)
        btile = sbt("e_btile", [128, 128], BF16)
        ut = sbt("e_ut", [128, 128], BF16)
        qt2 = sbt("e_qt2", [128, 128], BF16)
        ktr = sbt("e_ktr", [128, 128], BF16)
        stf = sbt("e_stf", [128, 128], F32)
        stb = sbt("e_stb", [128, 128], BF16)
        q = AutoQ(p)
        q.do("vector", lambda v: v.memset(onesf[:], 1.0), writes=[("onesf",)])
        q.dma("sync", identf[:], p.c_identf, ("identf",), None)
        for h in range(NHL):
            q.dma("sync", qh[:], p.sc_qd[h * 128:(h + 1) * 128, :], ("qh",), None)
            q.dma("sync", kh[:], p.sc_kd[h * 128:(h + 1) * 128, :], ("kh",), None)
            q.dma("sync", vh[:], p.sc_vd[h * 128:(h + 1) * 128, :], ("vh",), None)
            q.dma("sync", gh[:], p.sc_gg[h * 128:(h + 1) * 128, :], ("gh",), None)
            q.dma("sync", lach[:], p.sc_ab.rearrange("o (c w) -> o c w", w=128)[8 + h],
                  ("lach",), None)
            q.dma("sync", bch[:], p.sc_ab.rearrange("o (c w) -> o c w", w=128)[4 + h],
                  ("bch",), None)
            q.dma("gpsimd", brow[:], p.sc_ab[4 + h:5 + h, :], ("brow",), None)
            q.do("vector", lambda v: v.tensor_tensor_scan(
                Lch[:], lach[:], p.zeros[0:64, 0:128], 0.0, ALU.add, ALU.add),
                reads=[("lach",)], writes=[("Lch",)])
            q.dma("gpsimd", p.sc_st.rearrange("o (c w) -> o c w", w=128)[2], Lch[:],
                  ("strow2",), [("Lch",)])
            q.dma("sync", Lrow[:], p.sc_st[2:3, :], ("Lrow",), [("strow2",)])
            q.do("tensor", lambda t: t.transpose(p.ps[4][0:128, 0:64], Lch[:], identf[:]),
                 reads=[("Lch",), ("identf",)], writes=[("ps", 4)])
            q.do("vector", lambda v: v.tensor_copy(LT[:], p.ps[4][0:128, 0:64]),
                 reads=[("ps", 4)], writes=[("LT",)])
            q.do("tensor", lambda t: t.transpose(p.ps[4][0:128, 0:64], bch[:], identf[:]),
                 reads=[("bch",), ("identf",)], writes=[("ps", 4)])
            q.do("vector", lambda v: v.tensor_copy(bT[:], p.ps[4][0:128, 0:64]),
                 reads=[("ps", 4)], writes=[("bT",)])
            q.do("scalar", lambda sc: sc.activation(out=gT[:], in_=LT[:], func=AF.Exp),
                 reads=[("LT",)], writes=[("gT",)])
            q.do("vector", lambda v: v.tensor_mul(nbgT[:], bT[:], gT[:]),
                 reads=[("bT",), ("gT",)], writes=[("nbgT",)])
            q.do("vector", lambda v: v.tensor_scalar_mul(nbgT[:], nbgT[:], -1.0),
                 reads=[("nbgT",)], writes=[("nbgT",)])
            q.do("tensor", lambda t: t.transpose(p.ps[4][0:1, 0:64], Lch[:, 127:128], identf[:]),
                 reads=[("Lch",), ("identf",)], writes=[("ps", 4)])
            q.do("vector", lambda v: v.tensor_copy(lcrow[:], p.ps[4][0:1, 0:64]),
                 reads=[("ps", 4)], writes=[("lcrow",)])
            q.do("tensor", lambda t: t.matmul(p.ps[4][0:128, 0:64], onesf[:],
                                              lcrow[:], start=True, stop=True),
                 reads=[("lcrow",), ("onesf",)], writes=[("ps", 4)])
            q.do("vector", lambda v: v.tensor_sub(rrT[:], p.ps[4][0:128, 0:64], LT[:]),
                 reads=[("ps", 4), ("LT",)], writes=[("rrT",)])
            q.do("scalar", lambda sc: sc.activation(out=rrT[:], in_=rrT[:], func=AF.Exp),
                 reads=[("rrT",)], writes=[("rrT",)])
            q.do("scalar", lambda sc: sc.activation(out=gcb[:], in_=p.ps[4][0:128, 0:64],
                                                    func=AF.Exp),
                 reads=[("ps", 4)], writes=[("gcb",)])
            q.do("vector", lambda v: v.memset(stf[:], 0.0), writes=[("stf",)])
            q.do("vector", lambda v: v.memset(stb[:], 0.0), writes=[("stb",)])
            for c in range(NCH):
                cs = slice(c * CD, (c + 1) * CD)
                q.do("tensor", lambda t, cs=cs: t.matmul(
                    p.ps[4][0:128, 0:128], onesf[:], Lrow[0:1, cs], start=True, stop=True),
                    reads=[("Lrow",), ("onesf",)], writes=[("ps", 4)])
                q.do("scalar", lambda sc: sc.activation(out=gbc[:], in_=p.ps[4][0:128, 0:128],
                                                        func=AF.Exp),
                     reads=[("ps", 4)], writes=[("gbc",)])
                q.do("tensor", lambda t, cs=cs: t.matmul(
                    p.ps[5][0:128, 0:128], Lrow[0:1, cs], onesf[:], start=True, stop=True),
                    reads=[("Lrow",), ("onesf",)], writes=[("ps", 5)])
                q.do("vector", lambda v: v.tensor_copy(d2t[:], p.ps[5][0:128, 0:128]),
                     reads=[("ps", 5)], writes=[("d2t",)])
                q.do("vector", lambda v: v.tensor_sub(dtile[:], p.ps[4][0:128, 0:128],
                                                      d2t[:]),
                     reads=[("ps", 4), ("d2t",)], writes=[("dtile",)])
                q.do("vector", lambda v: v.tensor_scalar_min(dtile[:], dtile[:], 0.0),
                     reads=[("dtile",)], writes=[("dtile",)])
                q.do("scalar", lambda sc: sc.activation(out=g2[:], in_=dtile[:], func=AF.Exp),
                     reads=[("dtile",)], writes=[("g2",)])
                q.do("tensor", lambda t, cs=cs: t.matmul(
                    p.ps[0][0:128, 0:128], kh[:, cs], kh[:, cs], start=True, stop=True),
                    reads=[("kh",)], writes=[("ps", 0)])
                q.do("tensor", lambda t, cs=cs: t.matmul(
                    p.ps[1][0:128, 0:128], kh[:, cs], qh[:, cs], start=True, stop=True),
                    reads=[("kh",), ("qh",)], writes=[("ps", 1)])
                q.do("tensor", lambda t, cs=cs: t.matmul(
                    p.ps[2][0:128, 0:128], p.ones[0:1, 0:128], brow[0:1, cs],
                    start=True, stop=True),
                    reads=[("brow",)], writes=[("ps", 2)])
                q.do("vector", lambda v: v.tensor_mul(bm[:], p.maskUs[:], p.ps[2][0:128, 0:128]),
                     reads=[("ps", 2)], writes=[("bm",)])
                q.do("vector", lambda v: v.tensor_mul(tp1[:], p.ps[0][0:128, 0:128], g2[:]),
                     reads=[("ps", 0), ("g2",)], writes=[("tp1",)])
                q.do("vector", lambda v: v.tensor_mul(tpr[:], tp1[:], bm[:]),
                     reads=[("tp1",), ("bm",)], writes=[("tpr",)])
                q.do("vector", lambda v: v.tensor_mul(tp1[:], p.ps[1][0:128, 0:128], g2[:]),
                     reads=[("ps", 1), ("g2",)], writes=[("tp1",)])
                q.do("vector", lambda v: v.tensor_mul(aqk[:], tp1[:], p.maskUi[:]),
                     reads=[("tp1",)], writes=[("aqk",)])
                q.do("tensor", lambda t: t.transpose(p.psb[0][0:128, 0:128], tpr[:], p.ident[:]),
                     reads=[("tpr",)], writes=[("psb0",)])
                q.do("vector", lambda v: v.tensor_copy(tun[:], p.psb[0][0:128, 0:128]),
                     reads=[("psb0",)], writes=[("tun",)])
                pu, pp = [("tun",)], [("tpr",)]
                for i in range(6):
                    lhs_p = tpr if i == 0 else powp[i - 1]
                    lhs_u = tun if i == 0 else powu[i - 1]
                    rb_p = pp[-1]
                    rb_u = pu[-1]
                    q.do("tensor", lambda t, lhs_p=lhs_p, lhs_u=lhs_u: t.matmul(
                        p.ps[3][0:128, 0:128], lhs_p[:], lhs_u[:], start=True, stop=True),
                        reads=[rb_p, rb_u], writes=[("ps", 3)])
                    q.do("vector", lambda v, i=i: v.tensor_copy(powu[i][:], p.ps[3][0:128, 0:128]),
                         reads=[("ps", 3)], writes=[("powu", i)])
                    pu.append(("powu", i))
                    if i < 5:
                        q.do("tensor", lambda t, lhs_p=lhs_p, lhs_u=lhs_u: t.matmul(
                            p.ps[3][0:128, 0:128], lhs_u[:], lhs_p[:], start=True, stop=True),
                            reads=[rb_p, rb_u], writes=[("ps", 3)])
                        q.do("vector", lambda v, i=i: v.tensor_copy(powp[i][:],
                                                                    p.ps[3][0:128, 0:128]),
                             reads=[("ps", 3)], writes=[("powp", i)])
                        pp.append(("powp", i))
                q.do("vector", lambda v: v.tensor_sub(rch[0][:], p.ident[:], tpr[:]),
                     reads=[("tpr",)], writes=[("rch", 0)])
                for i in range(6):
                    src_r = rch[i % 2]
                    dst_r = rch[(i + 1) % 2]

                    def rstep(t, src_r=src_r, i=i):
                        t.matmul(p.ps[3][0:128, 0:128], p.ident[:], src_r[:],
                                 start=True, stop=False)
                        return t.matmul(p.ps[3][0:128, 0:128], powu[i][:], src_r[:],
                                        start=False, stop=True)
                    q.do("tensor", rstep, reads=[("rch", i % 2), ("powu", i)],
                         writes=[("ps", 3)])
                    q.do("vector", lambda v, dst_r=dst_r: v.tensor_copy(dst_r[:],
                                                                        p.ps[3][0:128, 0:128]),
                         reads=[("ps", 3)], writes=[("rch", (i + 1) % 2)])
                invb = ("rch", 0)
                q.do("tensor", lambda t, cs=cs: t.matmul(
                    p.ps[3][0:128, 0:128], kh[:, cs], stb[:], start=True, stop=True),
                    reads=[("kh",), ("stb",)], writes=[("ps", 3)])
                q.do("tensor", lambda t, cs=cs: t.transpose(
                    p.psb[1][0:128, 0:128], vh[:, cs], p.ident[:]),
                    reads=[("vh",)], writes=[("psb1",)])
                q.do("vector", lambda v, c=c: v.tensor_scalar(
                    vtb[:], p.psb[1][0:128, 0:128], bT[:, c:c + 1], None, ALU.mult),
                    reads=[("psb1",), ("bT",)], writes=[("vtb",)])
                q.do("vector", lambda v, c=c: v.scalar_tensor_tensor(
                    btile[:], p.ps[3][0:128, 0:128], nbgT[:, c:c + 1], vtb[:],
                    ALU.mult, ALU.add),
                    reads=[("ps", 3), ("nbgT",), ("vtb",)], writes=[("btile",)])
                q.do("tensor", lambda t: t.matmul(
                    p.ps[5][0:128, 0:128], rch[0][:], btile[:], start=True, stop=True),
                    reads=[invb, ("btile",)], writes=[("ps", 5)])
                q.do("vector", lambda v: v.tensor_copy(ut[:], p.ps[5][0:128, 0:128]),
                     reads=[("ps", 5)], writes=[("ut",)])
                q.do("vector", lambda v, cs=cs: v.tensor_mul(qt2[:], qh[:, cs], gbc[:]),
                     reads=[("qh",), ("gbc",)], writes=[("qt2",)])

                def ogrp(t):
                    t.matmul(p.ps[0][0:128, 0:128], stb[:], qt2[:], start=True, stop=False)
                    return t.matmul(p.ps[0][0:128, 0:128], ut[:], aqk[:],
                                    start=False, stop=True)
                q.do("tensor", ogrp, reads=[("stb",), ("qt2",), ("ut",), ("aqk",)],
                     writes=[("ps", 0)])
                q.do("vector", lambda v, cs=cs: v.tensor_mul(
                    ogh[:, cs], p.ps[0][0:128, 0:128], gh[:, cs]),
                    reads=[("ps", 0), ("gh",)], writes=[("ogh",)])
                q.do("tensor", lambda t, cs=cs: t.transpose(
                    p.psb[1][0:128, 0:128], kh[:, cs], p.ident[:]),
                    reads=[("kh",)], writes=[("psb1",)])
                q.do("vector", lambda v, c=c: v.tensor_scalar(
                    ktr[:], p.psb[1][0:128, 0:128], rrT[:, c:c + 1], None, ALU.mult),
                    reads=[("psb1",), ("rrT",)], writes=[("ktr",)])
                q.do("tensor", lambda t: t.matmul(
                    p.ps[1][0:128, 0:128], ktr[:], ut[:], start=True, stop=True),
                    reads=[("ktr",), ("ut",)], writes=[("ps", 1)])
                q.do("vector", lambda v, c=c: v.scalar_tensor_tensor(
                    stf[:], stf[:], gcb[:, c:c + 1], p.ps[1][0:128, 0:128],
                    ALU.mult, ALU.add),
                    reads=[("stf",), ("gcb",), ("ps", 1)], writes=[("stf",)])
                q.do("vector", lambda v: v.tensor_copy(stb[:], stf[:]),
                     reads=[("stf",)], writes=[("stb",)])
            q.dma("gpsimd", p.ag_og_in[h * 128:(h + 1) * 128, :], ogh[:],
                  ("og_sp", h), [("ogh",)])
        q.do("gpsimd", lambda g: g.collective_compute(
            "AllGather", ALU.bypass, replica_groups=[[0, 1, 2, 3], [4, 5, 6, 7]],
            ins=[p.ag_og_in], outs=[p.ag_og_out]),
            reads=[("og_sp", 0), ("og_sp", 1), ("og_sp", 2), ("og_sp", 3)],
            writes=[("og_out",)])
        q.finish()


def ph4(p):
    nc = p.nc
    ISR = 1.0 / math.sqrt(128.0)
    with ExitStack() as ctx:
        sbt = lambda n, s, d: ctx.enter_context(nc.sbuf_tensor(n, s, d))
        qf = sbt("f_qf", [128, T], BF16)
        kf = sbt("f_kf", [128, T], BF16)
        vt_sb = sbt("f_vt", [128, 64 * 132], BF16)
        probs = [sbt(f"f_pr{i}", [128, 512], BF16) for i in range(3)]
        macc = sbt("f_macc", [16, T], F32)
        bnd = sbt("f_bnd", [1, 8], F32)
        bmh = sbt("f_bmh", [128, 1], F32)
        rl = sbt("f_rl", [1, 512], F32)
        rlb = sbt("f_rlb", [1, 512], BF16)
        rlbc = sbt("f_rlbc", [16, 512], F32)
        mtmp = sbt("f_mtmp", [16, 512], F32)
        onesf = sbt("f_onesf", [1, 128], F32)
        q = AutoQ(p)
        q.do("vector", lambda v: v.memset(onesf[:], 1.0), writes=[("onesf",)])
        q.do("vector", lambda v: v.memset(macc[:], 0.0), writes=[("macc",)])
        q.dma("sync", vt_sb[:].rearrange("q (kc c) -> q kc c", c=132),
              p.sc_vt.rearrange("(kc q) c -> q kc c", q=128), ("vt",), None)
        q.dma("sync", bnd[:], p.sc_bnd, ("bnd",), None)
        q.do("vector", lambda v: v.tensor_mul(bnd[0:1, 4:8], bnd[0:1, 0:4], bnd[0:1, 4:8]),
             reads=[("bnd",)], writes=[("bnd",)])
        q.do("vector", lambda v: v.tensor_scalar_mul(bnd[0:1, 4:8], bnd[0:1, 4:8], 1.0 / 128.0),
             reads=[("bnd",)], writes=[("bnd",)])
        q.do("scalar", lambda sc: sc.activation(out=bnd[0:1, 4:8], in_=bnd[0:1, 4:8],
                                                func=AF.Sqrt),
             reads=[("bnd",)], writes=[("bnd",)])
        q.do("vector", lambda v: v.tensor_scalar_mul(bnd[0:1, 4:8], bnd[0:1, 4:8], -1.0),
             reads=[("bnd",)], writes=[("bnd",)])
        for h in range(NHL):
            if h < 3:
                ph7_record(p, q, ctx, h)
            q.dma("sync", qf[:], p.sc_qf[h * 128:(h + 1) * 128, :], ("qf",), None)
            q.dma("sync", kf[:], p.sc_kf[h * 128:(h + 1) * 128, :], ("kf",), None)
            q.do("tensor", lambda t, h=h: t.matmul(
                p.ps[3][0:128, 0:1], p.onesf32[:], bnd[0:1, 4 + h:5 + h],
                start=True, stop=True),
                reads=[("bnd",)], writes=[("ps", 3)])
            q.do("vector", lambda v: v.tensor_copy(bmh[:], p.ps[3][0:128, 0:1]),
                 reads=[("ps", 3)], writes=[("bmh",)])
            for qt in range(NTT):
                nkc = 4 * qt + 4

                def pvstep(kc, h=h, nkc=nkc):
                    pb3 = kc % 3
                    q.do("tensor", lambda t, kc=kc, pb3=pb3, s=(kc == 0), e=(kc == nkc - 1): t.matmul(
                        p.ps[2][0:33, 0:512],
                        vt_sb[:, kc * 132 + 33 * h:kc * 132 + 33 * h + 33],
                        probs[pb3][:], start=s, stop=e),
                        reads=[("probs", pb3), ("vt",)], writes=[("pspv",)])
                for kc in range(nkc):
                    pb = kc % 2
                    pb3 = kc % 3
                    q.do("tensor", lambda t, kc=kc, qt=qt, pb=pb: t.matmul(
                        p.ps[pb][0:128, 0:512], kf[:, kc * 128:(kc + 1) * 128],
                        qf[:, qt * 512:(qt + 1) * 512], start=True, stop=True),
                        reads=[("kf",), ("qf",)], writes=[("ps", pb)])
                    q.do("scalar", lambda sc, pb=pb, pb3=pb3: sc.activation(
                        out=probs[pb3][:], in_=p.ps[pb][0:128, 0:512], func=AF.Exp,
                        bias=bmh[:], scale=ISR),
                        reads=[("ps", pb), ("bmh",)], writes=[("probs", pb3)])
                    if kc >= 4 * qt:
                        a = kc - 4 * qt
                        q.do("vector", lambda v, pb3=pb3, a=a: v.tensor_mul(
                            probs[pb3][:], probs[pb3][:], p.diagm[:, a * 512:(a + 1) * 512]),
                            reads=[("probs", pb3)], writes=[("probs", pb3)])
                    if kc >= 1:
                        pvstep(kc - 1)
                pvstep(nkc - 1)
                q.do("vector", lambda v: v.reciprocal(rl[:], p.ps[2][32:33, 0:512]),
                     reads=[("pspv",)], writes=[("rl",)])
                q.do("vector", lambda v: v.tensor_copy(rlb[:], rl[:]),
                     reads=[("rl",)], writes=[("rlb",)])
                q.do("tensor", lambda t: t.matmul(
                    p.ps[3][0:16, 0:512], p.ones[0:1, 0:16], rlb[:], start=True, stop=True),
                    reads=[("rlb",)], writes=[("ps", 3)])
                q.do("vector", lambda v: v.tensor_copy(rlbc[:], p.ps[3][0:16, 0:512]),
                     reads=[("ps", 3)], writes=[("rlbc",)])
                q.do("vector", lambda v: v.tensor_mul(
                    mtmp[:], p.ps[2][0:16, 0:512], rlbc[:]),
                    reads=[("pspv",), ("rlbc",)], writes=[("mtmp",)])
                q.do("vector", lambda v, qt=qt: v.tensor_add(
                    macc[:, qt * 512:(qt + 1) * 512], macc[:, qt * 512:(qt + 1) * 512],
                    mtmp[:]),
                    reads=[("mtmp",), ("macc",)], writes=[("macc",)])
        q.dma("gpsimd", p.ar_m_in, macc[:], ("arin",), [("macc",)])
        q.do("gpsimd", lambda g: g.collective_compute(
            "AllReduce", ALU.add, replica_groups=[[0, 1, 2, 3], [4, 5, 6, 7]],
            ins=[p.ar_m_in], outs=[p.ar_m_out]),
            reads=[("arin",)], writes=[("arout",)])
        q.finish()


def ph5(p):
    nc = p.nc
    with ExitStack() as ctx0:
        Bsb = ctx0.enter_context(nc.sbuf_tensor("g_B", [16, FF], BF16))
        qz = AutoQ(p)
        qz.dma("gpsimd", Bsb[:], p.B_, ("Bsb",), None)
        qz.finish()
        nc.all_engine_barrier()
        for u in range(4):
            with ExitStack() as s1:
                sbt = lambda n, s, d: s1.enter_context(nc.sbuf_tensor(n, s, d))
                xq = sbt(f"g_xq{u}", [128, NDC * 512], F32)
                gsb = sbt(f"g_g{u}", [128, NDC * 512], BF16)
                prod = sbt(f"g_prod{u}", [128, 64 * 512], BF16)
                bcs = sbt(f"g_bcs{u}", [128, 2 * 512], BF16)
                srow = sbt(f"g_srow{u}", [1, 2 * 512], BF16)
                modt = sbt(f"g_modt{u}", [16, 512], F32)
                mh = sbt(f"g_mh{u}", [16, 512], BF16)
                tmp = sbt(f"g_tmp{u}", [128, 512], F32)
                tmp2 = sbt(f"g_tmp2{u}", [128, 512], F32)
                qa = AutoQ(p)
                qa.dma("sync", xq[:].rearrange("q (dc w) -> q dc w", w=512),
                       p.xq4T.rearrange("(dc q) w -> q dc w", q=128)[:, :, u * 512:(u + 1) * 512],
                       ("xq",), None)
                qa.dma("gpsimd", srow[:], dyn_off(p.sc_stb[0:2, 0:512], p.qcol[u]),
                       ("srow",), None)
                for j in range(2):
                    qa.do("tensor", lambda t, j=j: t.matmul(
                        p.ps[2 + j][0:128, 0:512], p.ones[0:1, 0:128],
                        srow[0:1, j * 512:(j + 1) * 512], start=True, stop=True),
                        reads=[("srow",)], writes=[("ps", 2 + j)])
                    qa.do("vector", lambda v, j=j: v.tensor_copy(
                        bcs[:, j * 512:(j + 1) * 512], p.ps[2 + j][0:128, 0:512]),
                        reads=[("ps", 2 + j)], writes=[("bcs",)])
                qa.dma("gpsimd", modt[:], dyn_off(p.ar_m_out[:, 0:512], p.qcol[u]),
                       ("modt",), None)
                qa.do("scalar", lambda sc: sc.activation(out=modt[:], in_=modt[:], func=AF.Tanh),
                      reads=[("modt",)], writes=[("modt",)])
                qa.do("vector", lambda v: v.tensor_scalar_mul(modt[:], modt[:], 0.1),
                      reads=[("modt",)], writes=[("modt",)])
                qa.finish()
                nc.all_engine_barrier()
                with ExitStack() as s2:
                    sb2 = lambda n, s, d: s2.enter_context(nc.sbuf_tensor(n, s, d))
                    hb = sb2(f"g_hb{u}", [128, NDC * 512], BF16)
                    win = [sb2(f"g_win{u}_{i}", [128, NDC * 128], BF16) for i in range(3)]
                    wup = [sb2(f"g_wup{u}_{i}", [128, NDC * 128], BF16) for i in range(3)]
                    gt = [sb2(f"g_gt{u}_{i}", [128, 512], BF16) for i in range(2)]
                    qc = AutoQ(p)
                    qc.dma("gpsimd", hb[:].rearrange("q (dc w) -> q dc w", w=512),
                           dyn_off(p.sc_xr.rearrange("(dc q) w -> q dc w", q=128)
                                   [:, :, 0:512], p.qcol[u]), ("hbx",), None)
                    for dc in range(NDC):
                        qc.do("vector", lambda v, dc=dc: v.tensor_scalar(
                            hb[:, dc * 512:(dc + 1) * 512], hb[:, dc * 512:(dc + 1) * 512],
                            p.v_ln2[:, dc:dc + 1], None, ALU.mult),
                            reads=[("hbx",)], writes=[("hbx",)])
                    def hagrp(t):
                        for dc in range(NDC):
                            ins = t.matmul(p.ps[4][0:16, 0:512],
                                           p.w_a2[:, dc * 16:(dc + 1) * 16],
                                           hb[:, dc * 512:(dc + 1) * 512],
                                           start=(dc == 0), stop=(dc == NDC - 1))
                        return ins
                    qc.do("tensor", hagrp, reads=[("hbx",)], writes=[("ps", 4)])
                    qc.do("vector", lambda v: v.tensor_mul(mh[:], modt[:], p.ps[4][0:16, 0:512]),
                          reads=[("ps", 4), ("modt",)], writes=[("mh",)])
                    for fc in range(64):
                        wb = fc % 3
                        gb = fc % 2
                        qc.dma("gpsimd", win[wb][:].rearrange("q (dc w) -> q dc w", w=128),
                               p.winT.rearrange("(dc q) w -> q dc w", q=128)
                               [:, :, fc * 128:(fc + 1) * 128], ("win", wb), None)
                        qc.dma("gpsimd", wup[wb][:].rearrange("q (dc w) -> q dc w", w=128),
                               p.wupT.rearrange("(dc q) w -> q dc w", q=128)
                               [:, :, fc * 128:(fc + 1) * 128], ("wup", wb), None)

                        def gategrp(t, wb=wb, fc=fc, gb=gb):
                            for dc in range(NDC):
                                t.matmul(p.ps[2 * gb][0:128, 0:512],
                                         win[wb][:, dc * 128:(dc + 1) * 128],
                                         hb[:, dc * 512:(dc + 1) * 512],
                                         start=(dc == 0), stop=False)
                            return t.matmul(p.ps[2 * gb][0:128, 0:512],
                                            Bsb[:, fc * 128:(fc + 1) * 128], mh[:],
                                            start=False, stop=True)
                        qc.do("tensor", gategrp,
                              reads=[("win", wb), ("hbx",), ("mh",), ("Bsb",)],
                              writes=[("ps", 2 * gb)])

                        def upgrp(t, wb=wb, gb=gb):
                            for dc in range(NDC):
                                ins = t.matmul(p.ps[2 * gb + 1][0:128, 0:512],
                                               wup[wb][:, dc * 128:(dc + 1) * 128],
                                               hb[:, dc * 512:(dc + 1) * 512],
                                               start=(dc == 0), stop=(dc == NDC - 1))
                            return ins
                        qc.do("tensor", upgrp, reads=[("wup", wb), ("hbx",)],
                              writes=[("ps", 2 * gb + 1)])
                        qc.do("scalar", lambda sc, gb=gb: sc.activation(
                            out=gt[gb][:], in_=p.ps[2 * gb][0:128, 0:512], func=AF.Silu),
                            reads=[("ps", 2 * gb)], writes=[("gt", gb)])
                        qc.do("vector", lambda v, fc=fc, gb=gb: v.tensor_mul(
                            prod[:, fc * 512:(fc + 1) * 512], gt[gb][:],
                            p.ps[2 * gb + 1][0:128, 0:512]),
                            reads=[("gt", gb), ("ps", 2 * gb + 1)], writes=[("prod",)])
                    qc.finish()
                    nc.all_engine_barrier()
                with ExitStack() as s3:
                    sb3 = lambda n, s, d: s3.enter_context(nc.sbuf_tensor(n, s, d))
                    wdn = [sb3(f"g_wdn{u}_{i}", [128, 64 * 128], BF16) for i in range(3)]
                    tmpd = [sb3(f"g_tmpd{u}_{i}", [128, 512], F32) for i in range(2)]
                    qd = AutoQ(p)
                    for dc in range(NDC):
                        wb = dc % 3
                        db = dc % 2
                        qd.dma("gpsimd", wdn[wb][:].rearrange("q (fc w) -> q fc w", w=128),
                               p.wdnT.rearrange("(fc q) w -> q fc w", q=128)
                               [:, :, dc * 128:(dc + 1) * 128], ("wdn", wb), None)

                        def dgrp(t, wb=wb, db=db):
                            for fc in range(64):
                                ins = t.matmul(p.ps[2 + db][0:128, 0:512],
                                               wdn[wb][:, fc * 128:(fc + 1) * 128],
                                               prod[:, fc * 512:(fc + 1) * 512],
                                               start=(fc == 0), stop=(fc == 63))
                            return ins
                        qd.do("tensor", dgrp, reads=[("wdn", wb), ("prod",)],
                              writes=[("ps", 2 + db)])
                        qd.do("vector", lambda v, dc=dc, db=db: v.tensor_mul(
                            tmpd[db][:], xq[:, dc * 512:(dc + 1) * 512], bcs[:, 0:512]),
                            reads=[("xq",), ("bcs",)], writes=[("tmpd", db)])
                        qd.do("vector", lambda v, db=db: v.tensor_add(
                            tmpd[db][:], tmpd[db][:], bcs[:, 512:1024]),
                            reads=[("tmpd", db), ("bcs",)], writes=[("tmpd", db)])
                        qd.do("vector", lambda v, db=db: v.tensor_add(
                            tmpd[db][:], tmpd[db][:], p.ps[2 + db][0:128, 0:512]),
                            reads=[("tmpd", db), ("ps", 2 + db)], writes=[("tmpd", db)])
                        qd.do("scalar", lambda sc, dc=dc, db=db: sc.activation(
                            out=gsb[:, dc * 512:(dc + 1) * 512], in_=tmpd[db][:], func=AF.Silu),
                            reads=[("tmpd", db)], writes=[("gsb",)])
                    qd.finish()
                    nc.all_engine_barrier()
                with ExitStack() as s4:
                    sb4 = lambda n, s, d: s4.enter_context(nc.sbuf_tensor(n, s, d))
                    ogq = sb4(f"g_ogq{u}", [128, NDC * 512], BF16)
                    wo = [sb4(f"g_wo{u}_{i}", [128, NDC * 128], BF16) for i in range(3)]
                    x1t = [sb4(f"g_x1t{u}_{i}", [128, 512], F32) for i in range(2)]
                    tmf = [sb4(f"g_tmf{u}_{i}", [128, 512], F32) for i in range(2)]
                    qf = AutoQ(p)
                    qf.dma("gpsimd", ogq[:].rearrange("q (dc w) -> q dc w", w=512),
                           dyn_off(p.ag_og_out.rearrange("(dc q) w -> q dc w", q=128)
                                   [:, :, 0:512], p.qcol[u]), ("ogq",), None)
                    for dc in range(NDC):
                        wb = dc % 3
                        db = dc % 2
                        qf.dma("gpsimd", wo[wb][:].rearrange("q (d2 w) -> q d2 w", w=128),
                               p.woT.rearrange("(d2 q) w -> q d2 w", q=128)
                               [:, :, dc * 128:(dc + 1) * 128], ("wo", wb), None)

                        def ygrp(t, wb=wb, db=db):
                            for d2 in range(NDC):
                                ins = t.matmul(p.ps[5 * db][0:128, 0:512],
                                               wo[wb][:, d2 * 128:(d2 + 1) * 128],
                                               ogq[:, d2 * 512:(d2 + 1) * 512],
                                               start=(d2 == 0), stop=(d2 == NDC - 1))
                            return ins
                        qf.do("tensor", ygrp, reads=[("wo", wb), ("ogq",)],
                              writes=[("ps", 5 * db)])
                        qf.do("vector", lambda v, dc=dc, db=db: v.tensor_mul(
                            tmf[db][:], p.ps[5 * db][0:128, 0:512],
                            gsb[:, dc * 512:(dc + 1) * 512]),
                            reads=[("ps", 5 * db), ("gsb",)], writes=[("tmf", db)])
                        qf.do("vector", lambda v, dc=dc, db=db: v.tensor_add(
                            x1t[db][:], tmf[db][:], xq[:, dc * 512:(dc + 1) * 512]),
                            reads=[("tmf", db), ("xq",)], writes=[("x1t", db)])
                        qf.dma("gpsimd",
                               p.sc_x1[dc * 128:(dc + 1) * 128, u * 512:(u + 1) * 512],
                               x1t[db][:], ("x1sp", db), [("x1t", db)])
                    qf.finish()
                    nc.all_engine_barrier()


def ph6(p):
    nc = p.nc
    p.xn1sb = p.ctx.enter_context(nc.sbuf_tensor("xn1sb", [128, NDC * TQ], BF16))
    p.halocol = p.ctx.enter_context(nc.sbuf_tensor("halocol", [128, NDC], BF16))
    with ExitStack() as ctx:
        sbt = lambda n, s, d: ctx.enter_context(nc.sbuf_tensor(n, s, d))
        xf = [sbt(f"h_xf{i}", [128, 512], F32) for i in range(2)]
        xb = [sbt(f"h_xb{i}", [128, 512], BF16) for i in range(2)]
        xsq = sbt("h_xsq", [128, 512], BF16)
        sx = sbt("h_sx", [33, TQ], F32)
        st = sbt("h_st", [128, 16 * 14], F32)
        srow = sbt("h_srow", [1, 2 * 512], BF16)
        bc = sbt("h_bc", [128, 2 * 512], BF16)
        tmp = sbt("h_tmp", [128, 512], BF16)
        halo = sbt("h_halo", [1, D], BF16)
        q = AutoQ(p)
        NIT = 4 * NDC
        for k in range(NIT):
            tt, dc = divmod(k, NDC)
            q.dma("sync", xf[k % 2][:],
                  p.sc_x1[dc * 128:(dc + 1) * 128, tt * 512:(tt + 1) * 512],
                  ("xf", k % 2), None)
            q.do("vector", lambda v, k=k: v.tensor_copy(xb[k % 2][:], xf[k % 2][:]),
                 reads=[("xf", k % 2)], writes=[("xb", k % 2)])
            q.do("vector", lambda v, k=k: v.tensor_mul(xsq[:], xb[k % 2][:], xb[k % 2][:]),
                 reads=[("xb", k % 2)], writes=[("xsq",)])

            def stats2(t, k=k, tt=tt, dc=dc):
                t.matmul(p.ps[2 + tt % 2][0:1, 0:512], p.ones[:, 0:1], xb[k % 2][:],
                         start=(dc == 0), stop=(dc == NDC - 1))
                return t.matmul(p.ps[4 + tt % 2][0:1, 0:512],
                                p.ones[:, 0:1], xsq[:],
                                start=(dc == 0), stop=(dc == NDC - 1))
            q.do("tensor", stats2, reads=[("xb", k % 2), ("xsq",)],
                 writes=[("pssa", tt % 2), ("pssb", tt % 2)])
            if dc == NDC - 1:
                q.do("scalar", lambda sc, tt=tt: sc.copy(
                    out=sx[0:1, tt * 512:(tt + 1) * 512],
                    in_=p.ps[2 + tt % 2][0:1, 0:512]),
                    reads=[("pssa", tt % 2)], writes=[("sx0",)])
                q.do("scalar", lambda sc, tt=tt: sc.copy(
                    out=sx[32:33, tt * 512:(tt + 1) * 512],
                    in_=p.ps[4 + tt % 2][0:1, 0:512]),
                    reads=[("pssb", tt % 2)], writes=[("sx1",)])
        q.dma("gpsimd", p.sc_st[2:3, 0:TQ], sx[0:1, :], ("st2",), [("sx0",)])
        q.dma("gpsimd", p.sc_st[3:4, 0:TQ], sx[32:33, :], ("st3",), [("sx1",)])
        c = lambda i: st[:, 16 * i:16 * (i + 1)]
        q.dma("sync", c(0), p.sc_st[2:3, 0:TQ].rearrange("o (q w) -> o q w", q=128)[0],
              ("c0",), [("st2",)])
        q.dma("sync", c(1), p.sc_st[3:4, 0:TQ].rearrange("o (q w) -> o q w", q=128)[0],
              ("c1",), [("st3",)])
        q.do("vector", lambda v: v.tensor_scalar_mul(c(2), c(0), 1.0 / D),
             reads=[("c0",)], writes=[("mu",)])
        q.do("vector", lambda v: v.tensor_scalar_mul(c(3), c(1), 1.0 / D),
             reads=[("c1",)], writes=[("m2",)])
        q.do("vector", lambda v: v.tensor_mul(c(5), c(2), c(2)),
             reads=[("mu",)], writes=[("t5",)])
        q.do("vector", lambda v: v.tensor_sub(c(4), c(3), c(5)),
             reads=[("m2",), ("t5",)], writes=[("var",)])
        q.do("vector", lambda v: v.tensor_scalar_max(c(4), c(4), 0.0),
             reads=[("var",)], writes=[("var",)])
        q.do("scalar", lambda sc: sc.activation(out=c(5), in_=c(4), func=AF.Sqrt,
                                                bias=p.cst(1e-5)),
             reads=[("var",)], writes=[("t5",)])
        q.do("vector", lambda v: v.reciprocal(c(6), c(5)),
             reads=[("t5",)], writes=[("a1",)])
        q.do("vector", lambda v: v.tensor_mul(c(7), c(2), c(6)),
             reads=[("mu",), ("a1",)], writes=[("b1",)])
        q.do("vector", lambda v: v.tensor_scalar_mul(c(7), c(7), -1.0),
             reads=[("b1",)], writes=[("b1",)])
        q.dma("gpsimd", p.sc_st[6:7, 0:TQ].rearrange("o (q w) -> o q w", q=128)[0], c(6),
              ("st6",), [("a1",)])
        q.dma("gpsimd", p.sc_st[7:8, 0:TQ].rearrange("o (q w) -> o q w", q=128)[0], c(7),
              ("st7",), [("b1",)])
        q.dma("gpsimd", p.sc_stb1[0:1, :], p.sc_st[6:7, 0:TQ], ("stb10",), [("st6",)])
        q.dma("gpsimd", p.sc_stb1[1:2, :], p.sc_st[7:8, 0:TQ], ("stb11",), [("st7",)])
        for k in range(NIT):
            tt, dc = divmod(k, NDC)
            if dc == 0:
                q.dma("gpsimd", srow[:], p.sc_stb1[:, tt * 512:(tt + 1) * 512],
                      ("srow",), [("stb10",), ("stb11",)])
                for j in range(2):
                    q.do("tensor", lambda t, j=j: t.matmul(
                        p.ps[j][0:128, 0:512], p.ones[0:1, 0:128],
                        srow[0:1, j * 512:(j + 1) * 512], start=True, stop=True),
                        reads=[("srow",)], writes=[("ps", j)])
                    q.do("vector", lambda v, j=j: v.tensor_copy(
                        bc[:, j * 512:(j + 1) * 512], p.ps[j][0:128, 0:512]),
                        reads=[("ps", j)], writes=[("bc",)])
            q.dma("sync", xf[k % 2][:],
                  p.sc_x1[dc * 128:(dc + 1) * 128, tt * 512:(tt + 1) * 512],
                  ("xf", k % 2), None)
            q.do("vector", lambda v, k=k: v.tensor_copy(xb[k % 2][:], xf[k % 2][:]),
                 reads=[("xf", k % 2)], writes=[("xb", k % 2)])
            q.do("vector", lambda v, k=k: v.tensor_mul(tmp[:], xb[k % 2][:], bc[:, 0:512]),
                 reads=[("xb", k % 2), ("bc",)], writes=[("tmpx",)])
            q.do("vector", lambda v, tt=tt, dc=dc: v.tensor_add(
                p.xn1sb[:, dc * TQ + tt * 512:dc * TQ + (tt + 1) * 512], tmp[:],
                bc[:, 512:1024]),
                reads=[("tmpx",), ("bc",)], writes=[("xn1",)])
        for dc in range(NDC):
            q.dma("gpsimd", p.ag_h_in[0:1, dc * 128:(dc + 1) * 128],
                  p.xn1sb[:, dc * TQ + TQ - 1:dc * TQ + TQ], ("haloin",), [("xn1",)])
        q.do("gpsimd", lambda g: g.collective_compute(
            "AllGather", ALU.bypass, replica_groups=[[0, 1, 2, 3], [4, 5, 6, 7]],
            ins=[p.ag_h_in], outs=[p.ag_h_out]),
            reads=[("haloin",)], writes=[("haloout",)])
        for j in range(4):
            q.dma("gpsimd", p.ag_h2[0:1, j * 512:(j + 1) * 512], p.c_zero[0:1, 0:512],
                  ("h2z", j), None)
        q.dma("gpsimd", p.ag_h2[1:5, :], p.ag_h_out, ("h2c",), [("haloout",)])
        q.dma("gpsimd", halo[:], dyn_off(p.ag_h2[0:1, :], p.qrow), ("halosb",),
              [("h2c",)] + [("h2z", j) for j in range(4)])
        for dc in range(NDC):
            q.dma("gpsimd", p.halocol[:, dc:dc + 1], halo[0:1, dc * 128:(dc + 1) * 128],
                  ("halocol",), [("halosb",)])
        q.finish()


def ph7(p):
    pass


def ph7_record(p, q, ctx, part):
    nc = p.nc
    if True:
        sbt = lambda n, s, d: ctx.enter_context(nc.sbuf_tensor(n, s, d))
        wt = [sbt(f"i_wt_{part}_{i}", [128, 512], F32) for i in range(2)]
        ab = sbt(f"i_ab_{part}", [128, 512], F32)
        acc = sbt(f"i_acc_{part}", [128, 176 * 2], F32)
        accr = sbt(f"i_accr_{part}", [128, 1], F32)
        tot = sbt(f"i_tot_{part}", [1, 4], F32)
        thr = sbt(f"i_thr_{part}", [128, 1], F32)
        msk = sbt(f"i_msk_{part}", [128, 512], BF16)
        sgn = sbt(f"i_sgn_{part}", [128, 512], BF16)
        qt = sbt(f"i_qt_{part}", [128, 512], BF16)
        onesfc = sbt(f"i_onesf_{part}", [128, 1], F32)
        q.do("vector", lambda v: v.memset(onesfc[:], 1.0), writes=[("onesfc",)])
        for wi, (wsrc, wdst, rows, cols) in [(part, [
                (p.w1T, p.sc_w1q, D, FB), (p.w2T, p.sc_w2q, D, FB),
                (p.w3T, p.sc_w3q, FB, D)][part])]:
            nrc = rows // 128
            ncj = cols // 512
            nt = nrc * ncj
            for ti in range(nt):
                rc, cj = divmod(ti, ncj)
                q.dma("sync", wt[ti % 2][:],
                      wsrc[rc * 128:(rc + 1) * 128, cj * 512:(cj + 1) * 512],
                      ("wt", ti % 2), None)
                q.do("scalar", lambda sc, ti=ti: sc.activation(
                    out=ab[:], in_=wt[ti % 2][:], func=AF.Abs,
                    accum_out=acc[:, ti:ti + 1]),
                    reads=[("wt", ti % 2)], writes=[("ab",), ("acc",)])
            q.do("vector", lambda v, nt=nt: v.reduce_sum(
                accr[:], acc[:, 0:nt], axis=mybir.AxisListType.X),
                reads=[("acc",)], writes=[("accr",)])
            q.do("tensor", lambda t: t.matmul(p.ps[4][0:1, 0:1], onesfc[:], accr[:],
                                              start=True, stop=True),
                 reads=[("accr",), ("onesfc",)], writes=[("ps", 4)])
            q.do("vector", lambda v, wi=wi, rows=rows, cols=cols: v.tensor_scalar(
                tot[0:1, wi:wi + 1], p.ps[4][0:1, 0:1], 0.5 / (rows * cols), 5e-9,
                ALU.mult, ALU.add),
                reads=[("ps", 4)], writes=[("tot",)])
            q.do("tensor", lambda t, wi=wi: t.matmul(
                p.ps[5][0:128, 0:1], p.onesf32[:], tot[0:1, wi:wi + 1],
                start=True, stop=True),
                reads=[("tot",)], writes=[("ps", 5)])
            q.do("vector", lambda v: v.tensor_copy(thr[:], p.ps[5][0:128, 0:1]),
                 reads=[("ps", 5)], writes=[("thr",)])
            for ti in range(nt):
                rc, cj = divmod(ti, ncj)
                q.dma("sync", wt[ti % 2][:],
                      wsrc[rc * 128:(rc + 1) * 128, cj * 512:(cj + 1) * 512],
                      ("wt", ti % 2), None)
                q.do("scalar", lambda sc, ti=ti: sc.activation(
                    out=ab[:], in_=wt[ti % 2][:], func=AF.Abs),
                    reads=[("wt", ti % 2)], writes=[("ab",)])
                q.do("vector", lambda v: v.tensor_scalar(
                    msk[:], ab[:], thr[:], None, ALU.is_ge),
                    reads=[("ab",), ("thr",)], writes=[("msk",)])
                q.do("scalar", lambda sc, ti=ti: sc.activation(
                    out=sgn[:], in_=wt[ti % 2][:], func=AF.Sign),
                    reads=[("wt", ti % 2)], writes=[("sgn",)])
                q.do("vector", lambda v: v.tensor_mul(qt[:], msk[:], sgn[:]),
                     reads=[("msk",), ("sgn",)], writes=[("qt",)])
                q.dma("gpsimd", wdst[rc * 128:(rc + 1) * 128, cj * 512:(cj + 1) * 512],
                      qt[:], ("qsp", wi), [("qt",)])


def ph8(p):
    nc = p.nc
    with ExitStack() as ctx:
        sbt = lambda n, s, d: ctx.enter_context(nc.sbuf_tensor(n, s, d))
        sh = sbt("j_sh", [128, TQ], BF16)
        t1 = sbt("j_t1", [128, TQ], BF16)
        xkb = sbt("j_xkb", [128, TQ], BF16)
        q = AutoQ(p)
        for dc in range(NDC):
            q.do("vector", lambda v, dc=dc: v.tensor_copy(
                sh[:, 1:TQ], p.xn1sb[:, dc * TQ:dc * TQ + TQ - 1]),
                reads=[("xn1g",)], writes=[("sh",)])
            q.do("vector", lambda v, dc=dc: v.tensor_copy(
                sh[:, 0:1], p.halocol[:, dc:dc + 1]),
                reads=[("halocol",)], writes=[("sh",)])
            for (mv, mv1, dst, db) in [(p.v_mk, p.v_mk1, p.sc_xk, "xk"),
                                       (p.v_mv, p.v_mv1, p.sc_xv, "xv"),
                                       (p.v_mr, p.v_mr1, p.sc_xrm, "xr")]:
                q.do("vector", lambda v, mv1=mv1, dc=dc: v.tensor_scalar(
                    t1[:], sh[:], mv1[:, dc:dc + 1], None, ALU.mult),
                    reads=[("sh",)], writes=[("t1",)])
                q.do("vector", lambda v, mv=mv, dc=dc: v.scalar_tensor_tensor(
                    xkb[:], p.xn1sb[:, dc * TQ:(dc + 1) * TQ], mv[:, dc:dc + 1], t1[:],
                    ALU.mult, ALU.add),
                    reads=[("t1",), ("xn1g",)], writes=[("xkb",)])
                q.dma("gpsimd", dst[dc * 128:(dc + 1) * 128, :], xkb[:],
                      ("mx", db), [("xkb",)])
        q.finish()
    nc.all_engine_barrier()
    with ExitStack() as ctx:
        sbt = lambda n, s, d: ctx.enter_context(nc.sbuf_tensor(n, s, d))
        wk = [sbt(f"j_wk{i}", [128, NDC * 128], BF16) for i in range(2)]
        xsrc = sbt("j_xsrc", [128, NDC * 512], BF16)
        kv = sbt("j_kv", [128, 512], BF16)
        q = AutoQ(p)
        for (wT, xsc, dst, sig, tag) in [(p.wk1T, p.sc_xk, p.sc_k1, False, "k"),
                                         (p.wv1T, p.sc_xv, p.sc_v1, False, "v"),
                                         (p.wr1T, p.sc_xrm, p.sc_sr, True, "r")]:
            for tt in range(4):
                q.dma("gpsimd", xsrc[:].rearrange("q (dc w) -> q dc w", w=512),
                      xsc.rearrange("(dc q) w -> q dc w", q=128)[:, :, tt * 512:(tt + 1) * 512],
                      ("xsrc", tag), None)
                for oc in range(NDC):
                    wb = oc % 2
                    q.dma("gpsimd", wk[wb][:].rearrange("q (dc w) -> q dc w", w=128),
                          wT.rearrange("(dc q) w -> q dc w", q=128)
                          [:, :, oc * 128:(oc + 1) * 128], ("wk1", tag, wb), None)

                    def grp(t, wb=wb):
                        for dc in range(NDC):
                            ins = t.matmul(p.ps[wb][0:128, 0:512],
                                           wk[wb][:, dc * 128:(dc + 1) * 128],
                                           xsrc[:, dc * 512:(dc + 1) * 512],
                                           start=(dc == 0), stop=(dc == NDC - 1))
                        return ins
                    q.do("tensor", grp, reads=[("wk1", tag, wb), ("xsrc", tag)],
                         writes=[("ps", wb)])
                    if sig:
                        q.do("scalar", lambda sc, wb=wb: sc.activation(
                            out=kv[:], in_=p.ps[wb][0:128, 0:512], func=AF.Sigmoid),
                            reads=[("ps", wb)], writes=[("kv",)])
                    else:
                        q.do("vector", lambda v, wb=wb: v.tensor_copy(
                            kv[:], p.ps[wb][0:128, 0:512]),
                            reads=[("ps", wb)], writes=[("kv",)])
                    q.dma("gpsimd", dst[oc * 128:(oc + 1) * 128, tt * 512:(tt + 1) * 512],
                          kv[:], ("kvsp", tag), [("kv",)])
        q.finish()
    nc.all_engine_barrier()
    with ExitStack() as ctx:
        sbt = lambda n, s, d: ctx.enter_context(nc.sbuf_tensor(n, s, d))
        kb = [sbt(f"j_kb{i}", [128, TQ], BF16) for i in range(2)]
        vb = [sbt(f"j_vb{i}", [128, TQ], BF16) for i in range(2)]
        pk = sbt("j_pk", [128, TQ], F32)
        pkv = sbt("j_pkv", [128, TQ], F32)
        Nl = sbt("j_Nl", [128, TQ], F32)
        Dl = sbt("j_Dl", [128, TQ], F32)
        ewb = sbt("j_ewb", [128, TQ], F32)
        endsNl = sbt("j_eNl", [128, NDC], F32)
        endsDl = sbt("j_eDl", [128, NDC], F32)
        q = AutoQ(p)
        for dc in range(NDC):
            db = dc % 2
            if dc % 4 == 0:
                bitffn_record(p, q, ctx, dc // 4)
            q.dma("sync", kb[db][:], p.sc_k1[dc * 128:(dc + 1) * 128, :], ("kb", db), None)
            q.dma("sync", vb[db][:], p.sc_v1[dc * 128:(dc + 1) * 128, :], ("vb", db), None)
            q.do("scalar", lambda sc, db=db: sc.activation(out=pk[:], in_=kb[db][:], func=AF.Exp),
                 reads=[("kb", db)], writes=[("pk",)])
            q.do("vector", lambda v, db=db: v.tensor_mul(pkv[:], pk[:], vb[db][:]),
                 reads=[("pk",), ("vb", db)], writes=[("pkv",)])
            q.do("vector", lambda v: v.memset(ewb[:], 0.0),
                 reads=[("ewb",)], writes=[("ewb",)])
            q.do("scalar", lambda sc, dc=dc: sc.activation(
                out=ewb[:], in_=ewb[:], func=AF.Exp, bias=p.v_w[:, dc:dc + 1]),
                reads=[("ewb",)], writes=[("ewb",)])
            q.do("vector", lambda v: v.tensor_tensor_scan(
                Nl[:], ewb[:], pkv[:], 0.0, ALU.mult, ALU.add),
                reads=[("ewb",), ("pkv",)], writes=[("Nl",)])
            q.do("vector", lambda v: v.tensor_tensor_scan(
                Dl[:], ewb[:], pk[:], 0.0, ALU.mult, ALU.add),
                reads=[("ewb",), ("pk",)], writes=[("Dl",)])
            q.dma("gpsimd", p.sc_Nl[dc * 128:(dc + 1) * 128, :], Nl[:], ("Nlsp",), [("Nl",)])
            q.dma("gpsimd", p.sc_Dl[dc * 128:(dc + 1) * 128, :], Dl[:], ("Dlsp",), [("Dl",)])
            q.do("vector", lambda v, dc=dc: v.tensor_copy(endsNl[:, dc:dc + 1],
                                                          Nl[:, TQ - 1:TQ]),
                 reads=[("Nl",)], writes=[("eNl",)])
            q.do("vector", lambda v, dc=dc: v.tensor_copy(endsDl[:, dc:dc + 1],
                                                          Dl[:, TQ - 1:TQ]),
                 reads=[("Dl",)], writes=[("eDl",)])
        q.dma("gpsimd", p.ag_e_in[0:1, :], endsNl[:], ("ein0",), [("eNl",)])
        q.dma("gpsimd", p.ag_e_in[1:2, :], endsDl[:], ("ein1",), [("eDl",)])
        q.do("gpsimd", lambda g: g.collective_compute(
            "AllGather", ALU.bypass, replica_groups=[[0, 1, 2, 3], [4, 5, 6, 7]],
            ins=[p.ag_e_in], outs=[p.ag_e_out]),
            reads=[("ein0",), ("ein1",)], writes=[("eout",)])
        q.finish()
    nc.all_engine_barrier()
    with ExitStack() as ctx:
        sbt = lambda n, s, d: ctx.enter_context(nc.sbuf_tensor(n, s, d))
        kb = [sbt(f"j2_kb{i}", [128, TQ], BF16) for i in range(2)]
        vb = [sbt(f"j2_vb{i}", [128, TQ], BF16) for i in range(2)]
        srb = [sbt(f"j2_srb{i}", [128, TQ], BF16) for i in range(2)]
        pku = sbt("j2_pku", [128, TQ], F32)
        Nl = [sbt(f"j2_Nl{i}", [128, TQ], F32) for i in range(2)]
        Dl = [sbt(f"j2_Dl{i}", [128, TQ], F32) for i in range(2)]
        num = sbt("j2_num", [128, TQ], F32)
        den = sbt("j2_den", [128, TQ], F32)
        Em = sbt("j2_Em", [128, TQ], F32)
        corr = sbt("j2_corr", [128, TQ], F32)
        rwv = sbt("j2_rwv", [128, TQ], BF16)
        iot = sbt("j2_iot", [128, TQ], F32)
        endsN = sbt("j2_eN", [128, NDC], F32)
        endsD = sbt("j2_eD", [128, NDC], F32)
        wp = sbt("j2_wp", [128, NDC], F32)
        mcol = sbt("j2_mcol", [128, 1], F32)
        enN = sbt("j2_enN", [128, NDC], F32)
        enD = sbt("j2_enD", [128, NDC], F32)
        msc = sbt("j2_msc", [1, 4], F32)
        msci = sbt("j2_msci", [1, 1], mybir.dt.int32)
        q = AutoQ(p)
        q.dma("sync", iot[:], p.c_iota, ("iot",), None)
        q.do("vector", lambda v: v.memset(enN[:], 0.0), writes=[("enN",)])
        q.do("vector", lambda v: v.memset(enD[:], 0.0), writes=[("enD",)])
        q.do("vector", lambda v: v.memset(msc[:], 0.0), writes=[("msc",)])
        q.do("gpsimd", lambda g: g.store(msci[0:1, 0:1], p.qv),
             reads=[], writes=[("msci",)])
        q.do("vector", lambda v: v.tensor_copy(msc[0:1, 0:1], msci[0:1, 0:1]),
             reads=[("msci",), ("msc",)], writes=[("mscq",)])
        for p_ in range(4):
            q.do("vector", lambda v, p_=p_: v.tensor_scalar(
                msc[0:1, 1:2], msc[0:1, 0:1], float(-1 - p_), None, ALU.add),
                reads=[("mscq",)], writes=[("mscm", p_)])
            q.do("tensor", lambda t: t.matmul(p.ps[3][0:128, 0:1], p.onesf32[:],
                                              msc[0:1, 1:2], start=True, stop=True),
                 reads=[("mscm", p_)], writes=[("ps", 3)])
            q.do("vector", lambda v: v.tensor_copy(mcol[:], p.ps[3][0:128, 0:1]),
                 reads=[("ps", 3)], writes=[("mcol",)])
            q.do("vector", lambda v: v.tensor_scalar(
                wp[:], p.v_wtq[:], mcol[:], None, ALU.mult),
                reads=[("mcol",)], writes=[("wp",)])
            q.do("vector", lambda v: v.tensor_scalar_min(wp[:], wp[:], 0.0),
                 reads=[("wp",)], writes=[("wp",)])
            q.do("scalar", lambda sc: sc.activation(out=wp[:], in_=wp[:], func=AF.Exp),
                 reads=[("wp",)], writes=[("wp",)])
            q.do("vector", lambda v: v.tensor_scalar(
                mcol[:], mcol[:], 0.0, None, ALU.is_ge),
                reads=[("mcol",)], writes=[("mcol",)])
            q.do("vector", lambda v: v.tensor_scalar(wp[:], wp[:], mcol[:], None, ALU.mult),
                 reads=[("wp",), ("mcol",)], writes=[("wp",)])
            q.dma("sync", endsN[:], p.ag_e_out.rearrange("o (q dc) -> o q dc", q=128)[2 * p_],
                  ("endsN",), None)
            q.dma("sync", endsD[:],
                  p.ag_e_out.rearrange("o (q dc) -> o q dc", q=128)[2 * p_ + 1],
                  ("endsD",), None)
            q.do("vector", lambda v: v.scalar_tensor_tensor(
                enN[:], endsN[:], 1.0, enN[:], ALU.mult, ALU.add)
                if False else v.tensor_mul(endsN[:], endsN[:], wp[:]),
                reads=[("endsN",), ("wp",)], writes=[("endsN",)])
            q.do("vector", lambda v: v.tensor_add(enN[:], enN[:], endsN[:]),
                 reads=[("endsN",), ("enN",)], writes=[("enN",)])
            q.do("vector", lambda v: v.tensor_mul(endsD[:], endsD[:], wp[:]),
                 reads=[("endsD",), ("wp",)], writes=[("endsD",)])
            q.do("vector", lambda v: v.tensor_add(enD[:], enD[:], endsD[:]),
                 reads=[("endsD",), ("enD",)], writes=[("enD",)])
        for dc in range(NDC):
            db = dc % 2
            q.dma("sync", kb[db][:], p.sc_k1[dc * 128:(dc + 1) * 128, :], ("kb", db), None)
            q.dma("sync", vb[db][:], p.sc_v1[dc * 128:(dc + 1) * 128, :], ("vb", db), None)
            q.dma("sync", srb[db][:], p.sc_sr[dc * 128:(dc + 1) * 128, :], ("srb", db), None)
            q.dma("sync", Nl[db][:], p.sc_Nl[dc * 128:(dc + 1) * 128, :], ("Nlr", db), None)
            q.dma("sync", Dl[db][:], p.sc_Dl[dc * 128:(dc + 1) * 128, :], ("Dlr", db), None)
            q.do("scalar", lambda sc, dc=dc, db=db: sc.activation(
                out=pku[:], in_=kb[db][:], func=AF.Exp, bias=p.v_tf[:, dc:dc + 1]),
                reads=[("kb", db)], writes=[("pku",)])
            q.do("scalar", lambda sc, dc=dc: sc.activation(
                out=Em[:], in_=iot[:], func=AF.Exp, scale=p.v_w[:, dc:dc + 1]),
                reads=[("iot",)], writes=[("Em",)])
            q.do("vector", lambda v, db=db: v.tensor_mul(num[:], pku[:], vb[db][:]),
                 reads=[("pku",), ("vb", db)], writes=[("num",)])
            q.do("vector", lambda v, db=db: v.tensor_add(num[:, 1:TQ], num[:, 1:TQ],
                                                         Nl[db][:, 0:TQ - 1]),
                 reads=[("num",), ("Nlr", db)], writes=[("num",)])
            q.do("vector", lambda v, dc=dc: v.tensor_scalar(
                corr[:], Em[:], enN[:, dc:dc + 1], None, ALU.mult),
                reads=[("Em",), ("enN",)], writes=[("corr",)])
            q.do("vector", lambda v: v.tensor_add(num[:, 1:TQ], num[:, 1:TQ],
                                                  corr[:, 0:TQ - 1]),
                 reads=[("corr",), ("num",)], writes=[("num",)])
            q.do("vector", lambda v, dc=dc: v.tensor_scalar(
                num[:, 0:1], enN[:, dc:dc + 1], 1.0, None, ALU.mult)
                if False else v.tensor_add(num[:, 0:1], num[:, 0:1],
                                           enN[:, dc:dc + 1]),
                reads=[("enN",), ("num",)], writes=[("num",)])
            q.do("vector", lambda v: v.tensor_copy(den[:], pku[:]),
                 reads=[("pku",)], writes=[("den",)])
            q.do("vector", lambda v, db=db: v.tensor_add(den[:, 1:TQ], den[:, 1:TQ],
                                                          Dl[db][:, 0:TQ - 1]),
                 reads=[("den",), ("Dlr", db)], writes=[("den",)])
            q.do("vector", lambda v, dc=dc: v.tensor_scalar(
                corr[:], Em[:], enD[:, dc:dc + 1], None, ALU.mult),
                reads=[("Em",), ("enD",)], writes=[("corr",)])
            q.do("vector", lambda v: v.tensor_add(den[:, 1:TQ], den[:, 1:TQ],
                                                  corr[:, 0:TQ - 1]),
                 reads=[("corr",), ("den",)], writes=[("den",)])
            q.do("vector", lambda v, dc=dc: v.tensor_add(den[:, 0:1], den[:, 0:1],
                                                         enD[:, dc:dc + 1]),
                 reads=[("enD",), ("den",)], writes=[("den",)])
            q.do("vector", lambda v: v.reciprocal(den[:], den[:]),
                 reads=[("den",)], writes=[("den",)])
            q.do("vector", lambda v: v.tensor_mul(num[:], num[:], den[:]),
                 reads=[("num",), ("den",)], writes=[("num",)])
            q.do("vector", lambda v, db=db: v.tensor_mul(rwv[:], num[:], srb[db][:]),
                 reads=[("num",), ("srb", db)], writes=[("rwv",)])
            q.dma("gpsimd", p.sc_rw[dc * 128:(dc + 1) * 128, :], rwv[:], ("rwsp",), [("rwv",)])
        q.finish()


def bitffn_record(p, q, ctx, tt):
    nc = p.nc
    sbt = lambda n, s, d: ctx.enter_context(nc.sbuf_tensor(n, s, d))
    if not hasattr(p, "_bfb"):
        p._bfb = {
            "hid": sbt("bf_hid", [128, 44 * 512], BF16),
            "w1b": [sbt(f"bf_w1b{i}", [128, NDC * 128], BF16) for i in range(2)],
            "w2b": [sbt(f"bf_w2b{i}", [128, NDC * 128], BF16) for i in range(2)],
            "w3b": [sbt("bf_w3b0", [128, 44 * 128], BF16)],
            "sg": sbt("bf_sg", [128, 512], BF16),
            "gq": sbt("bf_gq", [128, 512], BF16),
        }
    hid = p._bfb["hid"]; w1b = p._bfb["w1b"]; w2b = p._bfb["w2b"]
    w3b = p._bfb["w3b"]; sg = p._bfb["sg"]; gq = p._bfb["gq"]
    for fc in range(44):
        wb = fc % 2
        q.dma("sync", w1b[wb][:].rearrange("q (dc w) -> q dc w", w=128),
              p.sc_w1q.rearrange("(dc q) w -> q dc w", q=128)
              [:, :, fc * 128:(fc + 1) * 128], ("kw1", wb), None)
        q.dma("sync", w2b[wb][:].rearrange("q (dc w) -> q dc w", w=128),
              p.sc_w2q.rearrange("(dc q) w -> q dc w", q=128)
              [:, :, fc * 128:(fc + 1) * 128], ("kw2", wb), None)

        def h1grp(t, wb=wb, tt=tt, fb2=fc % 2):
            for dc in range(NDC):
                ins = t.matmul(p.ps[2 * fb2][0:128, 0:512],
                               w1b[wb][:, dc * 128:(dc + 1) * 128],
                               p.xn1sb[:, dc * TQ + tt * 512:dc * TQ + (tt + 1) * 512],
                               start=(dc == 0), stop=(dc == NDC - 1))
            return ins
        q.do("tensor", h1grp, reads=[("kw1", wb), ("xn1g",)],
             writes=[("ps", 2 * (fc % 2))])

        def h2grp(t, wb=wb, tt=tt, fb2=fc % 2):
            for dc in range(NDC):
                ins = t.matmul(p.ps[2 * fb2 + 1][0:128, 0:512],
                               w2b[wb][:, dc * 128:(dc + 1) * 128],
                               p.xn1sb[:, dc * TQ + tt * 512:dc * TQ + (tt + 1) * 512],
                               start=(dc == 0), stop=(dc == NDC - 1))
            return ins
        q.do("tensor", h2grp, reads=[("kw2", wb), ("xn1g",)],
             writes=[("ps", 2 * (fc % 2) + 1)])
        q.do("scalar", lambda sc, fb2=fc % 2: sc.activation(
            out=sg[:], in_=p.ps[2 * fb2][0:128, 0:512], func=AF.Sigmoid),
            reads=[("ps", 2 * (fc % 2))], writes=[("sg",)])
        q.do("vector", lambda v, fc=fc, fb2=fc % 2: v.tensor_mul(
            hid[:, fc * 512:(fc + 1) * 512], sg[:], p.ps[2 * fb2 + 1][0:128, 0:512]),
            reads=[("sg",), ("ps", 2 * (fc % 2) + 1)], writes=[("hid",)])
    for oc in range(NDC):
        wb3 = 0
        q.dma("sync", w3b[wb3][:].rearrange("q (fc w) -> q fc w", w=128),
              p.sc_w3q.rearrange("(fc q) w -> q fc w", q=128)
              [:, :, oc * 128:(oc + 1) * 128], ("kw3", wb3), None)

        def w3grp(t, ob2=oc % 2, wb3=wb3):
            for fc in range(44):
                ins = t.matmul(p.ps[4 + ob2][0:128, 0:512],
                               w3b[wb3][:, fc * 128:(fc + 1) * 128],
                               hid[:, fc * 512:(fc + 1) * 512],
                               start=(fc == 0), stop=(fc == 43))
            return ins
        q.do("tensor", w3grp, reads=[("kw3", wb3), ("hid",)],
             writes=[("ps", 4 + oc % 2)])
        q.do("scalar", lambda sc, ob2=oc % 2: sc.activation(
            out=gq[:], in_=p.ps[4 + ob2][0:128, 0:512], func=AF.Silu),
            reads=[("ps", 4 + oc % 2)], writes=[("gq",)])
        q.dma("gpsimd", p.sc_gq[oc * 128:(oc + 1) * 128, tt * 512:(tt + 1) * 512],
              gq[:], ("gqsp",), [("gq",)])


def ph9(p):
    nc = p.nc
    with ExitStack() as ctx:
        sbt = lambda n, s, d: ctx.enter_context(nc.sbuf_tensor(n, s, d))
        rw = sbt("k_rw", [128, NDC * 512], BF16)
        wo = [sbt(f"k_wo{i}", [128, NDC * 128], BF16) for i in range(3)]
        y1 = sbt("k_y1", [128, NDC * 512], BF16)
        gq = [sbt(f"k_gq{i}", [128, 512], BF16) for i in range(2)]
        xt = [sbt(f"k_xt{i}", [128, 512], F32) for i in range(2)]
        fin = [sbt(f"k_fin{i}", [128, 512], F32) for i in range(2)]
        q = AutoQ(p)
        for tt in range(4):
            q.dma("sync", rw[:].rearrange("q (dc w) -> q dc w", w=512),
                  p.sc_rw.rearrange("(dc q) w -> q dc w", q=128)[:, :, tt * 512:(tt + 1) * 512],
                  ("rw",), None)
            for oc in range(NDC):
                wb = oc % 3
                q.dma("gpsimd", wo[wb][:].rearrange("q (dc w) -> q dc w", w=128),
                      p.wo1T.rearrange("(dc q) w -> q dc w", q=128)
                      [:, :, oc * 128:(oc + 1) * 128], ("kwo", wb), None)

                def ygrp(t, wb=wb):
                    for dc in range(NDC):
                        ins = t.matmul(p.ps[wb][0:128, 0:512],
                                       wo[wb][:, dc * 128:(dc + 1) * 128],
                                       rw[:, dc * 512:(dc + 1) * 512],
                                       start=(dc == 0), stop=(dc == NDC - 1))
                    return ins
                q.do("tensor", ygrp, reads=[("kwo", wb), ("rw",)], writes=[("ps", wb)])
                q.do("vector", lambda v, oc=oc, wb=wb: v.tensor_copy(
                    y1[:, oc * 512:(oc + 1) * 512], p.ps[wb][0:128, 0:512]),
                    reads=[("ps", wb)], writes=[("y1",)])
            for oc in range(NDC):
                ob = oc % 2
                q.dma("sync", gq[ob][:],
                      p.sc_gq[oc * 128:(oc + 1) * 128, tt * 512:(tt + 1) * 512],
                      ("kgq", ob), None)
                q.dma("sync", xt[ob][:],
                      p.sc_x1[oc * 128:(oc + 1) * 128, tt * 512:(tt + 1) * 512],
                      ("kxt", ob), None)
                q.do("vector", lambda v, oc=oc, ob=ob: v.tensor_mul(
                    fin[ob][:], y1[:, oc * 512:(oc + 1) * 512], gq[ob][:]),
                    reads=[("y1",), ("kgq", ob)], writes=[("fin", ob)])
                q.do("vector", lambda v, ob=ob: v.tensor_add(fin[ob][:], fin[ob][:], xt[ob][:]),
                     reads=[("fin", ob), ("kxt", ob)], writes=[("fin", ob)])
                q.dma("gpsimd", p.out[oc * 128:(oc + 1) * 128, tt * 512:(tt + 1) * 512],
                      fin[ob][:], ("outsp", ob), [("fin", ob)])
        q.finish()


def make_in_maps(I):
    def C(x):
        return np.ascontiguousarray(np.asarray(x, dtype=np.float32))

    in_maps = []
    for c in range(8):
        b, g = c // 4, c % 4
        qr = slice(512 * g, 512 * (g + 1))
        m = {}
        m["xT"] = C(I["L0_attn_x"][b].T)
        m["qi"] = np.array([[g]], np.uint32)
        m["xq4T"] = C(I["L0_attn_x"][b, g * TQ:(g + 1) * TQ, :].T)
        rows = np.r_[np.arange(512 * g, 512 * (g + 1)),
                     2048 + np.arange(512 * g, 512 * (g + 1)),
                     4096 + np.arange(512 * g, 512 * (g + 1))]
        m["wqkvT"] = C(I["L0_attn_W_qkv"][rows].T)
        m["convw"] = C(I["L0_attn_conv_w"][rows])
        absel = np.r_[np.arange(g * 4, g * 4 + 4), 16 + np.arange(g * 4, g * 4 + 4)]
        m["wabT"] = C(I["L0_attn_W_ab"][absel].T)
        m["bab"] = C(I["L0_attn_b_ab"][absel][:, None])
        m["wgT"] = C(I["L0_attn_W_g"][qr].T)
        m["woT"] = C(I["L0_attn_W_o"].T)
        m["ln1r"] = C(I["L0_ffn_ln1_w"].reshape(NDC, 128).T)
        m["ln2r"] = C(I["L0_ffn_ln2_w"].reshape(NDC, 128).T)
        m["wqfT"] = C(I["L0_ffn_Wq"][qr].T)
        m["wkfT"] = C(I["L0_ffn_Wk"][qr].T)
        m["wofs"] = C(I["L0_ffn_Wo"][:, qr])
        m["wvfs"] = C(I["L0_ffn_Wv"][qr, :])
        m["A_"] = C(I["L0_ffn_A"])
        m["B_"] = C(I["L0_ffn_B"])
        m["winT"] = C(I["L0_ffn_W_in_base"].T)
        m["wupT"] = C(I["L0_ffn_W_up_base"].T)
        m["wdnT"] = C(I["L0_ffn_W_down"].T)
        m["tdr"] = C(I["L1_attn_time_decay"].reshape(NDC, 128).T)
        m["tfr"] = C(I["L1_attn_time_first"].reshape(NDC, 128).T)
        m["mkr"] = C(np.asarray(I["L1_attn_time_mix_k"]).reshape(-1).reshape(NDC, 128).T)
        m["mvr"] = C(np.asarray(I["L1_attn_time_mix_v"]).reshape(-1).reshape(NDC, 128).T)
        m["mrr"] = C(np.asarray(I["L1_attn_time_mix_r"]).reshape(-1).reshape(NDC, 128).T)
        m["wk1T"] = C(I["L1_attn_Wk"].T)
        m["wv1T"] = C(I["L1_attn_Wv"].T)
        m["wr1T"] = C(I["L1_attn_Wr"].T)
        m["wo1T"] = C(I["L1_attn_Wo"].T)
        m["w1T"] = C(I["L1_ffn_w1"].T)
        m["w2T"] = C(I["L1_ffn_w2"].T)
        m["w3T"] = C(I["L1_ffn_w3"].T)
        in_maps.append(m)
    return in_maps


def kernel(**I):
    in_maps = make_in_maps(I)
    nc = build_program()
    res = run_bass_kernel_spmd(nc, in_maps, core_ids=list(range(8)))
    out = np.empty((2, T, D), np.float32)
    for c in range(8):
        b, g = c // 4, c % 4
        out[b, g * TQ:(g + 1) * TQ, :] = res.results[c]["out"].T
    return out


# version 11
# speedup vs baseline: 1.0175x; 1.0171x over previous
import math
import numpy as np
from contextlib import ExitStack

import concourse.bass as bass
import concourse.mybir as mybir
from concourse.bass_utils import run_bass_kernel_spmd
import ml_dtypes

F32 = mybir.dt.float32
BF16 = mybir.dt.bfloat16
AF = mybir.ActivationFunctionType
ALU = mybir.AluOpType
bf16 = ml_dtypes.bfloat16

D = 2048
T = 8192
TQ = 2048
HD = 128
NHL = 4
CD = 128
NCH = T // CD
FF = 8192
FB = 5632
NDC = D // 128
RT = 512
NTT = T // RT

DEBUG = False
LAST_MARKS = None
N_PHASES = 10


class Sem:
    def __init__(self, nc, ctx, name):
        self.h = ctx.enter_context(nc.semaphore(name))
        self.n = 0

    def inc(self, k=1):
        self.n += k
        return self.n


class P:
    pass


def chain(eng, sem, ins):
    ins.then_inc(sem.h, 1)
    sem.inc()
    eng.wait_ge(sem.h, sem.n)


class AutoQ:

    QUEUES = ("sync", "gpsimd", "squeue")

    def __init__(self, p):
        self.p = p
        self.steps = {e: [] for e in ("sync", "gpsimd", "scalar", "vector", "tensor")}
        self.esem = {"scalar": p.s_ac, "vector": p.s_ve, "tensor": p.s_mm,
                     "gpsimd": p.s_gp, "sync": p.s_ld}
        self.writer = {}
        self.readers = {}
        self.waited = {}
        if not hasattr(p, "_qmap"):
            p._qmap = {}
            p._qcount = {"sync": 0, "gpsimd": 0, "squeue": 0}
            p._qlastbuf = {}

    def _qsem(self, q, buf):
        key = (q, buf)
        if key not in self.p._qmap:
            pool = {"sync": self.p.sqs, "gpsimd": self.p.sqg, "squeue": self.p.sql}[q]
            self.p._qmap[key] = pool[self.p._qcount[q] % len(pool)]
            self.p._qcount[q] += 1
        return self.p._qmap[key]

    def _dep_waits(self, engine, reads, writes):
        waits = []

        def add(dep):
            if dep is None:
                return
            sem, tgt, oeng = dep
            k = (engine, id(sem))
            if self.waited.get(k, -1) >= tgt:
                return
            self.waited[k] = tgt
            waits.append((sem, tgt))

        for r in reads:
            add(self.writer.get(r))
        for w in writes:
            add(self.writer.get(w))
            for dep in self.readers.get(w, ()):
                add(dep)
        return waits

    def do(self, engine, emit, reads=(), writes=(), n_inc=1):
        sem = self.esem[engine]
        waits = self._dep_waits(engine, reads, writes)
        tgt = sem.inc(n_inc)
        dep = (sem, tgt, engine)
        for w in writes:
            self.writer[w] = dep
            self.readers[w] = []
        for r in reads:
            self.readers.setdefault(r, []).append(dep)

        def run(eng, waits=waits, emit=emit, sem=sem, n_inc=n_inc,
                warm=(engine == "tensor" and bool(waits))):
            if warm:
                eng.ldweights(self.p.ident[:])
                eng.ldweights(self.p.ident[:])
                eng.ldweights(self.p.ident[:])
            for ws, wt in waits:
                eng.wait_ge(ws.h, wt)
            ins = emit(eng)
            ins.then_inc(sem.h, n_inc)

        self.steps[engine].append(run)

    def dma(self, q, out_ap, in_ap, out_buf, in_buf, **kw):
        qeng = {"sync": "sync", "gpsimd": "gpsimd", "squeue": "scalar"}[q]
        sem = self._qsem(q, out_buf)
        in_bufs = ([] if in_buf is None else
                   (list(in_buf) if isinstance(in_buf, list) else [in_buf]))
        waits = self._dep_waits(qeng, in_bufs, [out_buf])
        if self.p._qlastbuf.get(id(sem)) not in (None, out_buf):
            k = (qeng, id(sem))
            if self.waited.get(k, -1) < sem.n:
                self.waited[k] = sem.n
                waits.append((sem, sem.n))
        self.p._qlastbuf[id(sem)] = out_buf
        tgt = sem.inc(16)
        dep = (sem, tgt, qeng)
        self.writer[out_buf] = dep
        self.readers[out_buf] = []
        for ib in in_bufs:
            self.readers.setdefault(ib, []).append(dep)

        def run(eng, waits=waits, sem=sem, out_ap=out_ap, in_ap=in_ap, kw=kw,
                _ob=out_buf, _ib=in_bufs):
            for ws, wt in waits:
                eng.wait_ge(ws.h, wt)
            try:
                eng.dma_start(out=out_ap, in_=in_ap, **kw).then_inc(sem.h, 16)
            except Exception:
                import traceback
                print("DMA FAIL out_buf=", _ob, "in=", _ib)
                print("OUT:", out_ap)
                print("IN:", in_ap)
                traceback.print_exc()
                raise

        self.steps[qeng].append(run)

    def raw(self, engine, emit):
        self.steps[engine].append(emit)

    def finish(self):
        nc = self.p.nc
        qsems = {id(s): s for (q, b), s in self.p._qmap.items()}
        esems = [self.esem[e] for e in ("scalar", "vector", "tensor", "gpsimd", "sync")]

        def drain(eng):
            for s in qsems.values():
                eng.wait_ge(s.h, s.n)
            for s in esems:
                eng.wait_ge(s.h, s.n)

        with nc.Block() as b:
            for ename, reg in (("sync", b.sync), ("gpsimd", b.gpsimd),
                               ("scalar", b.scalar), ("vector", b.vector),
                               ("tensor", b.tensor)):
                def body(eng, ename=ename):
                    for fn in self.steps[ename]:
                        fn(eng)
                    if ename == "gpsimd":
                        drain(eng)
                reg(body)


def build_program():
    nc = bass.Bass("TRN2", target_bir_lowering=False)
    p = P()
    p.nc = nc
    ctx = ExitStack()
    p.ctx = ctx

    def ein(name, shape, dt=F32):
        return nc.dram_tensor(name, shape, dt, kind="ExternalInput").ap()

    p.xT = ein("xT", [D, T])
    p.wqkvT = ein("wqkvT", [D, 1536])
    p.convw = ein("convw", [1536, 4])
    p.wabT = ein("wabT", [D, 8])
    p.bab = ein("bab", [8, 1])
    p.wgT = ein("wgT", [D, 512])
    p.woT = ein("woT", [D, D])
    p.ln1r = ein("ln1r", [128, NDC])
    p.ln2r = ein("ln2r", [128, NDC])
    p.wqfT = ein("wqfT", [D, 512])
    p.wkfT = ein("wkfT", [D, 512])
    p.wofs = ein("wofs", [D, 512])
    p.wvfs = ein("wvfs", [512, D])
    p.A_ = ein("A_", [D, 16])
    p.B_ = ein("B_", [16, FF])
    p.winT = ein("winT", [D, FF])
    p.wupT = ein("wupT", [D, FF])
    p.wdnT = ein("wdnT", [FF, D])
    p.tdr = ein("tdr", [128, NDC])
    p.tfr = ein("tfr", [128, NDC])
    p.mkr = ein("mkr", [128, NDC])
    p.mvr = ein("mvr", [128, NDC])
    p.mrr = ein("mrr", [128, NDC])
    p.wk1T = ein("wk1T", [D, D])
    p.wv1T = ein("wv1T", [D, D])
    p.wr1T = ein("wr1T", [D, D])
    p.wo1T = ein("wo1T", [D, D])
    p.w1T = ein("w1T", [D, FB])
    p.w2T = ein("w2T", [D, FB])
    p.w3T = ein("w3T", [FB, D])
    p.qi = nc.dram_tensor("qi", [1, 1], mybir.dt.uint32, kind="ExternalInput").ap()
    p.xq4T = ein("xq4T", [D, TQ])

    p.out = nc.dram_tensor("out", [D, TQ], F32, kind="ExternalOutput").ap()

    dbg = "ExternalOutput" if DEBUG else "Internal"

    def intl(name, shape, dt=BF16, debug_vis=False):
        kind = dbg if debug_vis else "Internal"
        return nc.dram_tensor(name, shape, dt, kind=kind).ap()

    p.sc_qd = intl("sc_qd", [512, T], debug_vis=True)
    p.sc_kd = intl("sc_kd", [512, T], debug_vis=True)
    p.sc_vd = intl("sc_vd", [512, T], debug_vis=True)
    p.sc_gg = intl("sc_gg", [512, T], debug_vis=True)
    p.sc_qf = intl("sc_qf", [512, T], debug_vis=True)
    p.sc_kf = intl("sc_kf", [512, T], debug_vis=True)
    p.sc_vt = intl("sc_vt", [T, 4 * 33], debug_vis=True)
    p.sc_xr = intl("sc_xr", [D, T])
    p.sc_xn = intl("sc_xn", [D, T], debug_vis=True)
    p.sc_ab = intl("sc_ab", [16, T], F32, debug_vis=True)
    p.sc_bnd = intl("sc_bnd", [1, 8], F32, debug_vis=True)
    p.ag_og_in = intl("ag_og_in", [512, T])
    p.ag_og_out = intl("ag_og_out", [D, T], debug_vis=True)
    p.ar_m_in = intl("ar_m_in", [16, T], F32)
    p.ar_m_out = intl("ar_m_out", [16, T], F32, debug_vis=True)
    p.sc_x1 = intl("sc_x1", [D, TQ], F32, debug_vis=True)
    p.ag_h_in = intl("ag_h_in", [1, D])
    p.ag_h_out = intl("ag_h_out", [4, D])
    p.ag_e_in = intl("ag_e_in", [2, D], F32)
    p.ag_e_out = intl("ag_e_out", [8, D], F32)
    p.sc_k1 = intl("sc_k1", [D, TQ], debug_vis=True)
    p.sc_v1 = intl("sc_v1", [D, TQ])
    p.sc_r1 = intl("sc_r1", [D, TQ])
    p.sc_Nl = intl("sc_Nl", [D, TQ], F32)
    p.sc_Dl = intl("sc_Dl", [D, TQ], F32)
    p.sc_xk = intl("sc_xk", [D, TQ])
    p.sc_xv = intl("sc_xv", [D, TQ])
    p.sc_xrm = intl("sc_xrm", [D, TQ])
    p.sc_sr = intl("sc_sr", [D, TQ])
    p.sc_rw = intl("sc_rw", [D, TQ], debug_vis=True)
    p.sc_gq = intl("sc_gq", [D, TQ])
    p.sc_stb1 = intl("sc_stb1", [2, TQ])
    p.ag_h2 = intl("ag_h2", [5, D])
    p.sc_w1q = intl("sc_w1q", [D, FB], debug_vis=True)
    p.sc_w2q = intl("sc_w2q", [D, FB])
    p.sc_w3q = intl("sc_w3q", [FB, D])
    p.dbg = intl("dbgout", [128, 2048], F32, debug_vis=True)
    p.sc_st = intl("sc_st", [8, T], F32, debug_vis=True)
    p.sc_stb = intl("sc_stb", [4, T], BF16, debug_vis=True)

    p.c_ident = nc.inline_tensor(np.eye(128, dtype=bf16), name="c_ident").ap()
    p.c_ones = nc.inline_tensor(np.ones((128, 512), dtype=bf16), name="c_ones").ap()
    s_i = np.arange(CD)[:, None]
    t_i = np.arange(CD)[None, :]
    p.c_maskUs = nc.inline_tensor((s_i < t_i).astype(bf16), name="c_maskUs").ap()
    p.c_maskUi = nc.inline_tensor((s_i <= t_i).astype(bf16), name="c_maskUi").ap()
    dm = np.concatenate(
        [((np.arange(128)[:, None] + 128 * a) <= np.arange(512)[None, :]).astype(bf16)
         for a in range(4)], axis=1)
    p.c_diag = nc.inline_tensor(dm, name="c_diag").ap()
    p.c_iota = nc.inline_tensor(
        np.tile((np.arange(TQ, dtype=np.float32) + 1.0)[None, :], (128, 1)),
        name="c_iota").ap()
    p.c_zero = nc.inline_tensor(np.zeros((128, 512), np.float32), name="c_zero").ap()
    p.c_identf = nc.inline_tensor(np.eye(64, dtype=np.float32), name="c_identf").ap()

    sb = lambda name, shape, dt: ctx.enter_context(nc.sbuf_tensor(name, shape, dt))
    p.ident = sb("ident", [128, 128], BF16)
    p.ones = sb("ones", [128, 512], BF16)
    p.maskUs = sb("maskUs", [CD, CD], BF16)
    p.maskUi = sb("maskUi", [CD, CD], BF16)
    p.diagm = sb("diagm", [128, 4 * 512], BF16)
    p.zeros = sb("zeros", [128, 512], F32)

    p.v_babn = sb("v_babn", [8, 1], F32)
    p.qit = sb("qit", [1, 1], mybir.dt.uint32)
    p.w_conv = sb("w_conv", [128, 48], F32)
    p.v_bab = sb("v_bab", [8, 1], F32)
    p.v_ln1 = sb("v_ln1", [128, NDC], F32)
    p.v_ln2 = sb("v_ln2", [128, NDC], F32)
    p.v_td = sb("v_td", [128, NDC], F32)
    p.v_tf = sb("v_tf", [128, NDC], F32)
    p.v_mk = sb("v_mk", [128, NDC], F32)
    p.v_mv = sb("v_mv", [128, NDC], F32)
    p.v_mr = sb("v_mr", [128, NDC], F32)
    p.v_w = sb("v_w", [128, NDC], F32)
    p.v_ew = sb("v_ew", [128, NDC], F32)
    p.v_decq = sb("v_decq", [128, NDC], F32)
    p.v_mk1 = sb("v_mk1", [128, NDC], F32)
    p.v_mv1 = sb("v_mv1", [128, NDC], F32)
    p.v_mr1 = sb("v_mr1", [128, NDC], F32)
    p.v_wtq = sb("v_wtq", [128, NDC], F32)

    p.w_pall = sb("w_pall", [128, NDC * 64], BF16)
    p.w_a2 = sb("w_a2", [128, NDC * 16], BF16)

    p.ps = [ctx.enter_context(nc.psum_tensor(f"ps{i}", [128, 512], F32)) for i in range(6)]
    p.psb = [ctx.enter_context(nc.psum_tensor(f"psb{i}", [128, 1024], BF16)) for i in range(2)]

    p.constpool = sb("constpool", [128, 8], F32)
    p.onesf32 = sb("onesf32", [1, 128], F32)
    p.CONST_VALS = [1e-5, 1e-6, 0.5, -0.5, 1.0, 0.0, 0.0, 0.0]
    p.cst = lambda v: p.constpool[:, p.CONST_VALS.index(v):p.CONST_VALS.index(v) + 1]

    p.s_ld = Sem(nc, ctx, "s_ld")
    p.s_st = Sem(nc, ctx, "s_st")
    p.s_l2 = Sem(nc, ctx, "s_l2")
    p.s_mm = Sem(nc, ctx, "s_mm")
    p.s_ve = Sem(nc, ctx, "s_ve")
    p.s_ac = Sem(nc, ctx, "s_ac")
    p.s_gp = Sem(nc, ctx, "s_gp")
    p.sqs = [Sem(nc, ctx, f"sqs{i}") for i in range(8)]
    p.sqg = [Sem(nc, ctx, f"sqg{i}") for i in range(4)]
    p.sql = [Sem(nc, ctx, f"sql{i}") for i in range(4)]

    phases = [ph0, ph1, ph2, ph3, ph4, ph5, ph6, ph7, ph8, ph9][:N_PHASES]
    p.phase_marks = []
    for ph in phases:
        ph(p)
        nc.all_engine_barrier()
        p.phase_marks.append((ph.__name__, nc.n_instructions()))
    global LAST_MARKS
    LAST_MARKS = list(p.phase_marks)
    ctx.close()
    p.nc_ref = nc
    return nc


def ph0(p):
    nc = p.nc
    ctx0 = ExitStack()
    wofs_sb = ctx0.enter_context(nc.sbuf_tensor("wofs_sb", [128, NDC * 512], BF16))
    wvfs_sb = ctx0.enter_context(nc.sbuf_tensor("wvfs_sb", [128, 4 * D], BF16))
    w2_sb = ctx0.enter_context(nc.sbuf_tensor("w2_sb", [128, 64], BF16))
    aw_sb = ctx0.enter_context(nc.sbuf_tensor("aw_sb", [128, NDC * 16], BF16))

    ld0, st0, ve00 = p.s_ld.n, p.s_st.n, p.s_ve.n
    with nc.Block() as b:
        @b.sync
        def _(e):
            e.dma_start(out=p.ident[:], in_=p.c_ident).then_inc(p.s_ld.h, 16); p.s_ld.inc(16)
            e.dma_start(out=p.ones[:], in_=p.c_ones).then_inc(p.s_ld.h, 16); p.s_ld.inc(16)
            e.dma_start(out=p.maskUs[:], in_=p.c_maskUs).then_inc(p.s_ld.h, 16); p.s_ld.inc(16)
            e.dma_start(out=p.maskUi[:], in_=p.c_maskUi).then_inc(p.s_ld.h, 16); p.s_ld.inc(16)
            e.dma_start(out=p.diagm[:], in_=p.c_diag).then_inc(p.s_ld.h, 16); p.s_ld.inc(16)
            e.dma_start(out=p.zeros[:], in_=p.c_zero).then_inc(p.s_ld.h, 16); p.s_ld.inc(16)
            for dst, src in [(p.v_bab, p.bab), (p.v_ln1, p.ln1r), (p.v_ln2, p.ln2r),
                             (p.v_td, p.tdr), (p.v_tf, p.tfr), (p.v_mk, p.mkr),
                             (p.v_mv, p.mvr), (p.v_mr, p.mrr)]:
                e.dma_start(out=dst[:], in_=src).then_inc(p.s_ld.h, 16); p.s_ld.inc(16)
            e.dma_start(out=p.qit[:], in_=p.qi).then_inc(p.s_ld.h, 16); p.s_ld.inc(16)
            e.dma_start(out=p.w_conv[:].rearrange("q (r f) -> q r f", f=4),
                        in_=p.convw.rearrange("(r q) f -> q r f", q=128)
                        ).then_inc(p.s_ld.h, 16); p.s_ld.inc(16)
            e.wait_ge(p.s_ld.h, p.s_ld.n)

        @b.gpsimd
        def _(g):
            for dst, src, oc in [
                (wofs_sb, p.wofs, 512), (aw_sb, p.A_, 16),
            ]:
                g.dma_start(out=dst[:, 0:NDC * oc].rearrange("q (dc oc) -> q dc oc", oc=oc),
                            in_=src.rearrange("(dc q) oc -> q dc oc", q=128)
                            ).then_inc(p.s_st.h, 16); p.s_st.inc(16)
            g.dma_start(out=wvfs_sb[:].rearrange("q (hc oc) -> q hc oc", oc=D),
                        in_=p.wvfs.rearrange("(hc q) oc -> q hc oc", q=128)
                        ).then_inc(p.s_st.h, 16); p.s_st.inc(16)
            g.wait_ge(p.s_st.h, p.s_st.n)

        @b.scalar
        def _(sc):
            sc.wait_ge(p.s_ld.h, ld0 + 16 * 16)
            sc.wait_ge(p.s_st.h, st0 + 3 * 16)
            sc.wait_ge(p.s_ve.h, ve00 + 1)
            for dc in range(NDC):
                sc.mul(out=p.w_a2[:, dc * 16:(dc + 1) * 16],
                       in_=aw_sb[:, dc * 16:(dc + 1) * 16], mul=p.v_ln2[:, dc:dc + 1])
            sc.mul(out=p.v_babn[:], in_=p.v_bab[:], mul=-1.0)
            sc.activation(out=p.v_mk1[:], in_=p.v_mk[:], func=AF.Identity, scale=-1.0,
                          bias=p.cst(1.0))
            sc.activation(out=p.v_mv1[:], in_=p.v_mv[:], func=AF.Identity, scale=-1.0,
                          bias=p.cst(1.0))
            sc.activation(out=p.v_mr1[:], in_=p.v_mr[:], func=AF.Identity, scale=-1.0,
                          bias=p.cst(1.0))
            sc.activation(out=p.v_w[:], in_=p.v_td[:], func=AF.Exp
                          ).then_inc(p.s_ac.h, 1); _ = p.s_ac.inc()
            sc.wait_ge(p.s_ac.h, p.s_ac.n)
            sc.mul(out=p.v_w[:], in_=p.v_w[:], mul=-1.0).then_inc(p.s_ac.h, 1); _ = p.s_ac.inc()
            sc.wait_ge(p.s_ac.h, p.s_ac.n)
            sc.activation(out=p.v_ew[:], in_=p.v_w[:], func=AF.Exp)
            sc.activation(out=p.v_decq[:], in_=p.v_w[:], func=AF.Exp, scale=float(TQ))
            sc.mul(out=p.v_wtq[:], in_=p.v_w[:], mul=float(TQ)
                   ).then_inc(p.s_ac.h, 1); _ = p.s_ac.inc()

        @b.vector
        def _(v):
            for j, val in enumerate(p.CONST_VALS):
                v.memset(p.constpool[:, j:j + 1], float(val))
            ins = v.memset(p.onesf32[:], 1.0)
            ins.then_inc(p.s_ve.h, 1); _ = p.s_ve.inc()

        @b.tensor
        def _(t):
            t.wait_ge(p.s_st.h, st0 + 3 * 16)
            for ic in range(4):
                for dc in range(NDC):
                    ins = t.matmul(p.ps[0][0:128, ic * 16:(ic + 1) * 16],
                                   wofs_sb[:, dc * 512 + ic * 128: dc * 512 + ic * 128 + 128],
                                   aw_sb[:, dc * 16:(dc + 1) * 16],
                                   start=(dc == 0), stop=(dc == NDC - 1))
            ins.then_inc(p.s_mm.h, 1); _ = p.s_mm.inc()

    nc.all_engine_barrier()
    ve0, mm0, ac0 = p.s_ve.n, p.s_mm.n, p.s_ac.n
    with nc.Block() as b2:
        @b2.vector
        def _(v):
            v.tensor_copy(w2_sb[:], p.ps[0][0:128, 0:64]).then_inc(p.s_ve.h, 1); _ = p.s_ve.inc()

        @b2.tensor
        def _(t):
            t.wait_ge(p.s_ve.h, ve0 + 1)
            for dc in range(NDC):
                if dc >= 2:
                    t.wait_ge(p.s_ac.h, ac0 + dc - 1)
                bank = p.ps[1 + dc % 2]
                for h in range(NHL):
                    ins = t.matmul(bank[0:128, h * 16:(h + 1) * 16],
                                   wvfs_sb[:, h * D + dc * 128: h * D + dc * 128 + 128],
                                   w2_sb[:, h * 16:(h + 1) * 16], start=True, stop=True)
                ins.then_inc(p.s_mm.h, 1); _ = p.s_mm.inc()

        @b2.scalar
        def _(sc):
            for dc in range(NDC):
                sc.wait_ge(p.s_mm.h, mm0 + dc + 1)
                sc.mul(out=p.w_pall[:, dc * 64:(dc + 1) * 64],
                       in_=p.ps[1 + dc % 2][0:128, 0:64], mul=p.v_ln1[:, dc:dc + 1]
                       ).then_inc(p.s_ac.h, 1); _ = p.s_ac.inc()

        @b2.gpsimd
        def _(g):
            p.qv = nc.values_load(p.qit[:], engines=[mybir.EngineType.Pool])
            p.qcol = [nc.snap(p.qv * TQ + u * 512, engines=[mybir.EngineType.Pool],
                              min_val=0, max_val=T - 512) for u in range(4)]
            p.qrow = nc.snap(p.qv * D, engines=[mybir.EngineType.Pool],
                             min_val=0, max_val=3 * D)
    ctx0.close()


def ph1(p):
    nc = p.nc
    ctx1 = ExitStack()
    xf = [ctx1.enter_context(nc.sbuf_tensor(f"p1xf{i}", [128, 512], F32)) for i in range(2)]
    xb = [ctx1.enter_context(nc.sbuf_tensor(f"p1xb{i}", [128, 512], BF16)) for i in range(2)]
    xq = [ctx1.enter_context(nc.sbuf_tensor(f"p1xq{i}", [128, 512], BF16)) for i in range(2)]
    sx = ctx1.enter_context(nc.sbuf_tensor("p1sx", [33, T], F32))
    st = ctx1.enter_context(nc.sbuf_tensor("p1st", [128, 64 * 14], F32))

    NIT = NTT * NDC
    sx0, sx1 = p.sqs[0], p.sqs[1]
    x0_0, x1_0 = sx0.n, sx1.n
    ve0, mm0, ac0 = p.s_ve.n, p.s_mm.n, p.s_ac.n
    with nc.Block() as b:
        @b.sync
        def _(e):
            for k in range(NIT):
                tt, dc = divmod(k, NDC)
                sm = sx0 if k % 2 == 0 else sx1
                if k >= 2:
                    e.wait_ge(p.s_ve.h, ve0 + 2 * (k - 1))
                e.dma_start(out=xf[k % 2][:],
                            in_=p.xT[dc * 128:(dc + 1) * 128, tt * 512:(tt + 1) * 512]
                            ).then_inc(sm.h, 16)
                sm.inc(16)

        @b.vector
        def _(v):
            for k in range(NIT):
                base = x0_0 if k % 2 == 0 else x1_0
                v.wait_ge((sx0 if k % 2 == 0 else sx1).h, base + (k // 2 + 1) * 16)
                if k >= 2:
                    v.wait_ge(p.s_mm.h, mm0 + 2 * (k - 1))
                chain(v, p.s_ve, v.tensor_copy(xb[k % 2][:], xf[k % 2][:]))
                v.tensor_mul(xq[k % 2][:], xb[k % 2][:], xb[k % 2][:]
                             ).then_inc(p.s_ve.h, 1)
                p.s_ve.inc()

        @b.tensor
        def _(t):
            for k in range(NIT):
                tt, dc = divmod(k, NDC)
                t.wait_ge(p.s_ve.h, ve0 + 2 * k + 2)
                if dc == 0 and tt >= 2:
                    t.wait_ge(p.s_ac.h, ac0 + tt - 1)
                if k >= 2:
                    pass
                ba, bq = p.ps[2 * (tt % 2)], p.ps[2 * (tt % 2) + 1]
                t.matmul(ba[0:1, 0:512], p.ones[:, 0:1], xb[k % 2][:],
                         start=(dc == 0), stop=(dc == NDC - 1)).then_inc(p.s_mm.h, 1)
                t.matmul(bq[0:1, 0:512], p.ones[:, 0:1], xq[k % 2][:],
                         start=(dc == 0), stop=(dc == NDC - 1)
                         ).then_inc(p.s_mm.h, 1)
                p.s_mm.inc(2)

        @b.scalar
        def _(sc):
            for tt in range(NTT):
                sc.wait_ge(p.s_mm.h, mm0 + (tt + 1) * NDC * 2)
                sc.copy(out=sx[0:1, tt * 512:(tt + 1) * 512], in_=p.ps[2 * (tt % 2)][0:1, 0:512])
                sc.copy(out=sx[32:33, tt * 512:(tt + 1) * 512], in_=p.ps[2 * (tt % 2) + 1][0:1, 0:512]
                        ).then_inc(p.s_ac.h, 1); _ = p.s_ac.inc()

    nc.all_engine_barrier()
    c = lambda i: st[:, 64 * i:64 * (i + 1)]
    ld0, ve0, ac0 = p.s_ld.n, p.s_ve.n, p.s_ac.n
    with nc.Block() as b2:
        @b2.sync
        def _(e):
            e.dma_start(out=p.sc_st[0:1, :], in_=sx[0:1, :]).then_inc(p.s_ld.h, 16); p.s_ld.inc(16)
            e.dma_start(out=p.sc_st[1:2, :], in_=sx[32:33, :]).then_inc(p.s_ld.h, 16); p.s_ld.inc(16)
            e.wait_ge(p.s_ld.h, p.s_ld.n)
            e.dma_start(out=c(0), in_=p.sc_st.rearrange("o (q w) -> o q w", q=128)[0]
                        ).then_inc(p.s_ld.h, 16); p.s_ld.inc(16)
            e.dma_start(out=c(1), in_=p.sc_st.rearrange("o (q w) -> o q w", q=128)[1]
                        ).then_inc(p.s_ld.h, 16); p.s_ld.inc(16)
            e.wait_ge(p.s_ld.h, p.s_ld.n)

        @b2.vector
        def _(v):
            v.wait_ge(p.s_ld.h, ld0 + 64)
            chain(v, p.s_ve, v.tensor_scalar_mul(c(2), c(0), 1.0 / D))
            chain(v, p.s_ve, v.tensor_scalar_mul(c(3), c(1), 1.0 / D))
            chain(v, p.s_ve, v.tensor_mul(c(5), c(2), c(2)))
            chain(v, p.s_ve, v.tensor_sub(c(4), c(3), c(5)))
            v.tensor_scalar_max(c(4), c(4), 0.0).then_inc(p.s_ve.h, 1); _ = p.s_ve.inc()
            v.wait_ge(p.s_ac.h, ac0 + 1)
            chain(v, p.s_ve, v.reciprocal(c(6), c(5)))
            chain(v, p.s_ve, v.tensor_mul(c(7), c(4), c(6)))
            chain(v, p.s_ve, v.tensor_mul(c(7), c(7), c(6)))
            v.tensor_scalar_add(c(7), c(7), 1e-6).then_inc(p.s_ve.h, 1); _ = p.s_ve.inc()
            v.wait_ge(p.s_ac.h, ac0 + 2)
            chain(v, p.s_ve, v.reciprocal(c(9), c(8)))
            chain(v, p.s_ve, v.tensor_mul(c(10), c(6), c(9)))
            chain(v, p.s_ve, v.tensor_mul(c(11), c(2), c(6)))
            chain(v, p.s_ve, v.tensor_scalar_mul(c(11), c(11), -1.0))
            chain(v, p.s_ve, v.tensor_mul(c(12), c(2), c(10)))
            v.tensor_scalar_mul(c(12), c(12), -1.0).then_inc(p.s_ve.h, 1); _ = p.s_ve.inc()

        @b2.scalar
        def _(sc):
            sc.wait_ge(p.s_ve.h, ve0 + 5)
            sc.activation(out=c(5), in_=c(4), func=AF.Sqrt, bias=p.cst(1e-5)
                          ).then_inc(p.s_ac.h, 1); _ = p.s_ac.inc()
            sc.wait_ge(p.s_ve.h, ve0 + 9)
            sc.activation(out=c(8), in_=c(7), func=AF.Sqrt).then_inc(p.s_ac.h, 1); _ = p.s_ac.inc()

        @b2.gpsimd
        def _(g):
            g.wait_ge(p.s_ve.h, ve0 + 15)
            for i, cc in [(0, 6), (1, 11), (2, 10), (3, 12)]:
                g.dma_start(out=p.sc_st.rearrange("o (q w) -> o q w", q=128)[4 + i],
                            in_=c(cc)).then_inc(p.s_st.h, 16)
                p.s_st.inc(16)
                g.dma_start(out=p.sc_stb.rearrange("o (q w) -> o q w", q=128)[i],
                            in_=c(cc)).then_inc(p.s_st.h, 16)
                p.s_st.inc(16)
            g.wait_ge(p.s_st.h, p.s_st.n)
    ctx1.close()


def ph2(p):
    ph2a(p)
    p.nc.all_engine_barrier()
    ph2b(p)
    p.nc.all_engine_barrier()
    ph2c(p)
    p.nc.all_engine_barrier()
    ph2d(p)


def dyn_off(ap0, sym):
    import concourse.ap as _ap
    return _ap.AP(tensor=ap0.tensor, offset=sym, ap=[list(x) for x in ap0.ap])


def xn_tile_dram(p, dram, tt):
    return dram.rearrange("(dc q) w -> q dc w", q=128)[:, :, tt * 512:(tt + 1) * 512]


def ph2a(p):
    nc = p.nc
    with ExitStack() as ctx:
        sbt = lambda n, s, d: ctx.enter_context(nc.sbuf_tensor(n, s, d))
        xf = [sbt(f"a_xf{i}", [128, 512], F32) for i in range(2)]
        xbt = [sbt(f"a_xb{i}", [128, 512], BF16) for i in range(2)]
        bc = sbt("a_bc", [128, 4 * 512], BF16)
        xnt = [sbt(f"a_xnt{i}", [128, NDC * 512], BF16) for i in range(2)]
        xrt = [sbt(f"a_xrt{i}", [128, NDC * 512], BF16) for i in range(2)]
        tmp = sbt("a_tmp", [128, 512], BF16)
        srow = [sbt(f"a_srow{i}", [1, 4 * 512], BF16) for i in range(2)]
        q = AutoQ(p)
        for tt in range(NTT):
            pb = tt % 2
            q.dma("sync", srow[pb][:], p.sc_stb[:, tt * 512:(tt + 1) * 512],
                  ("srow", pb), None)
            for j in range(4):
                q.do("tensor", lambda t, j=j, pb=pb: t.matmul(
                    p.ps[2 + j][0:128, 0:512], p.ones[0:1, 0:128],
                    srow[pb][0:1, j * 512:(j + 1) * 512], start=True, stop=True),
                    reads=[("srow", pb)], writes=[("ps", 2 + j)])
                q.do("vector", lambda v, j=j: v.tensor_copy(
                    bc[:, j * 512:(j + 1) * 512], p.ps[2 + j][0:128, 0:512]),
                    reads=[("ps", 2 + j)], writes=[("bc", j)])
            for dc in range(NDC):
                k = tt * NDC + dc
                q.dma("sync", xf[k % 2][:],
                      p.xT[dc * 128:(dc + 1) * 128, tt * 512:(tt + 1) * 512],
                      ("xf", k % 2), None)
                q.do("vector", lambda v, k=k: v.tensor_copy(xbt[k % 2][:], xf[k % 2][:]),
                     reads=[("xf", k % 2)], writes=[("xb", k % 2)])
                o = dc * 512
                q.do("vector", lambda v, k=k: v.tensor_mul(tmp[:], xbt[k % 2][:], bc[:, 0:512]),
                     reads=[("xb", k % 2), ("bc", 0)], writes=[("tmp",)])
                q.do("vector", lambda v, o=o, pb=pb: v.tensor_add(
                    xnt[pb][:, o:o + 512], tmp[:], bc[:, 512:1024]),
                    reads=[("tmp",), ("bc", 1)], writes=[("xnt", pb)])
                q.do("vector", lambda v, k=k: v.tensor_mul(tmp[:], xbt[k % 2][:], bc[:, 1024:1536]),
                     reads=[("xb", k % 2), ("bc", 2)], writes=[("tmp",)])
                q.do("vector", lambda v, o=o, pb=pb: v.tensor_add(
                    xrt[pb][:, o:o + 512], tmp[:], bc[:, 1536:2048]),
                    reads=[("tmp",), ("bc", 3)], writes=[("xrt", pb)])
            q.dma("gpsimd", xn_tile_dram(p, p.sc_xn, tt),
                  xnt[pb][:].rearrange("q (dc w) -> q dc w", w=512),
                  ("sc_xn",), [("xnt", pb)])
            q.dma("gpsimd", xn_tile_dram(p, p.sc_xr, tt),
                  xrt[pb][:].rearrange("q (dc w) -> q dc w", w=512),
                  ("sc_xr",), [("xrt", pb)])
        q.finish()


def ph2b(p):
    nc = p.nc
    with ExitStack() as ctx:
        sbt = lambda n, s, d: ctx.enter_context(nc.sbuf_tensor(n, s, d))
        wq = sbt("b_wqkv", [128, NDC * 12 * 128], BF16)
        xn = [sbt(f"b_xn{i}", [128, NDC * 512], BF16) for i in range(2)]
        zb = sbt("b_zb", [128, 12 * 516], BF16)
        cv = sbt("b_cv", [128, 512], BF16)
        cv2 = sbt("b_cv2", [128, 512], BF16)
        sq = sbt("b_sq", [128, 512], BF16)
        nrow = sbt("b_nrow", [1, 8 * 512], F32)
        nrowb = sbt("b_nrowb", [1, 8 * 512], BF16)
        stg = sbt("b_stg", [128, 12 * 512], BF16)
        q = AutoQ(p)
        q.dma("gpsimd", wq[:].rearrange("q (dc oc) -> q dc oc", oc=12 * 128),
              p.wqkvT.rearrange("(dc q) oc -> q dc oc", q=128), ("wq",), None)
        for oc in range(12):
            q.do("vector", lambda v, oc=oc: v.memset(zb[:, oc * 516:oc * 516 + 3], 0.0),
                 reads=[], writes=[("zb", oc)])
        for tt in range(NTT):
            q.dma("sync", xn[tt % 2][:].rearrange("q (dc w) -> q dc w", w=512),
                  xn_tile_dram(p, p.sc_xn, tt), ("xn", tt % 2), None)
            for oc in range(12):
                zo = oc * 516

                def grp(t, oc=oc, tt=tt):
                    bank = p.ps[oc % 2]
                    for dc in range(NDC):
                        ins = t.matmul(bank[0:128, 0:512],
                                       wq[:, (dc * 12 + oc) * 128:(dc * 12 + oc) * 128 + 128],
                                       xn[tt % 2][:, dc * 512:(dc + 1) * 512],
                                       start=(dc == 0), stop=(dc == NDC - 1))
                    return ins
                q.do("tensor", grp, reads=[("wq",), ("xn", tt % 2)], writes=[("ps", oc % 2)])
                q.do("vector", lambda v, zo=zo, oc=oc: v.tensor_copy(
                    zb[:, zo + 3:zo + 515], p.ps[oc % 2][0:128, 0:512]),
                    reads=[("ps", oc % 2)], writes=[("zb", oc)])
                q.do("scalar", lambda sc, zo=zo, oc=oc: sc.mul(
                    out=cv2[:], in_=zb[:, zo + 3:zo + 515], mul=p.w_conv[:, oc * 4 + 3:oc * 4 + 4]),
                    reads=[("zb", oc)], writes=[("cv2",)])
                q.do("vector", lambda v, zo=zo, oc=oc: v.scalar_tensor_tensor(
                    cv[:], zb[:, zo + 2:zo + 514], p.w_conv[:, oc * 4 + 2:oc * 4 + 3],
                    cv2[:], ALU.mult, ALU.add),
                    reads=[("zb", oc), ("cv2",)], writes=[("cv",)])
                q.do("vector", lambda v, zo=zo, oc=oc: v.scalar_tensor_tensor(
                    cv[:], zb[:, zo + 1:zo + 513], p.w_conv[:, oc * 4 + 1:oc * 4 + 2],
                    cv[:], ALU.mult, ALU.add),
                    reads=[("zb", oc), ("cv",)], writes=[("cv",)])
                q.do("vector", lambda v, zo=zo, oc=oc: v.scalar_tensor_tensor(
                    cv[:], zb[:, zo + 0:zo + 512], p.w_conv[:, oc * 4 + 0:oc * 4 + 1],
                    cv[:], ALU.mult, ALU.add),
                    reads=[("zb", oc), ("cv",)], writes=[("cv",)])
                q.do("vector", lambda v, zo=zo: v.tensor_copy(zb[:, zo:zo + 3], zb[:, zo + 512:zo + 515]),
                     reads=[("zb", oc)], writes=[("zb", oc)])
                q.do("scalar", lambda sc, oc=oc: sc.activation(
                    out=stg[:, oc * 512:(oc + 1) * 512], in_=cv[:], func=AF.Silu),
                    reads=[("cv",)], writes=[("stg", oc)])
                if oc < 8:
                    q.do("vector", lambda v, oc=oc: v.tensor_mul(
                        sq[:], stg[:, oc * 512:(oc + 1) * 512], stg[:, oc * 512:(oc + 1) * 512]),
                        reads=[("stg", oc)], writes=[("sq",)])
                    q.do("tensor", lambda t: t.matmul(
                        p.ps[2][0:1, 0:512], p.ones[:, 0:1], sq[:], start=True, stop=True),
                        reads=[("sq",)], writes=[("ps", 2)])
                    q.do("scalar", lambda sc, oc=oc: sc.copy(
                        out=nrow[0:1, oc * 512:(oc + 1) * 512], in_=p.ps[2][0:1, 0:512]),
                        reads=[("ps", 2)], writes=[("nrm", oc)])
            rws = [("nrm", i) for i in range(8)]
            q.do("vector", lambda v: v.tensor_scalar_max(nrow[:], nrow[:], 1e-24),
                 reads=rws, writes=rws)
            q.do("scalar", lambda sc: sc.activation(out=nrow[:], in_=nrow[:], func=AF.Sqrt),
                 reads=rws, writes=rws)
            q.do("vector", lambda v: v.reciprocal(nrow[:], nrow[:]),
                 reads=rws, writes=rws)
            q.do("vector", lambda v: v.tensor_copy(nrowb[:], nrow[:]),
                 reads=rws, writes=[("nrowb",)])
            for oc in range(8):
                q.do("tensor", lambda t, oc=oc: t.matmul(
                    p.ps[3][0:128, 0:512], p.ones[0:1, 0:128],
                    nrowb[0:1, oc * 512:(oc + 1) * 512], start=True, stop=True),
                    reads=[("nrowb",)], writes=[("ps", 3)])
                q.do("vector", lambda v, oc=oc: v.tensor_mul(
                    stg[:, oc * 512:(oc + 1) * 512], stg[:, oc * 512:(oc + 1) * 512],
                    p.ps[3][0:128, 0:512]),
                    reads=[("ps", 3), ("stg", oc)], writes=[("stg", oc)])
            for dst, lo in ((p.sc_qd, 0), (p.sc_kd, 4), (p.sc_vd, 8)):
                q.dma("gpsimd",
                      dst.rearrange("(oc q) w -> q oc w", q=128)[:, :, tt * 512:(tt + 1) * 512],
                      stg[:, lo * 512:(lo + 4) * 512].rearrange("q (oc w) -> q oc w", w=512),
                      ("spill", dst.tensor.name), [("stg", lo + i) for i in range(4)])
        q.finish()


def ph2c(p):
    nc = p.nc
    with ExitStack() as ctx:
        sbt = lambda n, s, d: ctx.enter_context(nc.sbuf_tensor(n, s, d))
        wab = sbt("c_wab", [128, NDC * 8], BF16)
        wg = sbt("c_wg", [128, NDC * 512], BF16)
        xn = [sbt(f"c_xn{i}", [128, NDC * 512], BF16) for i in range(2)]
        sig = sbt("c_sig", [8, 512], F32)
        la = sbt("c_la", [4, 512], F32)
        stg = sbt("c_stg", [128, 4 * 512], BF16)
        q = AutoQ(p)
        q.dma("gpsimd", wab[:].rearrange("q (dc o) -> q dc o", o=8),
              p.wabT.rearrange("(dc q) o -> q dc o", q=128), ("wab",), None)
        q.dma("gpsimd", wg[:].rearrange("q (dc o) -> q dc o", o=512),
              p.wgT.rearrange("(dc q) o -> q dc o", q=128), ("wg",), None)
        for tt in range(NTT):
            q.dma("sync", xn[tt % 2][:].rearrange("q (dc w) -> q dc w", w=512),
                  xn_tile_dram(p, p.sc_xn, tt), ("xn", tt % 2), None)

            def abgrp(t, tt=tt):
                for dc in range(NDC):
                    ins = t.matmul(p.ps[2][0:8, 0:512], wab[:, dc * 8:(dc + 1) * 8],
                                   xn[tt % 2][:, dc * 512:(dc + 1) * 512],
                                   start=(dc == 0), stop=(dc == NDC - 1))
                return ins
            q.do("tensor", abgrp, reads=[("wab",), ("xn", tt % 2)], writes=[("ps", 2)])
            q.do("scalar", lambda sc: sc.activation(
                out=sig[:], in_=p.ps[2][0:8, 0:512], func=AF.Sigmoid, bias=p.v_bab[:]),
                reads=[("ps", 2)], writes=[("sig",)])
            q.do("scalar", lambda sc: sc.activation(
                out=la[:], in_=sig[0:4, :], func=AF.Ln),
                reads=[("sig",)], writes=[("la",)])
            q.dma("gpsimd", p.sc_ab[0:8, tt * 512:(tt + 1) * 512], sig[:], ("sc_ab8",), [("sig",)])
            q.dma("gpsimd", p.sc_ab[8:12, tt * 512:(tt + 1) * 512], la[:], ("sc_ab4",), [("la",)])
            for oc in range(4):
                def ggrp(t, oc=oc, tt=tt):
                    bank = p.ps[oc % 2]
                    for dc in range(NDC):
                        ins = t.matmul(bank[0:128, 0:512],
                                       wg[:, (dc * 4 + oc) * 128:(dc * 4 + oc) * 128 + 128],
                                       xn[tt % 2][:, dc * 512:(dc + 1) * 512],
                                       start=(dc == 0), stop=(dc == NDC - 1))
                    return ins
                q.do("tensor", ggrp, reads=[("wg",), ("xn", tt % 2)], writes=[("ps", oc % 2)])
                q.do("scalar", lambda sc, oc=oc: sc.activation(
                    out=stg[:, oc * 512:(oc + 1) * 512], in_=p.ps[oc % 2][0:128, 0:512],
                    func=AF.Silu),
                    reads=[("ps", oc % 2)], writes=[("stg", oc)])
            q.dma("gpsimd",
                  p.sc_gg.rearrange("(oc q) w -> q oc w", q=128)[:, :, tt * 512:(tt + 1) * 512],
                  stg[:].rearrange("q (oc w) -> q oc w", w=512),
                  ("spill_gg",), [("stg", i) for i in range(4)])
        q.finish()


def ph2d(p):
    nc = p.nc
    with ExitStack() as ctx:
        sbt = lambda n, s, d: ctx.enter_context(nc.sbuf_tensor(n, s, d))
        wqf = sbt("d_wqf", [128, NDC * 512], BF16)
        wkf = sbt("d_wkf", [128, NDC * 512], BF16)
        xr = [sbt(f"d_xr{i}", [128, NDC * 512], BF16) for i in range(2)]
        stg = sbt("d_stg", [128, 8 * 512], BF16)
        sq = sbt("d_sq", [128, 512], BF16)
        nrow = sbt("d_nrow", [1, 8 * 512], F32)
        brow = sbt("d_brow", [1, 8 * 512], F32)
        bfin = sbt("d_bfin", [1, 8], F32)
        vt = sbt("d_vt", [128, 512], BF16)
        stv = sbt("d_stv", [128, 4 * 132], BF16)
        q = AutoQ(p)
        q.dma("gpsimd", wqf[:].rearrange("q (dc o) -> q dc o", o=512),
              p.wqfT.rearrange("(dc q) o -> q dc o", q=128), ("wqf",), None)
        q.dma("gpsimd", wkf[:].rearrange("q (dc o) -> q dc o", o=512),
              p.wkfT.rearrange("(dc q) o -> q dc o", q=128), ("wkf",), None)
        for dc in range(NDC):
            q.do("scalar", lambda sc, dc=dc: sc.mul(
                out=wqf[:, dc * 512:(dc + 1) * 512], in_=wqf[:, dc * 512:(dc + 1) * 512],
                mul=p.v_ln1[:, dc:dc + 1]),
                reads=[("wqf",)], writes=[("wqf",)])
            q.do("scalar", lambda sc, dc=dc: sc.mul(
                out=wkf[:, dc * 512:(dc + 1) * 512], in_=wkf[:, dc * 512:(dc + 1) * 512],
                mul=p.v_ln1[:, dc:dc + 1]),
                reads=[("wkf",)], writes=[("wkf",)])
        q.do("vector", lambda v: v.memset(brow[:], 0.0), reads=[], writes=[("brow",)])
        q.do("vector", lambda v: v.memset(
            stv[:].rearrange("q (g c) -> q g c", c=33)[:, :, 16:32], 0.0),
            reads=[], writes=[("stv_ones",)])
        q.do("vector", lambda v: v.memset(
            stv[:].rearrange("q (g c) -> q g c", c=33)[:, :, 32:33], 1.0),
            reads=[("stv_ones",)], writes=[("stv_ones",)])
        for tt in range(NTT):
            q.dma("sync", xr[tt % 2][:].rearrange("q (dc w) -> q dc w", w=512),
                  xn_tile_dram(p, p.sc_xr, tt), ("xr", tt % 2), None)
            for oc in range(8):
                w = wqf if oc < 4 else wkf
                wb = ("wqf",) if oc < 4 else ("wkf",)

                def grp(t, oc=oc, tt=tt, w=w):
                    bank = p.ps[oc % 2]
                    for dc in range(NDC):
                        ins = t.matmul(bank[0:128, 0:512],
                                       w[:, (dc * 4 + oc % 4) * 128:(dc * 4 + oc % 4) * 128 + 128],
                                       xr[tt % 2][:, dc * 512:(dc + 1) * 512],
                                       start=(dc == 0), stop=(dc == NDC - 1))
                    return ins
                q.do("tensor", grp, reads=[wb, ("xr", tt % 2)], writes=[("ps", oc % 2)])
                q.do("scalar", lambda sc, oc=oc: sc.copy(
                    out=stg[:, oc * 512:(oc + 1) * 512], in_=p.ps[oc % 2][0:128, 0:512]),
                    reads=[("ps", oc % 2)], writes=[("stg", oc)])
                q.do("vector", lambda v, oc=oc: v.tensor_mul(
                    sq[:], stg[:, oc * 512:(oc + 1) * 512], stg[:, oc * 512:(oc + 1) * 512]),
                    reads=[("stg", oc)], writes=[("sq",)])
                q.do("tensor", lambda t: t.matmul(
                    p.ps[2][0:1, 0:512], p.ones[:, 0:1], sq[:], start=True, stop=True),
                    reads=[("sq",)], writes=[("ps", 2)])
                q.do("scalar", lambda sc, oc=oc: sc.copy(
                    out=nrow[0:1, oc * 512:(oc + 1) * 512], in_=p.ps[2][0:1, 0:512]),
                    reads=[("ps", 2)], writes=[("nrm", oc)])
            for dst, lo in ((p.sc_qf, 0), (p.sc_kf, 4)):
                q.dma("gpsimd",
                      dst.rearrange("(oc q) w -> q oc w", q=128)[:, :, tt * 512:(tt + 1) * 512],
                      stg[:, lo * 512:(lo + 4) * 512].rearrange("q (oc w) -> q oc w", w=512),
                      ("spill", dst.tensor.name), [("stg", lo + i) for i in range(4)])
            q.do("vector", lambda v: v.tensor_max(brow[:], brow[:], nrow[:]),
                 reads=[("brow",)] + [("nrm", i) for i in range(8)], writes=[("brow",)])
            def vgrp(t, tt=tt):
                for dc in range(NDC):
                    ins = t.matmul(p.ps[3][0:64, 0:512], p.w_pall[:, dc * 64:(dc + 1) * 64],
                                   xr[tt % 2][:, dc * 512:(dc + 1) * 512],
                                   start=(dc == 0), stop=(dc == NDC - 1))
                return ins
            q.do("tensor", vgrp, reads=[("xr", tt % 2)], writes=[("ps", 3)])
            q.do("vector", lambda v: v.tensor_copy(vt[0:64, :], p.ps[3][0:64, 0:512]),
                 reads=[("ps", 3)], writes=[("vt",)])
            for j in range(4):
                q.do("tensor", lambda t, j=j: t.transpose(
                    p.psb[0][0:128, j * 64:(j + 1) * 64], vt[0:64, j * 128:(j + 1) * 128],
                    p.ident[0:64, 0:64]),
                    reads=[("vt",)], writes=[("psb0",)])
                q.do("vector", lambda v, j=j: v.tensor_copy(
                    stv[:, j * 132:j * 132 + 132].rearrange("q (h c) -> q h c", c=33)[:, :, 0:16],
                    p.psb[0][0:128, j * 64:(j + 1) * 64].rearrange("q (h c) -> q h c", c=16)),
                    reads=[("psb0",)], writes=[("stv", j)])
            q.dma("gpsimd",
                  p.sc_vt[tt * 512:(tt + 1) * 512, :].rearrange("(j q) c -> q j c", q=128),
                  stv[:].rearrange("q (j c) -> q j c", c=132),
                  ("spill_vt",), [("stv", j) for j in range(4)] + [("stv_ones",)])
        for oc in range(8):
            q.do("vector", lambda v, oc=oc: v.reduce_max(
                bfin[0:1, oc:oc + 1], brow[0:1, oc * 512:(oc + 1) * 512],
                axis=mybir.AxisListType.X),
                reads=[("brow",)], writes=[("bfin",)])
        q.dma("gpsimd", p.sc_bnd[0:1, 0:8], bfin[:], ("sc_bnd0",), [("bfin",)])
        q.finish()


def ph3(p):
    nc = p.nc
    with ExitStack() as ctx:
        sbt = lambda n, s, d: ctx.enter_context(nc.sbuf_tensor(n, s, d))
        qh = sbt("e_qh", [128, T], BF16)
        kh = sbt("e_kh", [128, T], BF16)
        vh = sbt("e_vh", [128, T], BF16)
        gh = sbt("e_gh", [128, T], BF16)
        ogh = sbt("e_ogh", [128, T], BF16)
        lach = sbt("e_lach", [64, 128], F32)
        Lch = sbt("e_Lch", [64, 128], F32)
        bch = sbt("e_bch", [64, 128], F32)
        LT = sbt("e_LT", [128, 64], F32)
        bT = sbt("e_bT", [128, 64], F32)
        gT = sbt("e_gT", [128, 64], F32)
        nbgT = sbt("e_nbgT", [128, 64], F32)
        rrT = sbt("e_rrT", [128, 64], F32)
        gcb = sbt("e_gcb", [128, 64], F32)
        lcrow = sbt("e_lcrow", [1, 64], F32)
        Lrow = sbt("e_Lrow", [1, T], F32)
        brow = sbt("e_brow", [1, T], BF16)
        onesf = sbt("e_onesf", [1, 128], F32)
        identf = sbt("e_identf", [64, 64], F32)
        gbc = sbt("e_gbc", [128, 128], BF16)
        dtile = sbt("e_dtile", [128, 128], F32)
        d2t = sbt("e_d2t", [128, 128], F32)
        g2 = sbt("e_g2", [128, 128], BF16)
        bm = sbt("e_bm", [128, 128], BF16)
        tp1 = sbt("e_tp1", [128, 128], BF16)
        tpr = sbt("e_tpr", [128, 128], BF16)
        tun = sbt("e_tun", [128, 128], BF16)
        aqk = sbt("e_aqk", [128, 128], BF16)
        powu = [sbt(f"e_powu{i}", [128, 128], BF16) for i in range(6)]
        powp = [sbt(f"e_powp{i}", [128, 128], BF16) for i in range(5)]
        rch = [sbt(f"e_rch{i}", [128, 128], BF16) for i in range(2)]
        vtb = sbt("e_vtb", [128, 128], BF16)
        btile = sbt("e_btile", [128, 128], BF16)
        ut = sbt("e_ut", [128, 128], BF16)
        qt2 = sbt("e_qt2", [128, 128], BF16)
        ktr = sbt("e_ktr", [128, 128], BF16)
        stf = sbt("e_stf", [128, 128], F32)
        stb = sbt("e_stb", [128, 128], BF16)
        q = AutoQ(p)
        q.do("vector", lambda v: v.memset(onesf[:], 1.0), writes=[("onesf",)])
        q.dma("sync", identf[:], p.c_identf, ("identf",), None)
        for h in range(NHL):
            q.dma("sync", qh[:], p.sc_qd[h * 128:(h + 1) * 128, :], ("qh",), None)
            q.dma("sync", kh[:], p.sc_kd[h * 128:(h + 1) * 128, :], ("kh",), None)
            q.dma("sync", vh[:], p.sc_vd[h * 128:(h + 1) * 128, :], ("vh",), None)
            q.dma("sync", gh[:], p.sc_gg[h * 128:(h + 1) * 128, :], ("gh",), None)
            q.dma("sync", lach[:], p.sc_ab.rearrange("o (c w) -> o c w", w=128)[8 + h],
                  ("lach",), None)
            q.dma("sync", bch[:], p.sc_ab.rearrange("o (c w) -> o c w", w=128)[4 + h],
                  ("bch",), None)
            q.dma("gpsimd", brow[:], p.sc_ab[4 + h:5 + h, :], ("brow",), None)
            q.do("vector", lambda v: v.tensor_tensor_scan(
                Lch[:], lach[:], p.zeros[0:64, 0:128], 0.0, ALU.add, ALU.add),
                reads=[("lach",)], writes=[("Lch",)])
            q.dma("gpsimd", p.sc_st.rearrange("o (c w) -> o c w", w=128)[2], Lch[:],
                  ("strow2",), [("Lch",)])
            q.dma("sync", Lrow[:], p.sc_st[2:3, :], ("Lrow",), [("strow2",)])
            q.do("tensor", lambda t: t.transpose(p.ps[4][0:128, 0:64], Lch[:], identf[:]),
                 reads=[("Lch",), ("identf",)], writes=[("ps", 4)])
            q.do("vector", lambda v: v.tensor_copy(LT[:], p.ps[4][0:128, 0:64]),
                 reads=[("ps", 4)], writes=[("LT",)])
            q.do("tensor", lambda t: t.transpose(p.ps[4][0:128, 0:64], bch[:], identf[:]),
                 reads=[("bch",), ("identf",)], writes=[("ps", 4)])
            q.do("vector", lambda v: v.tensor_copy(bT[:], p.ps[4][0:128, 0:64]),
                 reads=[("ps", 4)], writes=[("bT",)])
            q.do("scalar", lambda sc: sc.activation(out=gT[:], in_=LT[:], func=AF.Exp),
                 reads=[("LT",)], writes=[("gT",)])
            q.do("vector", lambda v: v.tensor_mul(nbgT[:], bT[:], gT[:]),
                 reads=[("bT",), ("gT",)], writes=[("nbgT",)])
            q.do("vector", lambda v: v.tensor_scalar_mul(nbgT[:], nbgT[:], -1.0),
                 reads=[("nbgT",)], writes=[("nbgT",)])
            q.do("tensor", lambda t: t.transpose(p.ps[4][0:1, 0:64], Lch[:, 127:128], identf[:]),
                 reads=[("Lch",), ("identf",)], writes=[("ps", 4)])
            q.do("vector", lambda v: v.tensor_copy(lcrow[:], p.ps[4][0:1, 0:64]),
                 reads=[("ps", 4)], writes=[("lcrow",)])
            q.do("tensor", lambda t: t.matmul(p.ps[4][0:128, 0:64], onesf[:],
                                              lcrow[:], start=True, stop=True),
                 reads=[("lcrow",), ("onesf",)], writes=[("ps", 4)])
            q.do("vector", lambda v: v.tensor_sub(rrT[:], p.ps[4][0:128, 0:64], LT[:]),
                 reads=[("ps", 4), ("LT",)], writes=[("rrT",)])
            q.do("scalar", lambda sc: sc.activation(out=rrT[:], in_=rrT[:], func=AF.Exp),
                 reads=[("rrT",)], writes=[("rrT",)])
            q.do("scalar", lambda sc: sc.activation(out=gcb[:], in_=p.ps[4][0:128, 0:64],
                                                    func=AF.Exp),
                 reads=[("ps", 4)], writes=[("gcb",)])
            q.do("vector", lambda v: v.memset(stf[:], 0.0), writes=[("stf",)])
            q.do("vector", lambda v: v.memset(stb[:], 0.0), writes=[("stb",)])
            for c in range(NCH):
                cs = slice(c * CD, (c + 1) * CD)
                q.do("tensor", lambda t, cs=cs: t.matmul(
                    p.ps[4][0:128, 0:128], onesf[:], Lrow[0:1, cs], start=True, stop=True),
                    reads=[("Lrow",), ("onesf",)], writes=[("ps", 4)])
                q.do("scalar", lambda sc: sc.activation(out=gbc[:], in_=p.ps[4][0:128, 0:128],
                                                        func=AF.Exp),
                     reads=[("ps", 4)], writes=[("gbc",)])
                q.do("tensor", lambda t, cs=cs: t.matmul(
                    p.ps[5][0:128, 0:128], Lrow[0:1, cs], onesf[:], start=True, stop=True),
                    reads=[("Lrow",), ("onesf",)], writes=[("ps", 5)])
                q.do("vector", lambda v: v.tensor_copy(d2t[:], p.ps[5][0:128, 0:128]),
                     reads=[("ps", 5)], writes=[("d2t",)])
                q.do("vector", lambda v: v.tensor_sub(dtile[:], p.ps[4][0:128, 0:128],
                                                      d2t[:]),
                     reads=[("ps", 4), ("d2t",)], writes=[("dtile",)])
                q.do("vector", lambda v: v.tensor_scalar_min(dtile[:], dtile[:], 0.0),
                     reads=[("dtile",)], writes=[("dtile",)])
                q.do("scalar", lambda sc: sc.activation(out=g2[:], in_=dtile[:], func=AF.Exp),
                     reads=[("dtile",)], writes=[("g2",)])
                q.do("tensor", lambda t, cs=cs: t.matmul(
                    p.ps[0][0:128, 0:128], kh[:, cs], kh[:, cs], start=True, stop=True),
                    reads=[("kh",)], writes=[("ps", 0)])
                q.do("tensor", lambda t, cs=cs: t.matmul(
                    p.ps[1][0:128, 0:128], kh[:, cs], qh[:, cs], start=True, stop=True),
                    reads=[("kh",), ("qh",)], writes=[("ps", 1)])
                q.do("tensor", lambda t, cs=cs: t.matmul(
                    p.ps[2][0:128, 0:128], p.ones[0:1, 0:128], brow[0:1, cs],
                    start=True, stop=True),
                    reads=[("brow",)], writes=[("ps", 2)])
                q.do("vector", lambda v: v.tensor_mul(bm[:], p.maskUs[:], p.ps[2][0:128, 0:128]),
                     reads=[("ps", 2)], writes=[("bm",)])
                q.do("vector", lambda v: v.tensor_mul(tp1[:], p.ps[0][0:128, 0:128], g2[:]),
                     reads=[("ps", 0), ("g2",)], writes=[("tp1",)])
                q.do("vector", lambda v: v.tensor_mul(tpr[:], tp1[:], bm[:]),
                     reads=[("tp1",), ("bm",)], writes=[("tpr",)])
                q.do("vector", lambda v: v.tensor_mul(tp1[:], p.ps[1][0:128, 0:128], g2[:]),
                     reads=[("ps", 1), ("g2",)], writes=[("tp1",)])
                q.do("vector", lambda v: v.tensor_mul(aqk[:], tp1[:], p.maskUi[:]),
                     reads=[("tp1",)], writes=[("aqk",)])
                q.do("tensor", lambda t: t.transpose(p.psb[0][0:128, 0:128], tpr[:], p.ident[:]),
                     reads=[("tpr",)], writes=[("psb0",)])
                q.do("vector", lambda v: v.tensor_copy(tun[:], p.psb[0][0:128, 0:128]),
                     reads=[("psb0",)], writes=[("tun",)])
                pu, pp = [("tun",)], [("tpr",)]
                for i in range(6):
                    lhs_p = tpr if i == 0 else powp[i - 1]
                    lhs_u = tun if i == 0 else powu[i - 1]
                    rb_p = pp[-1]
                    rb_u = pu[-1]
                    q.do("tensor", lambda t, lhs_p=lhs_p, lhs_u=lhs_u: t.matmul(
                        p.ps[3][0:128, 0:128], lhs_p[:], lhs_u[:], start=True, stop=True),
                        reads=[rb_p, rb_u], writes=[("ps", 3)])
                    q.do("vector", lambda v, i=i: v.tensor_copy(powu[i][:], p.ps[3][0:128, 0:128]),
                         reads=[("ps", 3)], writes=[("powu", i)])
                    pu.append(("powu", i))
                    if i < 5:
                        q.do("tensor", lambda t, lhs_p=lhs_p, lhs_u=lhs_u: t.matmul(
                            p.ps[3][0:128, 0:128], lhs_u[:], lhs_p[:], start=True, stop=True),
                            reads=[rb_p, rb_u], writes=[("ps", 3)])
                        q.do("vector", lambda v, i=i: v.tensor_copy(powp[i][:],
                                                                    p.ps[3][0:128, 0:128]),
                             reads=[("ps", 3)], writes=[("powp", i)])
                        pp.append(("powp", i))
                q.do("vector", lambda v: v.tensor_sub(rch[0][:], p.ident[:], tpr[:]),
                     reads=[("tpr",)], writes=[("rch", 0)])
                for i in range(6):
                    src_r = rch[i % 2]
                    dst_r = rch[(i + 1) % 2]

                    def rstep(t, src_r=src_r, i=i):
                        t.matmul(p.ps[3][0:128, 0:128], p.ident[:], src_r[:],
                                 start=True, stop=False)
                        return t.matmul(p.ps[3][0:128, 0:128], powu[i][:], src_r[:],
                                        start=False, stop=True)
                    q.do("tensor", rstep, reads=[("rch", i % 2), ("powu", i)],
                         writes=[("ps", 3)])
                    q.do("vector", lambda v, dst_r=dst_r: v.tensor_copy(dst_r[:],
                                                                        p.ps[3][0:128, 0:128]),
                         reads=[("ps", 3)], writes=[("rch", (i + 1) % 2)])
                invb = ("rch", 0)
                q.do("tensor", lambda t, cs=cs: t.matmul(
                    p.ps[3][0:128, 0:128], kh[:, cs], stb[:], start=True, stop=True),
                    reads=[("kh",), ("stb",)], writes=[("ps", 3)])
                q.do("tensor", lambda t, cs=cs: t.transpose(
                    p.psb[1][0:128, 0:128], vh[:, cs], p.ident[:]),
                    reads=[("vh",)], writes=[("psb1",)])
                q.do("vector", lambda v, c=c: v.tensor_scalar(
                    vtb[:], p.psb[1][0:128, 0:128], bT[:, c:c + 1], None, ALU.mult),
                    reads=[("psb1",), ("bT",)], writes=[("vtb",)])
                q.do("vector", lambda v, c=c: v.scalar_tensor_tensor(
                    btile[:], p.ps[3][0:128, 0:128], nbgT[:, c:c + 1], vtb[:],
                    ALU.mult, ALU.add),
                    reads=[("ps", 3), ("nbgT",), ("vtb",)], writes=[("btile",)])
                q.do("tensor", lambda t: t.matmul(
                    p.ps[5][0:128, 0:128], rch[0][:], btile[:], start=True, stop=True),
                    reads=[invb, ("btile",)], writes=[("ps", 5)])
                q.do("vector", lambda v: v.tensor_copy(ut[:], p.ps[5][0:128, 0:128]),
                     reads=[("ps", 5)], writes=[("ut",)])
                q.do("vector", lambda v, cs=cs: v.tensor_mul(qt2[:], qh[:, cs], gbc[:]),
                     reads=[("qh",), ("gbc",)], writes=[("qt2",)])

                def ogrp(t):
                    t.matmul(p.ps[0][0:128, 0:128], stb[:], qt2[:], start=True, stop=False)
                    return t.matmul(p.ps[0][0:128, 0:128], ut[:], aqk[:],
                                    start=False, stop=True)
                q.do("tensor", ogrp, reads=[("stb",), ("qt2",), ("ut",), ("aqk",)],
                     writes=[("ps", 0)])
                q.do("vector", lambda v, cs=cs: v.tensor_mul(
                    ogh[:, cs], p.ps[0][0:128, 0:128], gh[:, cs]),
                    reads=[("ps", 0), ("gh",)], writes=[("ogh",)])
                q.do("tensor", lambda t, cs=cs: t.transpose(
                    p.psb[1][0:128, 0:128], kh[:, cs], p.ident[:]),
                    reads=[("kh",)], writes=[("psb1",)])
                q.do("vector", lambda v, c=c: v.tensor_scalar(
                    ktr[:], p.psb[1][0:128, 0:128], rrT[:, c:c + 1], None, ALU.mult),
                    reads=[("psb1",), ("rrT",)], writes=[("ktr",)])
                q.do("tensor", lambda t: t.matmul(
                    p.ps[1][0:128, 0:128], ktr[:], ut[:], start=True, stop=True),
                    reads=[("ktr",), ("ut",)], writes=[("ps", 1)])
                q.do("vector", lambda v, c=c: v.scalar_tensor_tensor(
                    stf[:], stf[:], gcb[:, c:c + 1], p.ps[1][0:128, 0:128],
                    ALU.mult, ALU.add),
                    reads=[("stf",), ("gcb",), ("ps", 1)], writes=[("stf",)])
                q.do("vector", lambda v: v.tensor_copy(stb[:], stf[:]),
                     reads=[("stf",)], writes=[("stb",)])
            q.dma("gpsimd", p.ag_og_in[h * 128:(h + 1) * 128, :], ogh[:],
                  ("og_sp", h), [("ogh",)])
        q.do("gpsimd", lambda g: g.collective_compute(
            "AllGather", ALU.bypass, replica_groups=[[0, 1, 2, 3], [4, 5, 6, 7]],
            ins=[p.ag_og_in], outs=[p.ag_og_out]),
            reads=[("og_sp", 0), ("og_sp", 1), ("og_sp", 2), ("og_sp", 3)],
            writes=[("og_out",)])
        q.finish()


def ph4(p):
    nc = p.nc
    ISR = 1.0 / math.sqrt(128.0)
    with ExitStack() as ctx:
        sbt = lambda n, s, d: ctx.enter_context(nc.sbuf_tensor(n, s, d))
        qf = sbt("f_qf", [128, T], BF16)
        kf = sbt("f_kf", [128, T], BF16)
        vt_sb = sbt("f_vt", [128, 64 * 132], BF16)
        probs = [sbt(f"f_pr{i}", [128, 512], BF16) for i in range(3)]
        macc = sbt("f_macc", [16, T], F32)
        bnd = sbt("f_bnd", [1, 8], F32)
        bmh = sbt("f_bmh", [128, 1], F32)
        rl = sbt("f_rl", [1, 512], F32)
        rlb = sbt("f_rlb", [1, 512], BF16)
        rlbc = sbt("f_rlbc", [16, 512], F32)
        mtmp = sbt("f_mtmp", [16, 512], F32)
        onesf = sbt("f_onesf", [1, 128], F32)
        q = AutoQ(p)
        q.do("vector", lambda v: v.memset(onesf[:], 1.0), writes=[("onesf",)])
        q.do("vector", lambda v: v.memset(macc[:], 0.0), writes=[("macc",)])
        q.dma("sync", vt_sb[:].rearrange("q (kc c) -> q kc c", c=132),
              p.sc_vt.rearrange("(kc q) c -> q kc c", q=128), ("vt",), None)
        q.dma("sync", bnd[:], p.sc_bnd, ("bnd",), None)
        q.do("vector", lambda v: v.tensor_mul(bnd[0:1, 4:8], bnd[0:1, 0:4], bnd[0:1, 4:8]),
             reads=[("bnd",)], writes=[("bnd",)])
        q.do("vector", lambda v: v.tensor_scalar_mul(bnd[0:1, 4:8], bnd[0:1, 4:8], 1.0 / 128.0),
             reads=[("bnd",)], writes=[("bnd",)])
        q.do("scalar", lambda sc: sc.activation(out=bnd[0:1, 4:8], in_=bnd[0:1, 4:8],
                                                func=AF.Sqrt),
             reads=[("bnd",)], writes=[("bnd",)])
        q.do("vector", lambda v: v.tensor_scalar_mul(bnd[0:1, 4:8], bnd[0:1, 4:8], -1.0),
             reads=[("bnd",)], writes=[("bnd",)])
        for h in range(NHL):
            if h < 3:
                ph7_record(p, q, ctx, h)
            q.dma("sync", qf[:], p.sc_qf[h * 128:(h + 1) * 128, :], ("qf",), None)
            q.dma("sync", kf[:], p.sc_kf[h * 128:(h + 1) * 128, :], ("kf",), None)
            q.do("tensor", lambda t, h=h: t.matmul(
                p.ps[3][0:128, 0:1], p.onesf32[:], bnd[0:1, 4 + h:5 + h],
                start=True, stop=True),
                reads=[("bnd",)], writes=[("ps", 3)])
            q.do("vector", lambda v: v.tensor_copy(bmh[:], p.ps[3][0:128, 0:1]),
                 reads=[("ps", 3)], writes=[("bmh",)])
            for qt in range(NTT):
                nkc = 4 * qt + 4

                def pvstep(kc, h=h, nkc=nkc):
                    pb3 = kc % 3
                    q.do("tensor", lambda t, kc=kc, pb3=pb3, s=(kc == 0), e=(kc == nkc - 1): t.matmul(
                        p.ps[2][0:33, 0:512],
                        vt_sb[:, kc * 132 + 33 * h:kc * 132 + 33 * h + 33],
                        probs[pb3][:], start=s, stop=e),
                        reads=[("probs", pb3), ("vt",)], writes=[("pspv",)])
                for kc in range(nkc):
                    pb = kc % 2
                    pb3 = kc % 3
                    q.do("tensor", lambda t, kc=kc, qt=qt, pb=pb: t.matmul(
                        p.ps[pb][0:128, 0:512], kf[:, kc * 128:(kc + 1) * 128],
                        qf[:, qt * 512:(qt + 1) * 512], start=True, stop=True),
                        reads=[("kf",), ("qf",)], writes=[("ps", pb)])
                    q.do("scalar", lambda sc, pb=pb, pb3=pb3: sc.activation(
                        out=probs[pb3][:], in_=p.ps[pb][0:128, 0:512], func=AF.Exp,
                        bias=bmh[:], scale=ISR),
                        reads=[("ps", pb), ("bmh",)], writes=[("probs", pb3)])
                    if kc >= 4 * qt:
                        a = kc - 4 * qt
                        q.do("vector", lambda v, pb3=pb3, a=a: v.tensor_mul(
                            probs[pb3][:], probs[pb3][:], p.diagm[:, a * 512:(a + 1) * 512]),
                            reads=[("probs", pb3)], writes=[("probs", pb3)])
                    if kc >= 1:
                        pvstep(kc - 1)
                pvstep(nkc - 1)
                q.do("vector", lambda v: v.reciprocal(rl[:], p.ps[2][32:33, 0:512]),
                     reads=[("pspv",)], writes=[("rl",)])
                q.do("vector", lambda v: v.tensor_copy(rlb[:], rl[:]),
                     reads=[("rl",)], writes=[("rlb",)])
                q.do("tensor", lambda t: t.matmul(
                    p.ps[3][0:16, 0:512], p.ones[0:1, 0:16], rlb[:], start=True, stop=True),
                    reads=[("rlb",)], writes=[("ps", 3)])
                q.do("vector", lambda v: v.tensor_copy(rlbc[:], p.ps[3][0:16, 0:512]),
                     reads=[("ps", 3)], writes=[("rlbc",)])
                q.do("vector", lambda v: v.tensor_mul(
                    mtmp[:], p.ps[2][0:16, 0:512], rlbc[:]),
                    reads=[("pspv",), ("rlbc",)], writes=[("mtmp",)])
                q.do("vector", lambda v, qt=qt: v.tensor_add(
                    macc[:, qt * 512:(qt + 1) * 512], macc[:, qt * 512:(qt + 1) * 512],
                    mtmp[:]),
                    reads=[("mtmp",), ("macc",)], writes=[("macc",)])
        q.dma("gpsimd", p.ar_m_in, macc[:], ("arin",), [("macc",)])
        q.do("gpsimd", lambda g: g.collective_compute(
            "AllReduce", ALU.add, replica_groups=[[0, 1, 2, 3], [4, 5, 6, 7]],
            ins=[p.ar_m_in], outs=[p.ar_m_out]),
            reads=[("arin",)], writes=[("arout",)])
        q.finish()


def ph5(p):
    nc = p.nc
    with ExitStack() as ctx0:
        Bsb = ctx0.enter_context(nc.sbuf_tensor("g_B", [16, FF], BF16))
        qz = AutoQ(p)
        qz.dma("gpsimd", Bsb[:], p.B_, ("Bsb",), None)
        qz.finish()
        nc.all_engine_barrier()
        for u in range(4):
            with ExitStack() as s1:
                sbt = lambda n, s, d: s1.enter_context(nc.sbuf_tensor(n, s, d))
                xq = sbt(f"g_xq{u}", [128, NDC * 512], F32)
                gsb = sbt(f"g_g{u}", [128, NDC * 512], BF16)
                prod = sbt(f"g_prod{u}", [128, 64 * 512], BF16)
                bcs = sbt(f"g_bcs{u}", [128, 2 * 512], BF16)
                srow = sbt(f"g_srow{u}", [1, 2 * 512], BF16)
                modt = sbt(f"g_modt{u}", [16, 512], F32)
                mh = sbt(f"g_mh{u}", [16, 512], BF16)
                tmp = sbt(f"g_tmp{u}", [128, 512], F32)
                tmp2 = sbt(f"g_tmp2{u}", [128, 512], F32)
                qa = AutoQ(p)
                qa.dma("sync", xq[:].rearrange("q (dc w) -> q dc w", w=512),
                       p.xq4T.rearrange("(dc q) w -> q dc w", q=128)[:, :, u * 512:(u + 1) * 512],
                       ("xq",), None)
                qa.dma("gpsimd", srow[:], dyn_off(p.sc_stb[0:2, 0:512], p.qcol[u]),
                       ("srow",), None)
                for j in range(2):
                    qa.do("tensor", lambda t, j=j: t.matmul(
                        p.ps[2 + j][0:128, 0:512], p.ones[0:1, 0:128],
                        srow[0:1, j * 512:(j + 1) * 512], start=True, stop=True),
                        reads=[("srow",)], writes=[("ps", 2 + j)])
                    qa.do("vector", lambda v, j=j: v.tensor_copy(
                        bcs[:, j * 512:(j + 1) * 512], p.ps[2 + j][0:128, 0:512]),
                        reads=[("ps", 2 + j)], writes=[("bcs",)])
                qa.dma("gpsimd", modt[:], dyn_off(p.ar_m_out[:, 0:512], p.qcol[u]),
                       ("modt",), None)
                qa.do("scalar", lambda sc: sc.activation(out=modt[:], in_=modt[:], func=AF.Tanh),
                      reads=[("modt",)], writes=[("modt",)])
                qa.do("vector", lambda v: v.tensor_scalar_mul(modt[:], modt[:], 0.1),
                      reads=[("modt",)], writes=[("modt",)])
                qa.finish()
                nc.all_engine_barrier()
                with ExitStack() as s2:
                    sb2 = lambda n, s, d: s2.enter_context(nc.sbuf_tensor(n, s, d))
                    hb = sb2(f"g_hb{u}", [128, NDC * 512], BF16)
                    win = [sb2(f"g_win{u}_{i}", [128, NDC * 128], BF16) for i in range(3)]
                    wup = [sb2(f"g_wup{u}_{i}", [128, NDC * 128], BF16) for i in range(3)]
                    gt = [sb2(f"g_gt{u}_{i}", [128, 512], BF16) for i in range(2)]
                    qc = AutoQ(p)
                    qc.dma("gpsimd", hb[:].rearrange("q (dc w) -> q dc w", w=512),
                           dyn_off(p.sc_xr.rearrange("(dc q) w -> q dc w", q=128)
                                   [:, :, 0:512], p.qcol[u]), ("hbx",), None)
                    for dc in range(NDC):
                        qc.do("vector", lambda v, dc=dc: v.tensor_scalar(
                            hb[:, dc * 512:(dc + 1) * 512], hb[:, dc * 512:(dc + 1) * 512],
                            p.v_ln2[:, dc:dc + 1], None, ALU.mult),
                            reads=[("hbx",)], writes=[("hbx",)])
                    def hagrp(t):
                        for dc in range(NDC):
                            ins = t.matmul(p.ps[4][0:16, 0:512],
                                           p.w_a2[:, dc * 16:(dc + 1) * 16],
                                           hb[:, dc * 512:(dc + 1) * 512],
                                           start=(dc == 0), stop=(dc == NDC - 1))
                        return ins
                    qc.do("tensor", hagrp, reads=[("hbx",)], writes=[("ps", 4)])
                    qc.do("vector", lambda v: v.tensor_mul(mh[:], modt[:], p.ps[4][0:16, 0:512]),
                          reads=[("ps", 4), ("modt",)], writes=[("mh",)])
                    for fc in range(64):
                        wb = fc % 3
                        gb = fc % 2
                        qc.dma("gpsimd", win[wb][:].rearrange("q (dc w) -> q dc w", w=128),
                               p.winT.rearrange("(dc q) w -> q dc w", q=128)
                               [:, :, fc * 128:(fc + 1) * 128], ("win", wb), None)
                        qc.dma("gpsimd", wup[wb][:].rearrange("q (dc w) -> q dc w", w=128),
                               p.wupT.rearrange("(dc q) w -> q dc w", q=128)
                               [:, :, fc * 128:(fc + 1) * 128], ("wup", wb), None)

                        def gategrp(t, wb=wb, fc=fc, gb=gb):
                            for dc in range(NDC):
                                t.matmul(p.ps[2 * gb][0:128, 0:512],
                                         win[wb][:, dc * 128:(dc + 1) * 128],
                                         hb[:, dc * 512:(dc + 1) * 512],
                                         start=(dc == 0), stop=False)
                            return t.matmul(p.ps[2 * gb][0:128, 0:512],
                                            Bsb[:, fc * 128:(fc + 1) * 128], mh[:],
                                            start=False, stop=True)
                        qc.do("tensor", gategrp,
                              reads=[("win", wb), ("hbx",), ("mh",), ("Bsb",)],
                              writes=[("ps", 2 * gb)])

                        def upgrp(t, wb=wb, gb=gb):
                            for dc in range(NDC):
                                ins = t.matmul(p.ps[2 * gb + 1][0:128, 0:512],
                                               wup[wb][:, dc * 128:(dc + 1) * 128],
                                               hb[:, dc * 512:(dc + 1) * 512],
                                               start=(dc == 0), stop=(dc == NDC - 1))
                            return ins
                        qc.do("tensor", upgrp, reads=[("wup", wb), ("hbx",)],
                              writes=[("ps", 2 * gb + 1)])
                        qc.do("scalar", lambda sc, gb=gb: sc.activation(
                            out=gt[gb][:], in_=p.ps[2 * gb][0:128, 0:512], func=AF.Silu),
                            reads=[("ps", 2 * gb)], writes=[("gt", gb)])
                        qc.do("vector", lambda v, fc=fc, gb=gb: v.tensor_mul(
                            prod[:, fc * 512:(fc + 1) * 512], gt[gb][:],
                            p.ps[2 * gb + 1][0:128, 0:512]),
                            reads=[("gt", gb), ("ps", 2 * gb + 1)], writes=[("prod",)])
                    qc.finish()
                    nc.all_engine_barrier()
                with ExitStack() as s3:
                    sb3 = lambda n, s, d: s3.enter_context(nc.sbuf_tensor(n, s, d))
                    wdn = [sb3(f"g_wdn{u}_{i}", [128, 64 * 128], BF16) for i in range(3)]
                    tmpd = [sb3(f"g_tmpd{u}_{i}", [128, 512], F32) for i in range(2)]
                    qd = AutoQ(p)
                    for dc in range(NDC):
                        wb = dc % 3
                        db = dc % 2
                        qd.dma("gpsimd", wdn[wb][:].rearrange("q (fc w) -> q fc w", w=128),
                               p.wdnT.rearrange("(fc q) w -> q fc w", q=128)
                               [:, :, dc * 128:(dc + 1) * 128], ("wdn", wb), None)

                        def dgrp(t, wb=wb, db=db):
                            for fc in range(64):
                                ins = t.matmul(p.ps[2 + db][0:128, 0:512],
                                               wdn[wb][:, fc * 128:(fc + 1) * 128],
                                               prod[:, fc * 512:(fc + 1) * 512],
                                               start=(fc == 0), stop=(fc == 63))
                            return ins
                        qd.do("tensor", dgrp, reads=[("wdn", wb), ("prod",)],
                              writes=[("ps", 2 + db)])
                        qd.do("vector", lambda v, dc=dc, db=db: v.tensor_mul(
                            tmpd[db][:], xq[:, dc * 512:(dc + 1) * 512], bcs[:, 0:512]),
                            reads=[("xq",), ("bcs",)], writes=[("tmpd", db)])
                        qd.do("vector", lambda v, db=db: v.tensor_add(
                            tmpd[db][:], tmpd[db][:], bcs[:, 512:1024]),
                            reads=[("tmpd", db), ("bcs",)], writes=[("tmpd", db)])
                        qd.do("vector", lambda v, db=db: v.tensor_add(
                            tmpd[db][:], tmpd[db][:], p.ps[2 + db][0:128, 0:512]),
                            reads=[("tmpd", db), ("ps", 2 + db)], writes=[("tmpd", db)])
                        qd.do("scalar", lambda sc, dc=dc, db=db: sc.activation(
                            out=gsb[:, dc * 512:(dc + 1) * 512], in_=tmpd[db][:], func=AF.Silu),
                            reads=[("tmpd", db)], writes=[("gsb",)])
                    qd.finish()
                    nc.all_engine_barrier()
                with ExitStack() as s4:
                    sb4 = lambda n, s, d: s4.enter_context(nc.sbuf_tensor(n, s, d))
                    ogq = sb4(f"g_ogq{u}", [128, NDC * 512], BF16)
                    wo = [sb4(f"g_wo{u}_{i}", [128, NDC * 128], BF16) for i in range(3)]
                    x1t = [sb4(f"g_x1t{u}_{i}", [128, 512], F32) for i in range(2)]
                    tmf = [sb4(f"g_tmf{u}_{i}", [128, 512], F32) for i in range(2)]
                    qf = AutoQ(p)
                    qf.dma("gpsimd", ogq[:].rearrange("q (dc w) -> q dc w", w=512),
                           dyn_off(p.ag_og_out.rearrange("(dc q) w -> q dc w", q=128)
                                   [:, :, 0:512], p.qcol[u]), ("ogq",), None)
                    for dc in range(NDC):
                        wb = dc % 3
                        db = dc % 2
                        qf.dma("gpsimd", wo[wb][:].rearrange("q (d2 w) -> q d2 w", w=128),
                               p.woT.rearrange("(d2 q) w -> q d2 w", q=128)
                               [:, :, dc * 128:(dc + 1) * 128], ("wo", wb), None)

                        def ygrp(t, wb=wb, db=db):
                            for d2 in range(NDC):
                                ins = t.matmul(p.ps[5 * db][0:128, 0:512],
                                               wo[wb][:, d2 * 128:(d2 + 1) * 128],
                                               ogq[:, d2 * 512:(d2 + 1) * 512],
                                               start=(d2 == 0), stop=(d2 == NDC - 1))
                            return ins
                        qf.do("tensor", ygrp, reads=[("wo", wb), ("ogq",)],
                              writes=[("ps", 5 * db)])
                        qf.do("vector", lambda v, dc=dc, db=db: v.tensor_mul(
                            tmf[db][:], p.ps[5 * db][0:128, 0:512],
                            gsb[:, dc * 512:(dc + 1) * 512]),
                            reads=[("ps", 5 * db), ("gsb",)], writes=[("tmf", db)])
                        qf.do("vector", lambda v, dc=dc, db=db: v.tensor_add(
                            x1t[db][:], tmf[db][:], xq[:, dc * 512:(dc + 1) * 512]),
                            reads=[("tmf", db), ("xq",)], writes=[("x1t", db)])
                        qf.dma("gpsimd",
                               p.sc_x1[dc * 128:(dc + 1) * 128, u * 512:(u + 1) * 512],
                               x1t[db][:], ("x1sp", db), [("x1t", db)])
                    qf.finish()
                    nc.all_engine_barrier()


def ph6(p):
    nc = p.nc
    p.xn1sb = p.ctx.enter_context(nc.sbuf_tensor("xn1sb", [128, NDC * TQ], BF16))
    p.halocol = p.ctx.enter_context(nc.sbuf_tensor("halocol", [128, NDC], BF16))
    with ExitStack() as ctx:
        sbt = lambda n, s, d: ctx.enter_context(nc.sbuf_tensor(n, s, d))
        xf = [sbt(f"h_xf{i}", [128, 512], F32) for i in range(2)]
        xb = [sbt(f"h_xb{i}", [128, 512], BF16) for i in range(2)]
        xsq = sbt("h_xsq", [128, 512], BF16)
        sx = sbt("h_sx", [33, TQ], F32)
        st = sbt("h_st", [128, 16 * 14], F32)
        srow = sbt("h_srow", [1, 2 * 512], BF16)
        bc = sbt("h_bc", [128, 2 * 512], BF16)
        tmp = sbt("h_tmp", [128, 512], BF16)
        halo = sbt("h_halo", [1, D], BF16)
        q = AutoQ(p)
        NIT = 4 * NDC
        for k in range(NIT):
            tt, dc = divmod(k, NDC)
            q.dma("sync", xf[k % 2][:],
                  p.sc_x1[dc * 128:(dc + 1) * 128, tt * 512:(tt + 1) * 512],
                  ("xf", k % 2), None)
            q.do("vector", lambda v, k=k: v.tensor_copy(xb[k % 2][:], xf[k % 2][:]),
                 reads=[("xf", k % 2)], writes=[("xb", k % 2)])
            q.do("vector", lambda v, k=k: v.tensor_mul(xsq[:], xb[k % 2][:], xb[k % 2][:]),
                 reads=[("xb", k % 2)], writes=[("xsq",)])

            def stats2(t, k=k, tt=tt, dc=dc):
                t.matmul(p.ps[2 + tt % 2][0:1, 0:512], p.ones[:, 0:1], xb[k % 2][:],
                         start=(dc == 0), stop=(dc == NDC - 1))
                return t.matmul(p.ps[4 + tt % 2][0:1, 0:512],
                                p.ones[:, 0:1], xsq[:],
                                start=(dc == 0), stop=(dc == NDC - 1))
            q.do("tensor", stats2, reads=[("xb", k % 2), ("xsq",)],
                 writes=[("pssa", tt % 2), ("pssb", tt % 2)])
            if dc == NDC - 1:
                q.do("scalar", lambda sc, tt=tt: sc.copy(
                    out=sx[0:1, tt * 512:(tt + 1) * 512],
                    in_=p.ps[2 + tt % 2][0:1, 0:512]),
                    reads=[("pssa", tt % 2)], writes=[("sx0",)])
                q.do("scalar", lambda sc, tt=tt: sc.copy(
                    out=sx[32:33, tt * 512:(tt + 1) * 512],
                    in_=p.ps[4 + tt % 2][0:1, 0:512]),
                    reads=[("pssb", tt % 2)], writes=[("sx1",)])
        q.dma("gpsimd", p.sc_st[2:3, 0:TQ], sx[0:1, :], ("st2",), [("sx0",)])
        q.dma("gpsimd", p.sc_st[3:4, 0:TQ], sx[32:33, :], ("st3",), [("sx1",)])
        c = lambda i: st[:, 16 * i:16 * (i + 1)]
        q.dma("sync", c(0), p.sc_st[2:3, 0:TQ].rearrange("o (q w) -> o q w", q=128)[0],
              ("c0",), [("st2",)])
        q.dma("sync", c(1), p.sc_st[3:4, 0:TQ].rearrange("o (q w) -> o q w", q=128)[0],
              ("c1",), [("st3",)])
        q.do("vector", lambda v: v.tensor_scalar_mul(c(2), c(0), 1.0 / D),
             reads=[("c0",)], writes=[("mu",)])
        q.do("vector", lambda v: v.tensor_scalar_mul(c(3), c(1), 1.0 / D),
             reads=[("c1",)], writes=[("m2",)])
        q.do("vector", lambda v: v.tensor_mul(c(5), c(2), c(2)),
             reads=[("mu",)], writes=[("t5",)])
        q.do("vector", lambda v: v.tensor_sub(c(4), c(3), c(5)),
             reads=[("m2",), ("t5",)], writes=[("var",)])
        q.do("vector", lambda v: v.tensor_scalar_max(c(4), c(4), 0.0),
             reads=[("var",)], writes=[("var",)])
        q.do("scalar", lambda sc: sc.activation(out=c(5), in_=c(4), func=AF.Sqrt,
                                                bias=p.cst(1e-5)),
             reads=[("var",)], writes=[("t5",)])
        q.do("vector", lambda v: v.reciprocal(c(6), c(5)),
             reads=[("t5",)], writes=[("a1",)])
        q.do("vector", lambda v: v.tensor_mul(c(7), c(2), c(6)),
             reads=[("mu",), ("a1",)], writes=[("b1",)])
        q.do("vector", lambda v: v.tensor_scalar_mul(c(7), c(7), -1.0),
             reads=[("b1",)], writes=[("b1",)])
        q.dma("gpsimd", p.sc_st[6:7, 0:TQ].rearrange("o (q w) -> o q w", q=128)[0], c(6),
              ("st6",), [("a1",)])
        q.dma("gpsimd", p.sc_st[7:8, 0:TQ].rearrange("o (q w) -> o q w", q=128)[0], c(7),
              ("st7",), [("b1",)])
        q.dma("gpsimd", p.sc_stb1[0:1, :], p.sc_st[6:7, 0:TQ], ("stb10",), [("st6",)])
        q.dma("gpsimd", p.sc_stb1[1:2, :], p.sc_st[7:8, 0:TQ], ("stb11",), [("st7",)])
        for k in range(NIT):
            tt, dc = divmod(k, NDC)
            if dc == 0:
                q.dma("gpsimd", srow[:], p.sc_stb1[:, tt * 512:(tt + 1) * 512],
                      ("srow",), [("stb10",), ("stb11",)])
                for j in range(2):
                    q.do("tensor", lambda t, j=j: t.matmul(
                        p.ps[j][0:128, 0:512], p.ones[0:1, 0:128],
                        srow[0:1, j * 512:(j + 1) * 512], start=True, stop=True),
                        reads=[("srow",)], writes=[("ps", j)])
                    q.do("vector", lambda v, j=j: v.tensor_copy(
                        bc[:, j * 512:(j + 1) * 512], p.ps[j][0:128, 0:512]),
                        reads=[("ps", j)], writes=[("bc",)])
            q.dma("sync", xf[k % 2][:],
                  p.sc_x1[dc * 128:(dc + 1) * 128, tt * 512:(tt + 1) * 512],
                  ("xf", k % 2), None)
            q.do("vector", lambda v, k=k: v.tensor_copy(xb[k % 2][:], xf[k % 2][:]),
                 reads=[("xf", k % 2)], writes=[("xb", k % 2)])
            q.do("vector", lambda v, k=k: v.tensor_mul(tmp[:], xb[k % 2][:], bc[:, 0:512]),
                 reads=[("xb", k % 2), ("bc",)], writes=[("tmpx",)])
            q.do("vector", lambda v, tt=tt, dc=dc: v.tensor_add(
                p.xn1sb[:, dc * TQ + tt * 512:dc * TQ + (tt + 1) * 512], tmp[:],
                bc[:, 512:1024]),
                reads=[("tmpx",), ("bc",)], writes=[("xn1",)])
        for dc in range(NDC):
            q.dma("gpsimd", p.ag_h_in[0:1, dc * 128:(dc + 1) * 128],
                  p.xn1sb[:, dc * TQ + TQ - 1:dc * TQ + TQ], ("haloin",), [("xn1",)])
        q.do("gpsimd", lambda g: g.collective_compute(
            "AllGather", ALU.bypass, replica_groups=[[0, 1, 2, 3], [4, 5, 6, 7]],
            ins=[p.ag_h_in], outs=[p.ag_h_out]),
            reads=[("haloin",)], writes=[("haloout",)])
        for j in range(4):
            q.dma("gpsimd", p.ag_h2[0:1, j * 512:(j + 1) * 512], p.c_zero[0:1, 0:512],
                  ("h2z", j), None)
        q.dma("gpsimd", p.ag_h2[1:5, :], p.ag_h_out, ("h2c",), [("haloout",)])
        q.dma("gpsimd", halo[:], dyn_off(p.ag_h2[0:1, :], p.qrow), ("halosb",),
              [("h2c",)] + [("h2z", j) for j in range(4)])
        for dc in range(NDC):
            q.dma("gpsimd", p.halocol[:, dc:dc + 1], halo[0:1, dc * 128:(dc + 1) * 128],
                  ("halocol",), [("halosb",)])
        q.finish()


def ph7(p):
    pass


def ph7_record(p, q, ctx, part):
    nc = p.nc
    if True:
        sbt = lambda n, s, d: ctx.enter_context(nc.sbuf_tensor(n, s, d))
        wt = [sbt(f"i_wt_{part}_{i}", [128, 512], F32) for i in range(2)]
        ab = sbt(f"i_ab_{part}", [128, 512], F32)
        acc = sbt(f"i_acc_{part}", [128, 176 * 2], F32)
        accr = sbt(f"i_accr_{part}", [128, 1], F32)
        tot = sbt(f"i_tot_{part}", [1, 4], F32)
        thr = sbt(f"i_thr_{part}", [128, 1], F32)
        msk = sbt(f"i_msk_{part}", [128, 512], BF16)
        sgn = sbt(f"i_sgn_{part}", [128, 512], BF16)
        qt = sbt(f"i_qt_{part}", [128, 512], BF16)
        onesfc = sbt(f"i_onesf_{part}", [128, 1], F32)
        q.do("vector", lambda v: v.memset(onesfc[:], 1.0), writes=[("onesfc",)])
        for wi, (wsrc, wdst, rows, cols) in [(part, [
                (p.w1T, p.sc_w1q, D, FB), (p.w2T, p.sc_w2q, D, FB),
                (p.w3T, p.sc_w3q, FB, D)][part])]:
            nrc = rows // 128
            ncj = cols // 512
            nt = nrc * ncj
            for ti in range(nt):
                rc, cj = divmod(ti, ncj)
                q.dma("sync", wt[ti % 2][:],
                      wsrc[rc * 128:(rc + 1) * 128, cj * 512:(cj + 1) * 512],
                      ("wt", ti % 2), None)
                q.do("scalar", lambda sc, ti=ti: sc.activation(
                    out=ab[:], in_=wt[ti % 2][:], func=AF.Abs,
                    accum_out=acc[:, ti:ti + 1]),
                    reads=[("wt", ti % 2)], writes=[("ab",), ("acc",)])
            q.do("vector", lambda v, nt=nt: v.reduce_sum(
                accr[:], acc[:, 0:nt], axis=mybir.AxisListType.X),
                reads=[("acc",)], writes=[("accr",)])
            q.do("tensor", lambda t: t.matmul(p.ps[4][0:1, 0:1], onesfc[:], accr[:],
                                              start=True, stop=True),
                 reads=[("accr",), ("onesfc",)], writes=[("ps", 4)])
            q.do("vector", lambda v, wi=wi, rows=rows, cols=cols: v.tensor_scalar(
                tot[0:1, wi:wi + 1], p.ps[4][0:1, 0:1], 0.5 / (rows * cols), 5e-9,
                ALU.mult, ALU.add),
                reads=[("ps", 4)], writes=[("tot",)])
            q.do("tensor", lambda t, wi=wi: t.matmul(
                p.ps[5][0:128, 0:1], p.onesf32[:], tot[0:1, wi:wi + 1],
                start=True, stop=True),
                reads=[("tot",)], writes=[("ps", 5)])
            q.do("vector", lambda v: v.tensor_copy(thr[:], p.ps[5][0:128, 0:1]),
                 reads=[("ps", 5)], writes=[("thr",)])
            for ti in range(nt):
                rc, cj = divmod(ti, ncj)
                q.dma("sync", wt[ti % 2][:],
                      wsrc[rc * 128:(rc + 1) * 128, cj * 512:(cj + 1) * 512],
                      ("wt", ti % 2), None)
                q.do("scalar", lambda sc, ti=ti: sc.activation(
                    out=ab[:], in_=wt[ti % 2][:], func=AF.Abs),
                    reads=[("wt", ti % 2)], writes=[("ab",)])
                q.do("vector", lambda v: v.tensor_scalar(
                    msk[:], ab[:], thr[:], None, ALU.is_ge),
                    reads=[("ab",), ("thr",)], writes=[("msk",)])
                q.do("scalar", lambda sc, ti=ti: sc.activation(
                    out=sgn[:], in_=wt[ti % 2][:], func=AF.Sign),
                    reads=[("wt", ti % 2)], writes=[("sgn",)])
                q.do("vector", lambda v: v.tensor_mul(qt[:], msk[:], sgn[:]),
                     reads=[("msk",), ("sgn",)], writes=[("qt",)])
                q.dma("gpsimd", wdst[rc * 128:(rc + 1) * 128, cj * 512:(cj + 1) * 512],
                      qt[:], ("qsp", wi), [("qt",)])


def ph8(p):
    nc = p.nc
    with ExitStack() as ctx:
        sbt = lambda n, s, d: ctx.enter_context(nc.sbuf_tensor(n, s, d))
        sh = sbt("j_sh", [128, TQ], BF16)
        t1 = sbt("j_t1", [128, TQ], BF16)
        xkb = sbt("j_xkb", [128, TQ], BF16)
        q = AutoQ(p)
        for dc in range(NDC):
            q.do("vector", lambda v, dc=dc: v.tensor_copy(
                sh[:, 1:TQ], p.xn1sb[:, dc * TQ:dc * TQ + TQ - 1]),
                reads=[("xn1g",)], writes=[("sh",)])
            q.do("vector", lambda v, dc=dc: v.tensor_copy(
                sh[:, 0:1], p.halocol[:, dc:dc + 1]),
                reads=[("halocol",)], writes=[("sh",)])
            for (mv, mv1, dst, db) in [(p.v_mk, p.v_mk1, p.sc_xk, "xk"),
                                       (p.v_mv, p.v_mv1, p.sc_xv, "xv"),
                                       (p.v_mr, p.v_mr1, p.sc_xrm, "xr")]:
                q.do("vector", lambda v, mv1=mv1, dc=dc: v.tensor_scalar(
                    t1[:], sh[:], mv1[:, dc:dc + 1], None, ALU.mult),
                    reads=[("sh",)], writes=[("t1",)])
                q.do("vector", lambda v, mv=mv, dc=dc: v.scalar_tensor_tensor(
                    xkb[:], p.xn1sb[:, dc * TQ:(dc + 1) * TQ], mv[:, dc:dc + 1], t1[:],
                    ALU.mult, ALU.add),
                    reads=[("t1",), ("xn1g",)], writes=[("xkb",)])
                q.dma("gpsimd", dst[dc * 128:(dc + 1) * 128, :], xkb[:],
                      ("mx", db), [("xkb",)])
        q.finish()
    nc.all_engine_barrier()
    with ExitStack() as ctx:
        sbt = lambda n, s, d: ctx.enter_context(nc.sbuf_tensor(n, s, d))
        wk = [sbt(f"j_wk{i}", [128, NDC * 128], BF16) for i in range(2)]
        xsrc = sbt("j_xsrc", [128, NDC * 512], BF16)
        kv = sbt("j_kv", [128, 512], BF16)
        q = AutoQ(p)
        for (wT, xsc, dst, sig, tag) in [(p.wk1T, p.sc_xk, p.sc_k1, False, "k"),
                                         (p.wv1T, p.sc_xv, p.sc_v1, False, "v"),
                                         (p.wr1T, p.sc_xrm, p.sc_sr, True, "r")]:
            for tt in range(4):
                q.dma("gpsimd", xsrc[:].rearrange("q (dc w) -> q dc w", w=512),
                      xsc.rearrange("(dc q) w -> q dc w", q=128)[:, :, tt * 512:(tt + 1) * 512],
                      ("xsrc", tag), None)
                for oc in range(NDC):
                    wb = oc % 2
                    q.dma("gpsimd", wk[wb][:].rearrange("q (dc w) -> q dc w", w=128),
                          wT.rearrange("(dc q) w -> q dc w", q=128)
                          [:, :, oc * 128:(oc + 1) * 128], ("wk1", tag, wb), None)

                    def grp(t, wb=wb):
                        for dc in range(NDC):
                            ins = t.matmul(p.ps[wb][0:128, 0:512],
                                           wk[wb][:, dc * 128:(dc + 1) * 128],
                                           xsrc[:, dc * 512:(dc + 1) * 512],
                                           start=(dc == 0), stop=(dc == NDC - 1))
                        return ins
                    q.do("tensor", grp, reads=[("wk1", tag, wb), ("xsrc", tag)],
                         writes=[("ps", wb)])
                    if sig:
                        q.do("scalar", lambda sc, wb=wb: sc.activation(
                            out=kv[:], in_=p.ps[wb][0:128, 0:512], func=AF.Sigmoid),
                            reads=[("ps", wb)], writes=[("kv",)])
                    else:
                        q.do("vector", lambda v, wb=wb: v.tensor_copy(
                            kv[:], p.ps[wb][0:128, 0:512]),
                            reads=[("ps", wb)], writes=[("kv",)])
                    q.dma("gpsimd", dst[oc * 128:(oc + 1) * 128, tt * 512:(tt + 1) * 512],
                          kv[:], ("kvsp", tag), [("kv",)])
        q.finish()
    nc.all_engine_barrier()
    with ExitStack() as ctx:
        sbt = lambda n, s, d: ctx.enter_context(nc.sbuf_tensor(n, s, d))
        kb = [sbt(f"j_kb{i}", [128, TQ], BF16) for i in range(2)]
        vb = [sbt(f"j_vb{i}", [128, TQ], BF16) for i in range(2)]
        pk = sbt("j_pk", [128, TQ], F32)
        pkv = sbt("j_pkv", [128, TQ], F32)
        Nl = sbt("j_Nl", [128, TQ], F32)
        Dl = sbt("j_Dl", [128, TQ], F32)
        ewb = sbt("j_ewb", [128, TQ], F32)
        endsNl = sbt("j_eNl", [128, NDC], F32)
        endsDl = sbt("j_eDl", [128, NDC], F32)
        q = AutoQ(p)
        for dc in range(NDC):
            db = dc % 2
            if dc % 4 == 0:
                bitffn_record(p, q, ctx, dc // 4)
            q.dma("sync", kb[db][:], p.sc_k1[dc * 128:(dc + 1) * 128, :], ("kb", db), None)
            q.dma("sync", vb[db][:], p.sc_v1[dc * 128:(dc + 1) * 128, :], ("vb", db), None)
            q.do("scalar", lambda sc, db=db: sc.activation(out=pk[:], in_=kb[db][:], func=AF.Exp),
                 reads=[("kb", db)], writes=[("pk",)])
            q.do("vector", lambda v, db=db: v.tensor_mul(pkv[:], pk[:], vb[db][:]),
                 reads=[("pk",), ("vb", db)], writes=[("pkv",)])
            q.do("vector", lambda v: v.memset(ewb[:], 0.0),
                 reads=[("ewb",)], writes=[("ewb",)])
            q.do("scalar", lambda sc, dc=dc: sc.activation(
                out=ewb[:], in_=ewb[:], func=AF.Exp, bias=p.v_w[:, dc:dc + 1]),
                reads=[("ewb",)], writes=[("ewb",)])
            q.do("vector", lambda v: v.tensor_tensor_scan(
                Nl[:], ewb[:], pkv[:], 0.0, ALU.mult, ALU.add),
                reads=[("ewb",), ("pkv",)], writes=[("Nl",)])
            q.do("vector", lambda v: v.tensor_tensor_scan(
                Dl[:], ewb[:], pk[:], 0.0, ALU.mult, ALU.add),
                reads=[("ewb",), ("pk",)], writes=[("Dl",)])
            q.dma("gpsimd", p.sc_Nl[dc * 128:(dc + 1) * 128, :], Nl[:], ("Nlsp",), [("Nl",)])
            q.dma("gpsimd", p.sc_Dl[dc * 128:(dc + 1) * 128, :], Dl[:], ("Dlsp",), [("Dl",)])
            q.do("vector", lambda v, dc=dc: v.tensor_copy(endsNl[:, dc:dc + 1],
                                                          Nl[:, TQ - 1:TQ]),
                 reads=[("Nl",)], writes=[("eNl",)])
            q.do("vector", lambda v, dc=dc: v.tensor_copy(endsDl[:, dc:dc + 1],
                                                          Dl[:, TQ - 1:TQ]),
                 reads=[("Dl",)], writes=[("eDl",)])
        q.dma("gpsimd", p.ag_e_in[0:1, :], endsNl[:], ("ein0",), [("eNl",)])
        q.dma("gpsimd", p.ag_e_in[1:2, :], endsDl[:], ("ein1",), [("eDl",)])
        q.do("gpsimd", lambda g: g.collective_compute(
            "AllGather", ALU.bypass, replica_groups=[[0, 1, 2, 3], [4, 5, 6, 7]],
            ins=[p.ag_e_in], outs=[p.ag_e_out]),
            reads=[("ein0",), ("ein1",)], writes=[("eout",)])
        q.finish()
    nc.all_engine_barrier()
    with ExitStack() as ctx:
        sbt = lambda n, s, d: ctx.enter_context(nc.sbuf_tensor(n, s, d))
        kb = [sbt(f"j2_kb{i}", [128, TQ], BF16) for i in range(2)]
        vb = [sbt(f"j2_vb{i}", [128, TQ], BF16) for i in range(2)]
        srb = [sbt(f"j2_srb{i}", [128, TQ], BF16) for i in range(2)]
        pku = sbt("j2_pku", [128, TQ], F32)
        Nl = [sbt(f"j2_Nl{i}", [128, TQ], F32) for i in range(2)]
        Dl = [sbt(f"j2_Dl{i}", [128, TQ], F32) for i in range(2)]
        num = sbt("j2_num", [128, TQ], F32)
        den = sbt("j2_den", [128, TQ], F32)
        Em = sbt("j2_Em", [128, TQ], F32)
        corr = sbt("j2_corr", [128, TQ], F32)
        rwv = sbt("j2_rwv", [128, TQ], BF16)
        iot = sbt("j2_iot", [128, TQ], F32)
        endsN = sbt("j2_eN", [128, NDC], F32)
        endsD = sbt("j2_eD", [128, NDC], F32)
        wp = sbt("j2_wp", [128, NDC], F32)
        mcol = sbt("j2_mcol", [128, 1], F32)
        enN = sbt("j2_enN", [128, NDC], F32)
        enD = sbt("j2_enD", [128, NDC], F32)
        msc = sbt("j2_msc", [1, 4], F32)
        msci = sbt("j2_msci", [1, 1], mybir.dt.int32)
        q = AutoQ(p)
        q.dma("sync", iot[:], p.c_iota, ("iot",), None)
        q.do("vector", lambda v: v.memset(enN[:], 0.0), writes=[("enN",)])
        q.do("vector", lambda v: v.memset(enD[:], 0.0), writes=[("enD",)])
        q.do("vector", lambda v: v.memset(msc[:], 0.0), writes=[("msc",)])
        q.do("gpsimd", lambda g: g.store(msci[0:1, 0:1], p.qv),
             reads=[], writes=[("msci",)])
        q.do("vector", lambda v: v.tensor_copy(msc[0:1, 0:1], msci[0:1, 0:1]),
             reads=[("msci",), ("msc",)], writes=[("mscq",)])
        for p_ in range(4):
            q.do("vector", lambda v, p_=p_: v.tensor_scalar(
                msc[0:1, 1:2], msc[0:1, 0:1], float(-1 - p_), None, ALU.add),
                reads=[("mscq",)], writes=[("mscm", p_)])
            q.do("tensor", lambda t: t.matmul(p.ps[3][0:128, 0:1], p.onesf32[:],
                                              msc[0:1, 1:2], start=True, stop=True),
                 reads=[("mscm", p_)], writes=[("ps", 3)])
            q.do("vector", lambda v: v.tensor_copy(mcol[:], p.ps[3][0:128, 0:1]),
                 reads=[("ps", 3)], writes=[("mcol",)])
            q.do("vector", lambda v: v.tensor_scalar(
                wp[:], p.v_wtq[:], mcol[:], None, ALU.mult),
                reads=[("mcol",)], writes=[("wp",)])
            q.do("vector", lambda v: v.tensor_scalar_min(wp[:], wp[:], 0.0),
                 reads=[("wp",)], writes=[("wp",)])
            q.do("scalar", lambda sc: sc.activation(out=wp[:], in_=wp[:], func=AF.Exp),
                 reads=[("wp",)], writes=[("wp",)])
            q.do("vector", lambda v: v.tensor_scalar(
                mcol[:], mcol[:], 0.0, None, ALU.is_ge),
                reads=[("mcol",)], writes=[("mcol",)])
            q.do("vector", lambda v: v.tensor_scalar(wp[:], wp[:], mcol[:], None, ALU.mult),
                 reads=[("wp",), ("mcol",)], writes=[("wp",)])
            q.dma("sync", endsN[:], p.ag_e_out.rearrange("o (q dc) -> o q dc", q=128)[2 * p_],
                  ("endsN",), None)
            q.dma("sync", endsD[:],
                  p.ag_e_out.rearrange("o (q dc) -> o q dc", q=128)[2 * p_ + 1],
                  ("endsD",), None)
            q.do("vector", lambda v: v.scalar_tensor_tensor(
                enN[:], endsN[:], 1.0, enN[:], ALU.mult, ALU.add)
                if False else v.tensor_mul(endsN[:], endsN[:], wp[:]),
                reads=[("endsN",), ("wp",)], writes=[("endsN",)])
            q.do("vector", lambda v: v.tensor_add(enN[:], enN[:], endsN[:]),
                 reads=[("endsN",), ("enN",)], writes=[("enN",)])
            q.do("vector", lambda v: v.tensor_mul(endsD[:], endsD[:], wp[:]),
                 reads=[("endsD",), ("wp",)], writes=[("endsD",)])
            q.do("vector", lambda v: v.tensor_add(enD[:], enD[:], endsD[:]),
                 reads=[("endsD",), ("enD",)], writes=[("enD",)])
        for dc in range(NDC):
            db = dc % 2
            q.dma("sync", kb[db][:], p.sc_k1[dc * 128:(dc + 1) * 128, :], ("kb", db), None)
            q.dma("sync", vb[db][:], p.sc_v1[dc * 128:(dc + 1) * 128, :], ("vb", db), None)
            q.dma("sync", srb[db][:], p.sc_sr[dc * 128:(dc + 1) * 128, :], ("srb", db), None)
            q.dma("sync", Nl[db][:], p.sc_Nl[dc * 128:(dc + 1) * 128, :], ("Nlr", db), None)
            q.dma("sync", Dl[db][:], p.sc_Dl[dc * 128:(dc + 1) * 128, :], ("Dlr", db), None)
            q.do("scalar", lambda sc, dc=dc, db=db: sc.activation(
                out=pku[:], in_=kb[db][:], func=AF.Exp, bias=p.v_tf[:, dc:dc + 1]),
                reads=[("kb", db)], writes=[("pku",)])
            q.do("scalar", lambda sc, dc=dc: sc.activation(
                out=Em[:], in_=iot[:], func=AF.Exp, scale=p.v_w[:, dc:dc + 1]),
                reads=[("iot",)], writes=[("Em",)])
            q.do("vector", lambda v, db=db: v.tensor_mul(num[:], pku[:], vb[db][:]),
                 reads=[("pku",), ("vb", db)], writes=[("num",)])
            q.do("vector", lambda v, db=db: v.tensor_add(num[:, 1:TQ], num[:, 1:TQ],
                                                         Nl[db][:, 0:TQ - 1]),
                 reads=[("num",), ("Nlr", db)], writes=[("num",)])
            q.do("vector", lambda v, dc=dc: v.tensor_scalar(
                corr[:], Em[:], enN[:, dc:dc + 1], None, ALU.mult),
                reads=[("Em",), ("enN",)], writes=[("corr",)])
            q.do("vector", lambda v: v.tensor_add(num[:, 1:TQ], num[:, 1:TQ],
                                                  corr[:, 0:TQ - 1]),
                 reads=[("corr",), ("num",)], writes=[("num",)])
            q.do("vector", lambda v, dc=dc: v.tensor_scalar(
                num[:, 0:1], enN[:, dc:dc + 1], 1.0, None, ALU.mult)
                if False else v.tensor_add(num[:, 0:1], num[:, 0:1],
                                           enN[:, dc:dc + 1]),
                reads=[("enN",), ("num",)], writes=[("num",)])
            q.do("vector", lambda v: v.tensor_copy(den[:], pku[:]),
                 reads=[("pku",)], writes=[("den",)])
            q.do("vector", lambda v, db=db: v.tensor_add(den[:, 1:TQ], den[:, 1:TQ],
                                                          Dl[db][:, 0:TQ - 1]),
                 reads=[("den",), ("Dlr", db)], writes=[("den",)])
            q.do("vector", lambda v, dc=dc: v.tensor_scalar(
                corr[:], Em[:], enD[:, dc:dc + 1], None, ALU.mult),
                reads=[("Em",), ("enD",)], writes=[("corr",)])
            q.do("vector", lambda v: v.tensor_add(den[:, 1:TQ], den[:, 1:TQ],
                                                  corr[:, 0:TQ - 1]),
                 reads=[("corr",), ("den",)], writes=[("den",)])
            q.do("vector", lambda v, dc=dc: v.tensor_add(den[:, 0:1], den[:, 0:1],
                                                         enD[:, dc:dc + 1]),
                 reads=[("enD",), ("den",)], writes=[("den",)])
            q.do("vector", lambda v: v.reciprocal(den[:], den[:]),
                 reads=[("den",)], writes=[("den",)])
            q.do("vector", lambda v: v.tensor_mul(num[:], num[:], den[:]),
                 reads=[("num",), ("den",)], writes=[("num",)])
            q.do("vector", lambda v, db=db: v.tensor_mul(rwv[:], num[:], srb[db][:]),
                 reads=[("num",), ("srb", db)], writes=[("rwv",)])
            q.dma("gpsimd", p.sc_rw[dc * 128:(dc + 1) * 128, :], rwv[:], ("rwsp",), [("rwv",)])
        q.finish()


def bitffn_record(p, q, ctx, tt):
    nc = p.nc
    sbt = lambda n, s, d: ctx.enter_context(nc.sbuf_tensor(n, s, d))
    if not hasattr(p, "_bfb"):
        p._bfb = {
            "hid": sbt("bf_hid", [128, 44 * 512], BF16),
            "w1b": [sbt(f"bf_w1b{i}", [128, NDC * 128], BF16) for i in range(2)],
            "w2b": [sbt(f"bf_w2b{i}", [128, NDC * 128], BF16) for i in range(2)],
            "w3b": [sbt("bf_w3b0", [128, 44 * 128], BF16)],
            "sg": sbt("bf_sg", [128, 512], BF16),
            "gq": sbt("bf_gq", [128, 512], BF16),
        }
    hid = p._bfb["hid"]; w1b = p._bfb["w1b"]; w2b = p._bfb["w2b"]
    w3b = p._bfb["w3b"]; sg = p._bfb["sg"]; gq = p._bfb["gq"]
    for fc in range(44):
        wb = fc % 2
        q.dma("sync", w1b[wb][:].rearrange("q (dc w) -> q dc w", w=128),
              p.sc_w1q.rearrange("(dc q) w -> q dc w", q=128)
              [:, :, fc * 128:(fc + 1) * 128], ("kw1", wb), None)
        q.dma("sync", w2b[wb][:].rearrange("q (dc w) -> q dc w", w=128),
              p.sc_w2q.rearrange("(dc q) w -> q dc w", q=128)
              [:, :, fc * 128:(fc + 1) * 128], ("kw2", wb), None)

        def h1grp(t, wb=wb, tt=tt, fb2=fc % 2):
            for dc in range(NDC):
                ins = t.matmul(p.ps[2 * fb2][0:128, 0:512],
                               w1b[wb][:, dc * 128:(dc + 1) * 128],
                               p.xn1sb[:, dc * TQ + tt * 512:dc * TQ + (tt + 1) * 512],
                               start=(dc == 0), stop=(dc == NDC - 1))
            return ins
        q.do("tensor", h1grp, reads=[("kw1", wb), ("xn1g",)],
             writes=[("ps", 2 * (fc % 2))])

        def h2grp(t, wb=wb, tt=tt, fb2=fc % 2):
            for dc in range(NDC):
                ins = t.matmul(p.ps[2 * fb2 + 1][0:128, 0:512],
                               w2b[wb][:, dc * 128:(dc + 1) * 128],
                               p.xn1sb[:, dc * TQ + tt * 512:dc * TQ + (tt + 1) * 512],
                               start=(dc == 0), stop=(dc == NDC - 1))
            return ins
        q.do("tensor", h2grp, reads=[("kw2", wb), ("xn1g",)],
             writes=[("ps", 2 * (fc % 2) + 1)])
        q.do("scalar", lambda sc, fb2=fc % 2: sc.activation(
            out=sg[:], in_=p.ps[2 * fb2][0:128, 0:512], func=AF.Sigmoid),
            reads=[("ps", 2 * (fc % 2))], writes=[("sg",)])
        q.do("vector", lambda v, fc=fc, fb2=fc % 2: v.tensor_mul(
            hid[:, fc * 512:(fc + 1) * 512], sg[:], p.ps[2 * fb2 + 1][0:128, 0:512]),
            reads=[("sg",), ("ps", 2 * (fc % 2) + 1)], writes=[("hid",)])
    for oc in range(NDC):
        wb3 = 0
        q.dma("sync", w3b[wb3][:].rearrange("q (fc w) -> q fc w", w=128),
              p.sc_w3q.rearrange("(fc q) w -> q fc w", q=128)
              [:, :, oc * 128:(oc + 1) * 128], ("kw3", wb3), None)

        def w3grp(t, ob2=oc % 2, wb3=wb3):
            for fc in range(44):
                ins = t.matmul(p.ps[4 + ob2][0:128, 0:512],
                               w3b[wb3][:, fc * 128:(fc + 1) * 128],
                               hid[:, fc * 512:(fc + 1) * 512],
                               start=(fc == 0), stop=(fc == 43))
            return ins
        q.do("tensor", w3grp, reads=[("kw3", wb3), ("hid",)],
             writes=[("ps", 4 + oc % 2)])
        q.do("scalar", lambda sc, ob2=oc % 2: sc.activation(
            out=gq[:], in_=p.ps[4 + ob2][0:128, 0:512], func=AF.Silu),
            reads=[("ps", 4 + oc % 2)], writes=[("gq",)])
        q.dma("gpsimd", p.sc_gq[oc * 128:(oc + 1) * 128, tt * 512:(tt + 1) * 512],
              gq[:], ("gqsp",), [("gq",)])


def ph9(p):
    nc = p.nc
    with ExitStack() as ctx:
        sbt = lambda n, s, d: ctx.enter_context(nc.sbuf_tensor(n, s, d))
        rw = sbt("k_rw", [128, NDC * 512], BF16)
        wo = [sbt(f"k_wo{i}", [128, NDC * 128], BF16) for i in range(3)]
        y1 = sbt("k_y1", [128, NDC * 512], BF16)
        gq = [sbt(f"k_gq{i}", [128, 512], BF16) for i in range(2)]
        xt = [sbt(f"k_xt{i}", [128, 512], F32) for i in range(2)]
        fin = [sbt(f"k_fin{i}", [128, 512], F32) for i in range(2)]
        q = AutoQ(p)
        for tt in range(4):
            q.dma("sync", rw[:].rearrange("q (dc w) -> q dc w", w=512),
                  p.sc_rw.rearrange("(dc q) w -> q dc w", q=128)[:, :, tt * 512:(tt + 1) * 512],
                  ("rw",), None)
            for oc in range(NDC):
                wb = oc % 3
                q.dma("gpsimd", wo[wb][:].rearrange("q (dc w) -> q dc w", w=128),
                      p.wo1T.rearrange("(dc q) w -> q dc w", q=128)
                      [:, :, oc * 128:(oc + 1) * 128], ("kwo", wb), None)

                def ygrp(t, wb=wb):
                    for dc in range(NDC):
                        ins = t.matmul(p.ps[wb][0:128, 0:512],
                                       wo[wb][:, dc * 128:(dc + 1) * 128],
                                       rw[:, dc * 512:(dc + 1) * 512],
                                       start=(dc == 0), stop=(dc == NDC - 1))
                    return ins
                q.do("tensor", ygrp, reads=[("kwo", wb), ("rw",)], writes=[("ps", wb)])
                q.do("vector", lambda v, oc=oc, wb=wb: v.tensor_copy(
                    y1[:, oc * 512:(oc + 1) * 512], p.ps[wb][0:128, 0:512]),
                    reads=[("ps", wb)], writes=[("y1",)])
            for oc in range(NDC):
                ob = oc % 2
                q.dma("sync", gq[ob][:],
                      p.sc_gq[oc * 128:(oc + 1) * 128, tt * 512:(tt + 1) * 512],
                      ("kgq", ob), None)
                q.dma("sync", xt[ob][:],
                      p.sc_x1[oc * 128:(oc + 1) * 128, tt * 512:(tt + 1) * 512],
                      ("kxt", ob), None)
                q.do("vector", lambda v, oc=oc, ob=ob: v.tensor_mul(
                    fin[ob][:], y1[:, oc * 512:(oc + 1) * 512], gq[ob][:]),
                    reads=[("y1",), ("kgq", ob)], writes=[("fin", ob)])
                q.do("vector", lambda v, ob=ob: v.tensor_add(fin[ob][:], fin[ob][:], xt[ob][:]),
                     reads=[("fin", ob), ("kxt", ob)], writes=[("fin", ob)])
                q.dma("gpsimd", p.out[oc * 128:(oc + 1) * 128, tt * 512:(tt + 1) * 512],
                      fin[ob][:], ("outsp", ob), [("fin", ob)])
        q.finish()


def make_in_maps(I):
    def C(x):
        return np.ascontiguousarray(np.asarray(x, dtype=np.float32))

    in_maps = []
    for c in range(8):
        b, g = c // 4, c % 4
        qr = slice(512 * g, 512 * (g + 1))
        m = {}
        m["xT"] = C(I["L0_attn_x"][b].T)
        m["qi"] = np.array([[g]], np.uint32)
        m["xq4T"] = C(I["L0_attn_x"][b, g * TQ:(g + 1) * TQ, :].T)
        rows = np.r_[np.arange(512 * g, 512 * (g + 1)),
                     2048 + np.arange(512 * g, 512 * (g + 1)),
                     4096 + np.arange(512 * g, 512 * (g + 1))]
        m["wqkvT"] = C(I["L0_attn_W_qkv"][rows].T)
        m["convw"] = C(I["L0_attn_conv_w"][rows])
        absel = np.r_[np.arange(g * 4, g * 4 + 4), 16 + np.arange(g * 4, g * 4 + 4)]
        m["wabT"] = C(I["L0_attn_W_ab"][absel].T)
        m["bab"] = C(I["L0_attn_b_ab"][absel][:, None])
        m["wgT"] = C(I["L0_attn_W_g"][qr].T)
        m["woT"] = C(I["L0_attn_W_o"].T)
        m["ln1r"] = C(I["L0_ffn_ln1_w"].reshape(NDC, 128).T)
        m["ln2r"] = C(I["L0_ffn_ln2_w"].reshape(NDC, 128).T)
        m["wqfT"] = C(I["L0_ffn_Wq"][qr].T)
        m["wkfT"] = C(I["L0_ffn_Wk"][qr].T)
        m["wofs"] = C(I["L0_ffn_Wo"][:, qr])
        m["wvfs"] = C(I["L0_ffn_Wv"][qr, :])
        m["A_"] = C(I["L0_ffn_A"])
        m["B_"] = C(I["L0_ffn_B"])
        m["winT"] = C(I["L0_ffn_W_in_base"].T)
        m["wupT"] = C(I["L0_ffn_W_up_base"].T)
        m["wdnT"] = C(I["L0_ffn_W_down"].T)
        m["tdr"] = C(I["L1_attn_time_decay"].reshape(NDC, 128).T)
        m["tfr"] = C(I["L1_attn_time_first"].reshape(NDC, 128).T)
        m["mkr"] = C(np.asarray(I["L1_attn_time_mix_k"]).reshape(-1).reshape(NDC, 128).T)
        m["mvr"] = C(np.asarray(I["L1_attn_time_mix_v"]).reshape(-1).reshape(NDC, 128).T)
        m["mrr"] = C(np.asarray(I["L1_attn_time_mix_r"]).reshape(-1).reshape(NDC, 128).T)
        m["wk1T"] = C(I["L1_attn_Wk"].T)
        m["wv1T"] = C(I["L1_attn_Wv"].T)
        m["wr1T"] = C(I["L1_attn_Wr"].T)
        m["wo1T"] = C(I["L1_attn_Wo"].T)
        m["w1T"] = C(I["L1_ffn_w1"].T)
        m["w2T"] = C(I["L1_ffn_w2"].T)
        m["w3T"] = C(I["L1_ffn_w3"].T)
        in_maps.append(m)
    return in_maps


def kernel(**I):
    in_maps = make_in_maps(I)
    nc = build_program()
    res = run_bass_kernel_spmd(nc, in_maps, core_ids=list(range(8)))
    out = np.empty((2, T, D), np.float32)
    for c in range(8):
        b, g = c // 4, c % 4
        out[b, g * TQ:(g + 1) * TQ, :] = res.results[c]["out"].T
    return out
